# version 06
# baseline (speedup 1.0000x reference)
import contextlib
import numpy as np
import ml_dtypes
import concourse.bass as bass
import concourse.mybir as mybir
from concourse.bass_utils import run_bass_kernel_spmd

F32 = mybir.dt.float32
BF16 = mybir.dt.bfloat16
I32 = mybir.dt.int32
AF = mybir.ActivationFunctionType
ALU = mybir.AluOpType

D = 1024
TOK = 2048
SEQ = 8192
NC = 8
HALF = 1024
EPS = 1e-5
ENGINES = ("sync", "gpsimd", "tensor", "scalar", "vector")


def f32(x):
    return np.float32(x)


class Emit:

    def __init__(self, sems):
        self.progs = {e: [] for e in ENGINES}
        self.ctr = {}
        self.sems = sems

    def op(self, eng, fn):
        self.progs[eng].append(fn)

    def val(self, name):
        return self.ctr.get(name, 0)

    def inc(self, name, k=1):
        self.ctr[name] = self.val(name) + k
        return self.ctr[name]

    def wait(self, eng, ctr, v=None):
        if v is None:
            v = self.val(ctr)
        if v <= 0:
            return
        s = self.sems[ctr]
        self.op(eng, lambda e, s=s, v=v: e.wait_ge(s, v))

    def dma(self, eng, out, in_, ctr):
        v = self.inc(ctr, 16)
        s = self.sems[ctr]
        self.op(eng, lambda e, out=out, in_=in_, s=s:
                e.dma_start(out=out, in_=in_).then_inc(s, 16))
        return v

    def play(self, block):
        for eng in ENGINES:
            prog = self.progs[eng]

            def run(e, prog=prog):
                for fn in prog:
                    fn(e)
            getattr(block, eng)(run)


SEM_NAMES = ("d_c", "d_w", "d_w0", "d_w1", "d_sk", "d_sv", "d_kv0", "d_kv1", "d_out", "cc",
             "pe", "ac", "dv", "pe_s", "pe_pv", "ac_e",
             "dv_v65", "dv_z", "dv_z1", "dv_z2", "pe_bc")


def _common_setup(nc, T, es):
    def sb(name, shape, dt):
        T[name] = es.enter_context(nc.sbuf_tensor(name, list(shape), dt))
        return T[name]

    def psa(name, shape, dt):
        T[name] = es.enter_context(nc.psum_tensor(name, list(shape), dt))
        return T[name]
    return sb, psa



def emit_xb_tiles(em, T, src_sb, tok0):
    arena = T["arena"]
    A_XT = T["A_XT"]
    em.wait("vector", "pe")
    s = em.sems["dv"]
    for ec in range(NC):
        em.inc("dv")
        em.op("vector", lambda vv, ec=ec, arena=arena, src=src_sb, tok0=tok0, A_XT=A_XT, s=s:
              vv.tensor_copy(arena[:, A_XT + ec * 512: A_XT + ec * 512 + 512],
                             src[:, ec * TOK + tok0: ec * TOK + tok0 + 512]).then_inc(s, 1))
    return em.val("dv")


def xb_slice(T, ec):
    arena = T["arena"]
    A_XT = T["A_XT"]
    return arena[:, A_XT + ec * 512: A_XT + ec * 512 + 512]


def emit_mm_group(em, T, lhs_list, rhs_list, outv, n):
    pes = em.sems["pe"]
    nl = len(lhs_list)
    for i, (lw, rh) in enumerate(zip(lhs_list, rhs_list)):
        if i < nl - 1:
            em.op("tensor", lambda t, lw=lw, rh=rh, i=i, outv=outv, n=n:
                  t.matmul(outv[0:n, :], lw, rh, start=(i == 0), stop=False))
    v = em.inc("pe")
    lw, rh = lhs_list[-1], rhs_list[-1]
    em.op("tensor", lambda t, lw=lw, rh=rh, outv=outv, n=n, s=pes, first=(nl == 1):
          t.matmul(outv[0:n, :], lw, rh, start=first, stop=True).then_inc(s, 1))
    return v


def emit_proj_group(em, T, w_sb, colbase, ncols, outv):
    lhs = [w_sb[:, colbase(ec):colbase(ec) + ncols] for ec in range(NC)]
    rhs = [xb_slice(T, ec) for ec in range(NC)]
    return emit_mm_group(em, T, lhs, rhs, outv, ncols)


def emit_ln(em, T, src_sb, gamma_beta=None, out_cols=None, tbs=None):
    arena = T["arena"]
    ps01, ps23 = T["ps01"], T["ps23"]
    ones_col = T["ones_col"]
    st_mu, st_ms, st_rs = T["st_mu"], T["st_ms"], T["st_rs"]
    st_var = T["st_var"][0:1, 0:512]
    st_B = T["st_B"][0:1, 512:1024]
    srow, ev0, ev1 = T["srow"], T["ev0"], T["ev1"]
    dvs, acs, pes = em.sems["dv"], em.sems["ac"], em.sems["pe"]
    em.op("vector", lambda v: v.drain())
    for tb in range(4):
        tok0 = tb * 512
        emit_xb_tiles(em, T, src_sb, tok0)
        for ec in range(NC):
            sq = (ev0, ev1)[ec % 2]
            em.wait("scalar", "dv")
            em.wait("scalar", "pe")
            va = em.inc("ac")
            em.op("scalar", lambda s, sq=sq, ec=ec, T=T, ss=acs:
                  s.activation(out=sq[:], in_=xb_slice(T, ec), func=AF.Square).then_inc(ss, 1))
            em.wait("tensor", "ac", va)
            em.inc("pe")
            em.op("tensor", lambda t, sq=sq, ec=ec, T=T, oc=ones_col, s=pes:
                  (t.matmul(ps01[0:1, 0:512], oc[:], xb_slice(T, ec),
                            start=(ec == 0), stop=(ec == NC - 1)),
                   t.matmul(ps01[0:1, 512:1024], oc[:], sq[:],
                            start=(ec == 0), stop=(ec == NC - 1)).then_inc(s, 1)))
        em.wait("vector", "pe")
        em.op("vector", lambda v: v.tensor_scalar_mul(st_mu[:], ps01[0:1, 0:512], 1.0 / D))
        em.op("vector", lambda v: v.tensor_scalar_mul(st_ms[:], ps01[0:1, 512:1024], 1.0 / D))
        em.op("vector", lambda v: v.drain())
        em.op("vector", lambda v: v.tensor_tensor(st_var, st_mu[:], st_mu[:], ALU.mult))
        em.op("vector", lambda v: v.drain())
        v1 = em.inc("dv")
        em.op("vector", lambda v, s=dvs:
              v.tensor_tensor(st_var, st_ms[:], st_var, ALU.subtract).then_inc(s, 1))
        em.wait("scalar", "dv", v1)
        em.op("scalar", lambda s: s.activation(out=srow[0:1, 0:512], in_=st_var[:],
                                               func=AF.Ln, bias=f32(EPS), scale=f32(1.0)))
        em.op("scalar", lambda s: s.drain())
        va = em.inc("ac")
        em.op("scalar", lambda s, ss=acs:
              s.activation(out=st_rs[:], in_=srow[0:1, 0:512], func=AF.Exp,
                           scale=f32(-0.5)).then_inc(ss, 1))
        em.wait("vector", "ac", va)
        v2 = em.inc("dv")
        em.op("vector", lambda v, s=dvs:
              v.tensor_tensor(st_B, st_mu[:], st_rs[:], ALU.mult).then_inc(s, 1))
        em.wait("tensor", "dv", v2)
        if gamma_beta is None:
            orow32 = T["ones_row32"]
            em.inc("pe")
            em.op("tensor", lambda t, orow=orow32, s=pes:
                  (t.matmul(ps23[:, 0:512], orow[:], st_rs[:], start=True, stop=True),
                   t.matmul(ps23[:, 512:1024], orow[:], st_B, start=True, stop=True).then_inc(s, 1)))
            em.wait("vector", "pe")
            for ec in range(NC):
                xs = src_sb[:, ec * TOK + tok0: ec * TOK + tok0 + 512]
                em.op("vector", lambda v, xs=xs: v.tensor_tensor(xs, xs, ps23[:, 0:512], ALU.mult))
                em.op("vector", lambda v: v.drain())
                em.inc("dv")
                em.op("vector", lambda v, xs=xs, s=dvs:
                      v.tensor_tensor(xs, xs, ps23[:, 512:1024], ALU.subtract).then_inc(s, 1))
                em.op("vector", lambda v: v.drain())
        else:
            g_c, b_c = gamma_beta
            orow32 = T["ones_row32"]
            em.inc("pe")
            em.op("tensor", lambda t, orow=orow32, s=pes:
                  (t.matmul(ps23[:, 0:512], orow[:], st_rs[:], start=True, stop=True),
                   t.matmul(ps23[:, 512:1024], orow[:], st_B, start=True, stop=True).then_inc(s, 1)))
            em.wait("vector", "pe")
            for ec in range(NC):
                xs = src_sb[:, ec * TOK + tok0: ec * TOK + tok0 + 512]
                tf = (T["tf0"], T["tf1"])[ec % 2]
                em.op("vector", lambda v, xs=xs, tf=tf:
                      v.tensor_tensor(tf[:], xs, ps23[:, 0:512], ALU.mult))
                em.op("vector", lambda v: v.drain())
                em.op("vector", lambda v, tf=tf:
                      v.tensor_tensor(tf[:], tf[:], ps23[:, 512:1024], ALU.subtract))
                em.op("vector", lambda v: v.drain())
                em.inc("dv")
                dst = arena[:, out_cols + ec * TOK + tok0: out_cols + ec * TOK + tok0 + 512]
                em.op("vector", lambda v, tf=tf, dst=dst, ec=ec, s=dvs, g_c=g_c, b_c=b_c:
                      v.tensor_scalar(out=dst, in0=tf[:], scalar1=g_c[:, ec:ec + 1],
                                      scalar2=b_c[:, ec:ec + 1], op0=ALU.mult,
                                      op1=ALU.add).then_inc(s, 1))
                em.op("vector", lambda v: v.drain())
    return em.val("dv")


def emit_store_x(em, T, x_out):
    x_sb = T["x_sb"]
    em.wait("sync", "dv")
    em.wait("sync", "ac")
    em.dma("sync", x_out.ap().rearrange("(c p) t -> p c t", p=128),
           x_sb[:].rearrange("p (c t) -> p c t", c=NC), "d_out")
    em.wait("sync", "d_out")



def build_attn(layer: int, part: str):
    nc = bass.Bass("TRN2", target_bir_lowering=False)
    din = {}

    def dram_in(name, shape, dt=F32):
        din[name] = nc.dram_tensor(name, list(shape), dt, kind="ExternalInput")
        return din[name]

    dram_in("x_in", [D, TOK])
    dram_in("c_ones_col", [128, 1])
    dram_in("c_ones_row", [1, 128])
    dram_in("c_ident", [128, 128])
    dram_in("c_mask01", [128, 2])
    dram_in("c_sel2", [2, 128])
    dram_in("c_ones_row32", [1, 128])
    if part == "kv":
        dram_in("w_k", [128, 8192])
        dram_in("w_v", [128, 8192])
        if layer == 1:
            dram_in("ks_in", [128, 8])
        kt_loc = nc.dram_tensor("kt_out", [D, TOK], BF16, kind="ExternalOutput")
        v_loc = nc.dram_tensor("v_out", [TOK, D], BF16, kind="ExternalOutput")
        kt_full = v_full = None
    else:
        dram_in("w_q", [128, 8192])
        dram_in("w_o2", [64, 16384])
        if layer == 1:
            dram_in("qs_in", [64, 16])
            dram_in("bo_in", [128, 8])
        kt_full = dram_in("kt_full", [4 * D, TOK], BF16)
        v_full = dram_in("v_full", [SEQ, D], BF16)
        x_out = nc.dram_tensor("x_out", [D, TOK], F32, kind="ExternalOutput")
        rl_dram = nc.dram_tensor("rl_dram", [1, 1024], F32)

    es = contextlib.ExitStack()
    sems = {s: es.enter_context(nc.semaphore(s)) for s in SEM_NAMES}
    em = Emit(sems)
    T = {}
    sb, psa = _common_setup(nc, T, es)

    x_sb = sb("x_sb", [128, NC * TOK], F32)
    arena = sb("arena", [128, 18496], BF16)
    qt_sb = sb("qt_sb", [128, 16384], BF16)
    zt_sb = sb("zt_sb", [128, 16384], BF16)
    w_sb = sb("w_sb", [128, 8192], BF16)
    ev0 = sb("ev0", [128, 512], BF16)
    ev1 = sb("ev1", [128, 512], BF16)
    sq_t = sb("sq_t", [128, 512], BF16)
    tf0 = sb("tf0", [128, 512], F32)
    tf1 = sb("tf1", [128, 512], F32)
    rl_row = sb("rl_row", [1, 1024], F32)
    if part == "core":
        bcP = sb("bcP", [64, 1024], F32)
        T["bcP"] = bcP
    srow = sb("srow", [2 if part == "kv" else 1, 1024], F32)
    bcA = sb("bcA", [64, 512], BF16) if part == "kv" else None
    bc64 = None
    st_mu = sb("st_mu", [1, 512], F32)
    st_ms = sb("st_ms", [1, 512], F32)
    st_var = rl_row[0:1, 0:512].tensor if False else None
    st_rs = sb("st_rs", [1, 512], F32)
    ones_col = sb("ones_col", [128, 1], BF16)
    ones_row = sb("ones_row", [1, 128], BF16)
    ones_row32 = sb("ones_row32", [1, 128], F32)
    ident = mask01 = sel2 = None
    if part == "kv":
        ident = sb("ident", [128, 128], BF16)
        mask01 = sb("mask01", [128, 2], BF16)
        sel2 = sb("sel2", [2, 128], F32)
    qs_sb = ks_sb = bo_sb = None
    if layer == 1 and part == "core":
        qs_sb = sb("qs_sb", [64, 16], F32)
        bo_sb = sb("bo_sb", [128, 8], F32)
    if layer == 1 and part == "kv":
        ks_sb = sb("ks_sb", [128, 8], F32)

    ps01 = psa("ps01", [128, 1024], F32)
    ps23 = psa("ps23", [128, 1024], F32)
    ps45 = psa("ps45", [128, 1024], F32)
    ps6 = psa("ps6", [128, 512], F32)
    psT = psa("psT", [128, 1024], BF16)

    A_STAGE, A_XT = 0, 8192
    A_KT, A_VT, A_V65, A_P, A_WO = 0, 8192, 10240, 14400, 0
    T.update(arena=arena, x_sb=x_sb, A_XT=A_XT, ps01=ps01, ps23=ps23,
             ones_col=ones_col, ones_row=ones_row, ones_row32=ones_row32, st_mu=st_mu, st_ms=st_ms,
             st_rs=st_rs, srow=srow, ev0=ev0, ev1=ev1,
             tf0=tf0, tf1=tf1)
    T["st_var"] = rl_row
    T["st_B"] = rl_row
    if part == "core":
        T["rl_dram"] = rl_dram
    bc64 = bcA if part == "kv" else bcP[:, 0:512]

    em.dma("gpsimd", x_sb[:].rearrange("p (c t) -> p c t", c=NC),
           din["x_in"].ap().rearrange("(c p) t -> p c t", p=128), "d_c")
    const_list = [("c_ones_col", ones_col), ("c_ones_row", ones_row),
                  ("c_ones_row32", ones_row32)]
    if part == "kv":
        const_list += [("c_ident", ident), ("c_mask01", mask01), ("c_sel2", sel2)]
    for nm, dst in const_list:
        em.dma("gpsimd", dst[:], din[nm].ap(), "d_c")
    if layer == 1 and part == "core":
        em.dma("gpsimd", qs_sb[:], din["qs_in"].ap(), "d_c")
        em.dma("gpsimd", bo_sb[:], din["bo_in"].ap(), "d_c")
    if layer == 1 and part == "kv":
        em.dma("gpsimd", ks_sb[:], din["ks_in"].ap(), "d_c")
    em.wait("vector", "d_c")
    em.wait("tensor", "d_c")

    def load_w(name):
        em.wait("gpsimd", "pe")
        return em.dma("gpsimd", w_sb[:].rearrange("p (c f) -> p c f", c=NC),
                      din[name].ap().rearrange("p (c f) -> p c f", c=NC), "d_w")

    def pslot(i):
        t = (ps01, ps23)[(i // 2) % 2]
        return t[:, (i % 2) * 512:(i % 2) * 512 + 512]

    dvs, acs, pes = sems["dv"], sems["ac"], sems["pe"]

    def norm_stage_slice(dst, kind, idx):
        nrows = 2 if kind == "k" else 1
        em.wait("vector", "dv")
        em.op("vector", lambda v: v.drain())
        v1 = em.inc("dv")
        em.op("vector", lambda v, dst=dst, s=dvs:
              v.tensor_tensor(sq_t[0:dst.shape[0], :], dst, dst, ALU.mult).then_inc(s, 1))
        em.wait("tensor", "dv", v1)
        em.inc("pe")
        if kind == "k":
            em.op("tensor", lambda t, s=pes:
                  t.matmul(ps6[0:2, :], mask01[:], sq_t[:], start=True, stop=True).then_inc(s, 1))
        else:
            em.op("tensor", lambda t, s=pes:
                  t.matmul(ps6[0:1, :], ones_col[0:64, :], sq_t[0:64, :],
                           start=True, stop=True).then_inc(s, 1))
        em.wait("scalar", "pe")
        em.op("scalar", lambda s, nr=nrows:
              s.activation(out=srow[0:nr, 0:512], in_=ps6[0:nr, :], func=AF.Ln, scale=f32(1.0)))
        em.op("scalar", lambda s: s.drain())
        va = em.inc("ac")
        em.op("scalar", lambda s, nr=nrows, ss=acs:
              s.activation(out=srow[0:nr, 512:1024], in_=srow[0:nr, 0:512], func=AF.Exp,
                           scale=f32(-0.5)).then_inc(ss, 1))
        em.wait("vector", "ac", va)
        v2 = em.inc("dv")
        em.op("vector", lambda v, nr=nrows, s=dvs:
              v.tensor_single_scalar(srow[0:nr, 0:512], srow[0:nr, 512:1024],
                                     1e12, ALU.min).then_inc(s, 1))
        em.wait("tensor", "dv", v2)
        em.inc("pe")
        if kind == "k":
            em.op("tensor", lambda t, s=pes:
                  t.matmul(ps6[:, :], sel2[:], srow[0:2, 0:512], start=True, stop=True).then_inc(s, 1))
        else:
            em.op("tensor", lambda t, s=pes:
                  t.matmul(ps6[0:64, :], ones_row32[0:1, 0:64], srow[0:1, 0:512],
                           start=True, stop=True).then_inc(s, 1))
        em.wait("vector", "pe")
        nb = dst.shape[0]
        bct = bc64 if nb == 64 else sq_t[0:nb, 0:512]
        em.op("vector", lambda v, nb=nb, bct=bct: v.tensor_copy(bct, ps6[0:nb, :]))
        em.op("vector", lambda v: v.drain())
        em.op("vector", lambda v, dst=dst, nb=nb, bct=bct:
              v.tensor_tensor(dst, dst, bct, ALU.mult))
        em.op("vector", lambda v: v.drain())
        sc = (ks_sb if kind == "k" else qs_sb)
        v4 = em.inc("dv")
        em.op("vector", lambda v, dst=dst, sc=sc, idx=idx, s=dvs:
              v.tensor_scalar_mul(dst, dst, sc[0:dst.shape[0], idx:idx + 1]).then_inc(s, 1))
        em.op("vector", lambda v: v.drain())
        return v4

    if part == "kv":
      load_w("w_k")
      em.wait("tensor", "d_w")
      for half in range(2):
        em.wait("vector", "d_sk")
        for tb in range(2):
            tok0 = half * HALF + tb * 512
            emit_xb_tiles(em, T, x_sb, tok0)
            em.wait("tensor", "dv")
            for fc in range(NC):
                outv = pslot(em.inc("psl") % 4)
                em.wait("tensor", "dv")
                emit_proj_group(em, T, w_sb, lambda ec, fc=fc: ec * 1024 + fc * 128, 128, outv)
                em.wait("vector", "pe")
                dst = arena[:, A_STAGE + fc * 1024 + tb * 512: A_STAGE + fc * 1024 + tb * 512 + 512]
                em.inc("dv")
                em.op("vector", lambda vv, outv=outv, dst=dst, s=dvs:
                      vv.tensor_copy(dst, outv).then_inc(s, 1))
        if layer == 1:
            scr = qt_sb[0:128, 0:1024]
            em.wait("vector", "pe")
            em.op("vector", lambda v: v.drain())
            km_mark = {}
            for fc in range(NC):
                stg = arena[:, A_STAGE + fc * 1024: A_STAGE + fc * 1024 + 1024]
                if fc >= 1:
                    em.wait("vector", "pe", km_mark[fc - 1])
                v1 = em.inc("dv")
                em.op("vector", lambda v, stg=stg, scr=scr, s=dvs:
                      v.tensor_tensor(scr, stg, stg, ALU.mult).then_inc(s, 1))
                em.wait("tensor", "dv", v1)
                m1 = em.inc("pe")
                em.op("tensor", lambda t, scr=scr, s=sems["pe"], mk=mask01:
                      (t.matmul(ps01[0:2, 0:512], mk[:], scr[:, 0:512],
                                start=True, stop=True),
                       t.matmul(ps01[0:2, 512:1024], mk[:], scr[:, 512:1024],
                                start=True, stop=True).then_inc(s, 1)))
                em.wait("scalar", "pe", m1)
                em.op("scalar", lambda sc: sc.activation(out=srow[0:2, 0:1024],
                                                         in_=ps01[0:2, 0:1024],
                                                         func=AF.Ln, scale=f32(1.0)))
                em.op("scalar", lambda sc: sc.drain())
                va = em.inc("ac")
                em.op("scalar", lambda sc, ss=acs:
                      sc.activation(out=srow[0:2, 0:1024], in_=srow[0:2, 0:1024],
                                    func=AF.Exp, scale=f32(-0.5)).then_inc(ss, 1))
                em.wait("vector", "ac", va)
                em.op("vector", lambda v: v.drain())
                v2 = em.inc("dv")
                em.op("vector", lambda v, s=dvs:
                      v.tensor_single_scalar(srow[0:2, 0:1024], srow[0:2, 0:1024],
                                             1e12, ALU.min).then_inc(s, 1))
                em.wait("tensor", "dv", v2)
                m2 = em.inc("pe")
                em.op("tensor", lambda t, s=sems["pe"], sl2=sel2:
                      (t.matmul(ps23[:, 0:512], sl2[:], srow[0:2, 0:512],
                                start=True, stop=True),
                       t.matmul(ps23[:, 512:1024], sl2[:], srow[0:2, 512:1024],
                                start=True, stop=True).then_inc(s, 1)))
                km_mark[fc] = m2
                em.wait("vector", "pe", m2)
                em.op("vector", lambda v, stg=stg:
                      v.tensor_tensor(stg, stg, ps23[:, 0:1024], ALU.mult))
                em.op("vector", lambda v: v.drain())
                em.inc("dv")
                em.op("vector", lambda v, stg=stg, fc=fc, s=dvs, ks=ks_sb:
                      v.tensor_scalar_mul(stg, stg, ks[:, fc:fc + 1]).then_inc(s, 1))
                em.op("vector", lambda v: v.drain())
        em.wait("sync", "dv")
        em.dma("sync",
               kt_loc.ap().rearrange("(c p) t -> p c t", p=128)[:, :, half * HALF:(half + 1) * HALF],
               arena[:, A_STAGE:A_STAGE + 8192].rearrange("p (c t) -> p c t", c=8), "d_sk")

    if part == "kv":
      load_w("w_v")
      em.wait("tensor", "d_w")
      for half in range(2):
        em.wait("vector", "d_sk")
        em.wait("vector", "d_sv")
        for tb in range(2):
            tok0 = half * HALF + tb * 512
            emit_xb_tiles(em, T, x_sb, tok0)
            em.wait("tensor", "dv")
            for fc in range(NC):
                outv = pslot(em.inc("psl") % 4)
                em.wait("tensor", "dv")
                emit_proj_group(em, T, w_sb, lambda ec, fc=fc: ec * 1024 + fc * 128, 128, outv)
                em.wait("vector", "pe")
                raw = (ev0, ev1)[em.inc("evs") % 2]
                v0 = em.inc("dv")
                em.op("vector", lambda vv, outv=outv, raw=raw, s=dvs:
                      vv.tensor_copy(raw[:], outv).then_inc(s, 1))
                em.wait("tensor", "dv", v0)
                em.wait("tensor", "dv")
                em.inc("pe")
                for tt in range(4):
                    em.op("tensor", lambda t, raw=raw, tt=tt, s=pes:
                          (t.transpose(psT[:, tt * 128: tt * 128 + 128],
                                       raw[:, tt * 128:tt * 128 + 128], ident[:]).then_inc(s, 1)
                           if tt == 3 else
                           t.transpose(psT[:, tt * 128: tt * 128 + 128],
                                       raw[:, tt * 128:tt * 128 + 128], ident[:])))
                em.wait("vector", "pe")
                for tt in range(4):
                    tc = tb * 4 + tt
                    dst = arena[:, A_STAGE + tc * 1024 + fc * 128: A_STAGE + tc * 1024 + fc * 128 + 128]
                    em.inc("dv")
                    em.op("vector", lambda vv, dst=dst, tt=tt, s=dvs:
                          vv.tensor_copy(dst, psT[:, tt * 128:tt * 128 + 128]).then_inc(s, 1))
        em.wait("sync", "dv")
        em.dma("sync",
               v_loc.ap().rearrange("(c p) f -> p c f", p=128)[:, half * 8:half * 8 + 8, :],
               arena[:, A_STAGE:A_STAGE + 8192].rearrange("p (c f) -> p c f", c=8), "d_sv")
      em.wait("sync", "d_sk")
      em.wait("sync", "d_sv")
      with nc.Block() as block:
          em.play(block)
      es.close()
      return nc

    CC_ALL = 0
    load_w("w_q")

    em.op("vector", lambda v: v.memset(qt_sb[64:128, :], 0.0))
    em.op("vector", lambda v: v.memset(zt_sb[64:128, :], 0.0))
    em.op("vector", lambda v: v.memset(arena[64:128, 0:8192], 0.0))
    em.op("vector", lambda v: v.memset(arena[:, 14400:18496], 0.0))
    em.op("vector", lambda v: v.drain())
    vpad = em.inc("dv")
    em.op("vector", lambda v, s=dvs: v.nop().then_inc(s, 1))
    em.wait("tensor", "dv", vpad)
    em.wait("sync", "dv", vpad)

    def kt_off(par, ab):
        return A_KT + (ab * 2 + par) * 2048

    kv_ld = {}
    for half in range(2):
        em.wait("tensor", "d_w")
        em.wait("tensor", "ac_e", em.val("ac_e"))
        for tb in range(2):
            tok0 = half * HALF + tb * 512
            emit_xb_tiles(em, T, x_sb, tok0)
            em.wait("tensor", "dv")
            for hc in range(16):
                outv = pslot(em.inc("psl") % 4)
                em.wait("tensor", "dv")
                emit_proj_group(em, T, w_sb, lambda ec, hc=hc: ec * 1024 + hc * 64, 64, outv)
                em.wait("vector", "pe")
                ab, pair = hc % 2, hc // 2
                dst = qt_sb[0:64, ab * 8192 + pair * 1024 + tb * 512:
                            ab * 8192 + pair * 1024 + tb * 512 + 512]
                em.inc("dv")
                em.op("vector", lambda vv, outv=outv, dst=dst, s=dvs:
                      vv.tensor_copy(dst, outv[0:64, :]).then_inc(s, 1))
        if layer == 1:
            scr = arena[0:64, 0:1024]
            em.wait("vector", "pe")
            em.op("vector", lambda v: v.drain())
            nm_mark = {}
            for idx in range(16):
                ab, pair = idx % 2, idx // 2
                hc = pair * 2 + ab
                qsl = qt_sb[0:64, ab * 8192 + pair * 1024: ab * 8192 + pair * 1024 + 1024]
                if idx >= 1:
                    em.wait("vector", "pe", nm_mark[idx - 1][0])
                v1 = em.inc("dv")
                em.op("vector", lambda v, qsl=qsl, scr=scr, s=dvs:
                      v.tensor_tensor(scr, qsl, qsl, ALU.mult).then_inc(s, 1))
                em.wait("tensor", "dv", v1)
                m1 = em.inc("pe")
                em.op("tensor", lambda t, scr=scr, s=sems["pe"], oc=ones_col:
                      (t.matmul(ps01[0:1, 0:512], oc[0:64, :], scr[:, 0:512],
                                start=True, stop=True),
                       t.matmul(ps01[0:1, 512:1024], oc[0:64, :], scr[:, 512:1024],
                                start=True, stop=True).then_inc(s, 1)))
                em.wait("scalar", "pe", m1)
                em.op("scalar", lambda sc: sc.activation(out=srow[0:1, 0:1024],
                                                         in_=ps01[0:1, 0:1024],
                                                         func=AF.Ln, scale=f32(1.0)))
                em.op("scalar", lambda sc: sc.drain())
                va = em.inc("ac")
                em.op("scalar", lambda sc, ss=acs:
                      sc.activation(out=rl_row[0:1, 0:1024], in_=srow[0:1, 0:1024],
                                    func=AF.Exp, scale=f32(-0.5)).then_inc(ss, 1))
                em.wait("vector", "ac", va)
                em.op("vector", lambda v: v.drain())
                v2 = em.inc("dv")
                em.op("vector", lambda v, s=dvs:
                      v.tensor_single_scalar(rl_row[0:1, 0:1024], rl_row[0:1, 0:1024],
                                             1e12, ALU.min).then_inc(s, 1))
                em.wait("tensor", "dv", v2)
                m2 = em.inc("pe")
                em.op("tensor", lambda t, s=sems["pe"], orow=ones_row32:
                      (t.matmul(ps23[0:64, 0:512], orow[0:1, 0:64], rl_row[0:1, 0:512],
                                start=True, stop=True),
                       t.matmul(ps23[0:64, 512:1024], orow[0:1, 0:64], rl_row[0:1, 512:1024],
                                start=True, stop=True).then_inc(s, 1)))
                em.wait("vector", "pe", m2)
                em.op("vector", lambda v, qsl=qsl:
                      v.tensor_tensor(qsl, qsl, ps23[0:64, 0:1024], ALU.mult))
                em.op("vector", lambda v: v.drain())
                v3 = em.inc("dv")
                em.op("vector", lambda v, qsl=qsl, hc=hc, s=dvs, qs=qs_sb:
                      v.tensor_scalar_mul(qsl, qsl, qs[:, hc:hc + 1]).then_inc(s, 1))
                nm_mark[idx] = (m2, v3)
        QP_pe = em.val("pe")
        QP_dv = em.val("dv")

        em.wait("vector", "pe")
        em.op("vector", lambda v:
              v.memset(arena[:, A_V65 + 0 * 2080:A_V65 + 1 * 2080]
                       .rearrange("p (c h u) -> p c h u", h=2, u=65)[:, :, :, 64:65], 1.0))
        em.op("vector", lambda v:
              v.memset(arena[:, A_V65 + 1 * 2080:A_V65 + 2 * 2080]
                       .rearrange("p (c h u) -> p c h u", h=2, u=65)[:, :, :, 64:65], 1.0))
        em.op("vector", lambda v: v.drain())
        vones = em.inc("dv")
        em.op("vector", lambda v, s=dvs: v.nop().then_inc(s, 1))

        for tidx in range(16):
            pair, qb = tidx // 2, tidx % 2
            j = half * 16 + tidx
            if tidx > 0:
                _emit_trio_evac(em, T, sems, half, tidx - 1, pair_prev=(tidx - 1) // 2,
                                qb_prev=(tidx - 1) % 2, zt_sb=zt_sb, ps45=ps45, ps6=ps6,
                                rl_row=rl_row, bcA=bcA)
            for g in range(4):
                Gg = j * 4 + g
                par = Gg % 2
                kvs = "d_kv%d" % par
                if Gg >= 2:
                    em.wait("sync", "pe_s", 16 * (Gg - 1))
                if Gg >= 1:
                    em.wait("sync", "dv_v65", Gg)
                if tidx == 0:
                    em.wait("sync", "pe", QP_pe)
                    em.wait("sync", "dv", QP_dv)

                em.dma("sync", arena[0:64, kt_off(par, 0):kt_off(par, 0) + 2048],
                       kt_full.ap()[g * 1024 + pair * 128: g * 1024 + pair * 128 + 64, :], kvs)
                em.dma("sync", arena[0:64, kt_off(par, 1):kt_off(par, 1) + 2048],
                       kt_full.ap()[g * 1024 + pair * 128 + 64: g * 1024 + pair * 128 + 128, :], kvs)
                em.dma("sync", arena[:, A_VT:A_VT + 2048]
                       .rearrange("p (c f) -> p c f", c=16),
                       v_full.ap().rearrange("(c p) f -> p c f", p=128)
                       [:, g * 16:(g + 1) * 16, pair * 128:(pair + 1) * 128], kvs)
                kv_ld[Gg] = (kvs, em.val(kvs))
                em.wait("vector", kv_ld[Gg][0], kv_ld[Gg][1])
                if Gg >= 2:
                    em.wait("vector", "pe_pv", 16 * (Gg - 1))
                if tidx == 0:
                    em.wait("vector", "dv", vones)
                vsrc = arena[:, A_VT:A_VT + 2048] \
                    .rearrange("p (c h dd) -> p c h dd", c=16, h=2, dd=64)
                vdst = arena[:, A_V65 + par * 2080:A_V65 + (par + 1) * 2080] \
                    .rearrange("p (c h u) -> p c h u", c=16, h=2, u=65)
                em.inc("dv_v65")
                em.op("vector", lambda v, vsrc=vsrc, vdst=vdst, s=sems["dv_v65"]:
                      v.tensor_copy(vdst[:, :, :, 0:64], vsrc).then_inc(s, 1))
            for ch in range(64):
                I = half * 1024 + tidx * 64 + ch
                Gg = I // 16
                par = Gg % 2
                psS = (ps01, ps23)[I % 2]
                c16 = ch % 16
                qA = qt_sb[0:128, pair * 1024 + qb * 512: pair * 1024 + qb * 512 + 512]
                qB = qt_sb[0:128, 8192 + pair * 1024 + qb * 512: 8192 + pair * 1024 + qb * 512 + 512]
                if c16 == 0:
                    em.wait("tensor", kv_ld[Gg][0], kv_ld[Gg][1])
                em.wait("tensor", "ac_e", I - 1)
                ktA = arena[0:128, kt_off(par, 0) + c16 * 128: kt_off(par, 0) + c16 * 128 + 128]
                ktB = arena[0:128, kt_off(par, 1) + c16 * 128: kt_off(par, 1) + c16 * 128 + 128]
                em.op("tensor", lambda t, ktA=ktA, qA=qA, psS=psS:
                      t.matmul(psS[:, 0:512], ktA, qA, start=True, stop=True))
                em.inc("pe_s")
                em.op("tensor", lambda t, ktB=ktB, qB=qB, psS=psS, s=sems["pe_s"]:
                      t.matmul(psS[:, 512:1024], ktB, qB, start=True, stop=True).then_inc(s, 1))
                if I - 2 >= half * 1024:
                    _emit_pv(em, T, sems, half, I - 2, arena, qt_sb, zt_sb, ps45, ps6,
                             rl_row, bcA, A_P, A_V65, kt_off)
                em.wait("scalar", "pe_s", I + 1)
                em.wait("scalar", "pe_pv", I - 2)
                em.inc("ac_e")
                pAB = arena[:, A_P + (I % 3) * 1024: A_P + (I % 3) * 1024 + 1024]
                em.op("scalar", lambda s, psS=psS, pAB=pAB, ss=sems["ac_e"]:
                      s.activation(out=pAB, in_=psS[:, 0:1024], func=AF.Exp,
                                   scale=f32(0.125)).then_inc(ss, 1))
        for Itail in (half * 1024 + 1022, half * 1024 + 1023):
            em.wait("tensor", "ac_e", Itail + 1)
            _emit_pv(em, T, sems, half, Itail, arena, qt_sb, zt_sb, ps45, ps6,
                     rl_row, bcA, A_P, A_V65, kt_off)
        _emit_trio_evac(em, T, sems, half, 15, pair_prev=7, qb_prev=1, zt_sb=zt_sb,
                        ps45=ps45, ps6=ps6, rl_row=rl_row, bcA=bcA)

        em.wait("gpsimd", "pe_pv", (half + 1) * 1024)
        em.dma("gpsimd", arena[0:64, A_WO:A_WO + 16384],
               din["w_o2"].ap(), "d_w")
        em.wait("tensor", "d_w")
        em.wait("tensor", "dv_z", (half + 1) * 16)
        for dc in range(NC):
            for tb in range(2):
                tok0 = half * HALF + tb * 512
                sl = em.inc("ops") % 2
                outv = ps45[:, sl * 512: sl * 512 + 512]
                em.wait("tensor", "dv")
                lhs = [arena[0:128, A_WO + hc * 1024 + dc * 128: A_WO + hc * 1024 + dc * 128 + 128]
                       for hc in range(16)]
                rhs = [zt_sb[0:128, hc * 1024 + tb * 512: hc * 1024 + tb * 512 + 512]
                       for hc in range(16)]
                emit_mm_group(em, T, lhs, rhs, outv, 128)
                em.wait("vector", "pe")
                xs = x_sb[:, dc * TOK + tok0: dc * TOK + tok0 + 512]
                if layer == 0:
                    em.inc("dv")
                    em.op("vector", lambda v, xs=xs, outv=outv, s=dvs:
                          v.scalar_tensor_tensor(out=xs, in0=xs, scalar=2.0, in1=outv,
                                                 op0=ALU.mult, op1=ALU.add).then_inc(s, 1))
                    em.op("vector", lambda v: v.drain())
                else:
                    em.wait("scalar", "pe")
                    tfx = (tf0, tf1)[em.inc("tfs") % 2]
                    va = em.inc("ac")
                    em.op("scalar", lambda s, outv=outv, tfx=tfx, dc=dc, ss=acs, bo=bo_sb:
                          s.activation(out=tfx[:], in_=outv, func=AF.Identity,
                                       bias=bo[:, dc:dc + 1], scale=f32(1.0)).then_inc(ss, 1))
                    em.wait("vector", "ac", va)
                    em.inc("dv")
                    em.op("vector", lambda v, xs=xs, tfx=tfx, s=dvs:
                          v.tensor_tensor(xs, xs, tfx[:], ALU.add).then_inc(s, 1))
                    em.op("vector", lambda v: v.drain())

    em.wait("tensor", "ac_e", em.val("ac_e"))
    emit_ln(em, T, x_sb, None, None)
    emit_store_x(em, T, x_out)

    with nc.Block() as block:
        em.play(block)
    es.close()
    return nc


def _emit_pv(em, T, sems, half, Iprev, arena, qt_sb, zt_sb, ps45, ps6, rl_row, bcA,
             A_P, A_V65, kt_off):
    i_loc = Iprev - half * 1024
    tidx = i_loc // 64
    ch = i_loc % 64
    pair, qb = tidx // 2, tidx % 2
    Gg = Iprev // 16
    par = Gg % 2
    c16 = ch % 16
    j = half * 16 + tidx
    if ch == 0:
        em.wait("tensor", "dv_z", j)
    if c16 == 0:
        em.wait("tensor", "dv_v65", Gg + 1)
    vA = arena[:, A_V65 + par * 2080 + c16 * 130: A_V65 + par * 2080 + c16 * 130 + 65]
    vB = arena[:, A_V65 + par * 2080 + c16 * 130 + 65: A_V65 + par * 2080 + c16 * 130 + 130]
    pA = arena[:, A_P + (Iprev % 3) * 1024: A_P + (Iprev % 3) * 1024 + 512]
    pB = arena[:, A_P + (Iprev % 3) * 1024 + 512: A_P + (Iprev % 3) * 1024 + 1024]
    em.op("tensor", lambda t, vA=vA, pA=pA, st=(ch == 0), sp=(ch == 63):
          t.matmul(ps45[0:65, 0:512], vA, pA, start=st, stop=sp))
    em.inc("pe_pv")
    em.op("tensor", lambda t, vB=vB, pB=pB, st=(ch == 0), sp=(ch == 63), s=sems["pe_pv"]:
          t.matmul(ps45[0:65, 512:1024], vB, pB, start=st, stop=sp).then_inc(s, 1))



def _emit_trio_evac(em, T, sems, half, tidx, pair_prev, qb_prev, zt_sb, ps45, ps6,
                    rl_row, bcA):
    j = half * 16 + tidx
    dvs = em.sems["dv"]
    bcP = T["bcP"]
    srow = T["srow"]
    em.wait("scalar", "pe_pv", 64 * (j + 1))
    em.op("scalar", lambda s: s.activation(out=srow[0:1, 0:1024], in_=ps45[64:65, 0:1024],
                                           func=AF.Ln, scale=f32(1.0)))
    em.op("scalar", lambda s: s.drain())
    em.inc("dv_z1")
    em.op("scalar", lambda s, ss=sems["dv_z1"]:
          s.activation(out=rl_row[0:1, 0:1024], in_=srow[0:1, 0:1024], func=AF.Exp,
                       scale=f32(-1.0)).then_inc(ss, 1))
    em.wait("gpsimd", "dv_z1", j + 1)
    em.wait("gpsimd", "dv_z", j)
    em.dma("gpsimd", T["rl_dram"].ap(), rl_row[0:1, 0:1024], "d_w1")
    em.wait("gpsimd", "d_w1")
    em.dma("gpsimd", bcP[:, 0:1024], T["rl_dram"].ap().partition_broadcast(64), "d_w0")
    em.wait("vector", "d_w0")
    dstA = zt_sb[0:64, (2 * pair_prev) * 1024 + qb_prev * 512:
                 (2 * pair_prev) * 1024 + qb_prev * 512 + 512]
    em.inc("dv_z2")
    em.op("vector", lambda v, dstA=dstA, s=sems["dv_z2"]:
          v.tensor_tensor(dstA, ps45[0:64, 0:512], bcP[:, 0:512], ALU.mult).then_inc(s, 1))
    dstB = zt_sb[0:64, (2 * pair_prev + 1) * 1024 + qb_prev * 512:
                 (2 * pair_prev + 1) * 1024 + qb_prev * 512 + 512]
    em.inc("dv_z")
    em.op("vector", lambda v, dstB=dstB, s=sems["dv_z"]:
          v.tensor_tensor(dstB, ps45[0:64, 512:1024], bcP[:, 512:1024], ALU.mult).then_inc(s, 1))





def build_ffn(layer: int):
    nc = bass.Bass("TRN2", target_bir_lowering=False)
    din = {}

    def dram_in(name, shape, dt=F32):
        din[name] = nc.dram_tensor(name, list(shape), dt, kind="ExternalInput")
        return din[name]

    dram_in("x_in", [D, TOK])
    dram_in("c_ones_col", [128, 1])
    dram_in("c_ones_row32", [1, 128])
    dram_in("c_minus_one", [1, 512])
    dram_in("c_ident", [128, 128])
    if layer == 0:
        dram_in("q1_cm", [128, 8 * 4096], I32)
        dram_in("q2_cm", [128, 32 * 1024], I32)
        dram_in("s1_flat", [1, 32768])
        dram_in("s2_flat", [1, 32768])
        dram_in("b1_c", [128, 32])
        dram_in("b2_c", [128, 8])
        dram_in("g_cols", [128, 8])
        dram_in("b_cols", [128, 8])
    else:
        dram_in("wexp_cm", [8, 128, 8192])
        dram_in("b_exp", [8, 1024])
        dram_in("lg_in", [1, 8])
        dram_in("uu_in", [1, 8])
    x_out = nc.dram_tensor("x_out", [D, TOK], F32, kind="ExternalOutput")
    if layer == 0:
        w1bf_dram = nc.dram_tensor("w1bf_dram", [128, 8 * 4096], BF16)
        w2bf_dram = nc.dram_tensor("w2bf_dram", [128, 32 * 1024], BF16)

    es = contextlib.ExitStack()
    sems = {ss: es.enter_context(nc.semaphore(ss)) for ss in SEM_NAMES}
    em = Emit(sems)
    T = {}
    sb, psa = _common_setup(nc, T, es)

    x_sb = sb("x_sb", [128, NC * TOK], F32)
    if layer == 0:
        arena = sb("arena", [128, 36864], BF16)
        A_H, A_Y, A_XT = 0, 16384, 32768
        i32a = sb("i32a", [128, 2048], I32)
        i32b = sb("i32b", [128, 2048], I32)
        spa = sb("spa", [128, 2048], F32)
        spb = sb("spb", [128, 2048], F32)
        w1bf0 = sb("w1bf0", [128, 2048], BF16)
        w1bf1 = sb("w1bf1", [128, 2048], BF16)
        sw0 = sb("sw0", [128, 2048], BF16)
        sw1 = sb("sw1", [128, 2048], BF16)
        g_c = sb("g_c", [128, 8], F32)
        b_c = sb("b_c", [128, 8], F32)
        b1c = sb("b1c", [128, 32], F32)
        b2c = sb("b2c", [128, 8], F32)
        hb2 = sb("hb2", [128, 8], F32)
    else:
        arena = sb("arena", [128, 12288], BF16)
        A_XT = 8192
        weff = sb("weff", [128, 8192], BF16)
        wt0 = sb("wt0", [128, 1024], F32)
        wt1 = sb("wt1", [128, 1024], F32)
        acc0 = sb("acc0", [128, 1024], F32)
        acc1 = sb("acc1", [128, 1024], F32)
        b_sb = sb("b_sb", [8, 1024], BF16)
        lg = sb("lg", [1, 8], F32)
        uu = sb("uu", [1, 8], F32)
        oh_r = sb("oh_r", [1, 8], F32)
        oh_b = sb("oh_b", [128, 128], BF16)
        ohT = sb("ohT", [8, 1], BF16)
        ohb_sb = sb("ohb_sb", [128, 8], F32)
        beff_sb = sb("beff_sb", [128, 8], F32)
    mo = sb("minus_one", [1, 512], F32)
    ev0 = sb("ev0", [128, 512], BF16)
    ev1 = sb("ev1", [128, 512], BF16)
    tf0 = sb("tf0", [128, 512], F32)
    tf1 = sb("tf1", [128, 512], F32)
    srow = sb("srow", [2, 1024], F32)
    st_mu = sb("st_mu", [1, 512], F32)
    st_ms = sb("st_ms", [1, 512], F32)
    st_vb = sb("st_vb", [1, 1024], F32)
    st_rs = sb("st_rs", [1, 512], F32)
    ones_col = sb("ones_col", [128, 1], BF16)
    ones_row32 = sb("ones_row32", [1, 128], F32)
    ident = sb("ident", [128, 128], BF16)

    ps01 = psa("ps01", [128, 1024], F32)
    ps23 = psa("ps23", [128, 1024], F32)
    ps45 = psa("ps45", [128, 1024], F32)
    if layer == 1:
        psT = psa("psT", [128, 128], BF16)

    T.update(arena=arena, x_sb=x_sb, A_XT=A_XT, ps01=ps01, ps23=ps23,
             ones_col=ones_col, ones_row32=ones_row32, st_mu=st_mu, st_ms=st_ms,
             st_rs=st_rs, srow=srow, ev0=ev0, ev1=ev1,
             tf0=tf0, tf1=tf1, minus_one=mo, st_var=st_vb, st_B=st_vb)
    dvs, acs, pes = sems["dv"], sems["ac"], sems["pe"]

    em.dma("gpsimd", x_sb[:].rearrange("p (c t) -> p c t", c=NC),
           din["x_in"].ap().rearrange("(c p) t -> p c t", p=128), "d_c")
    for nm, dst in (("c_ones_col", ones_col), ("c_ones_row32", ones_row32),
                    ("c_minus_one", mo), ("c_ident", ident)):
        em.dma("gpsimd", dst[:], din[nm].ap(), "d_c")
    if layer == 0:
        for nm, dst in (("g_cols", g_c), ("b_cols", b_c), ("b1_c", b1c), ("b2_c", b2c)):
            em.dma("gpsimd", dst[:], din[nm].ap(), "d_c")
    else:
        em.dma("gpsimd", b_sb[:], din["b_exp"].ap(), "d_c")
        em.dma("gpsimd", lg[:], din["lg_in"].ap(), "d_c")
        em.dma("gpsimd", uu[:], din["uu_in"].ap(), "d_c")
    em.wait("vector", "d_c")
    em.wait("tensor", "d_c")
    em.wait("scalar", "d_c")

    if layer == 0:
        v0 = em.inc("dv")
        em.op("vector", lambda v, s=dvs: v.tensor_scalar_mul(hb2[:], b2c[:], 0.5).then_inc(s, 1))

        slot_mark = [0, 0]

        def sweep(nblk, qsrc, sflat, dst):
            for blk in range(nblk):
                sl = em.inc("wsl") % 2
                dwn, dsn = "d_w%d" % sl, "d_kv%d" % sl
                i32t, spt, wbf = ((i32a, spa, sw0), (i32b, spb, sw1))[sl]
                em.wait("gpsimd", "dv", slot_mark[sl])
                em.wait("sync", "dv", slot_mark[sl])
                vq = em.dma("gpsimd", i32t[:], qsrc(blk), dwn)
                vs = em.dma("sync", spt[:], sflat(blk), dsn)
                em.wait("vector", dwn, vq)
                em.wait("vector", dsn, vs)
                em.wait("vector", "d_sk", max(0, em.val("d_sk") - 16))
                em.inc("dv")
                em.op("vector", lambda v, i32t=i32t, spt=spt, wbf=wbf, s=dvs:
                      v.scalar_tensor_tensor(out=wbf[:], in0=i32t[:], scalar=-128.0,
                                             in1=spt[:], op0=ALU.add,
                                             op1=ALU.mult).then_inc(s, 1))
                em.wait("sync", "dv")
                em.dma("sync", dst(blk), wbf[:], "d_sk")
                slot_mark[sl] = em.val("dv")

        def sweep_part(which, b0, nblk):
            qn, sn, dst = (("q1_cm", "s1_flat", w1bf_dram) if which == 1
                           else ("q2_cm", "s2_flat", w2bf_dram))
            sweep(nblk,
                  lambda blk: din[qn].ap()[:, (b0 + blk) * 2048:(b0 + blk) * 2048 + 2048],
                  lambda blk: din[sn].ap()[0:1, (b0 + blk) * 2048:(b0 + blk) * 2048 + 2048]
                  .partition_broadcast(128),
                  lambda blk: dst.ap()[:, (b0 + blk) * 2048:(b0 + blk) * 2048 + 2048])

        sweep_part(1, 0, 16)
        SWEEP_ST1 = em.val("d_sk")
        emit_ln(em, T, x_sb, (g_c, b_c), A_H)
        sweep_part(2, 0, 16)
        SWEEP_ST = em.val("d_sk")

        wslot_mark = [0, 0]

        def wbf_load(cols_ap, gate):
            sl = em.inc("wbl") % 2
            dwn = "d_w%d" % sl
            wbf = (w1bf0, w1bf1)[sl]
            em.wait("gpsimd", "pe", wslot_mark[sl])
            if em.val("hot_gate") < gate:
                em.ctr["hot_gate"] = gate
                em.wait("gpsimd", "d_sk", gate)
            v = em.dma("gpsimd", wbf[:], cols_ap, dwn)
            return sl, wbf, v, dwn

        for tb in range(4):
            tok0 = tb * 512
            for fp in range(16):
                sl, wbf, vld, dwn = wbf_load(
                    w1bf_dram.ap().rearrange("p (c f) -> p c f", c=8)
                    [:, :, fp * 256:fp * 256 + 256], SWEEP_ST1)
                em.wait("tensor", dwn, vld)
                for fo in range(2):
                    fcol = fp * 2 + fo
                    sl4 = em.inc("psl") % 4
                    outv = (ps01, ps23)[sl4 // 2][:, (sl4 % 2) * 512:(sl4 % 2) * 512 + 512]
                    em.wait("tensor", "ac", max(0, em.val("ac") - 2))
                    for ec in range(NC):
                        kk = ec * 2 + fo
                        rh = arena[:, A_H + ec * TOK + tok0:A_H + ec * TOK + tok0 + 512]
                        if ec == NC - 1:
                            em.inc("pe")
                            em.op("tensor", lambda t, wbf=wbf, kk=kk, outv=outv, ec=ec, rh=rh, s=pes:
                                  t.matmul(outv, wbf[:, kk * 128:kk * 128 + 128], rh,
                                           start=(ec == 0), stop=True).then_inc(s, 1))
                        else:
                            em.op("tensor", lambda t, wbf=wbf, kk=kk, outv=outv, ec=ec, rh=rh:
                                  t.matmul(outv, wbf[:, kk * 128:kk * 128 + 128], rh,
                                           start=(ec == 0), stop=False))
                    em.wait("scalar", "pe")
                    em.inc("ac")
                    em.op("scalar", lambda sc, outv=outv, fcol=fcol, ss=acs:
                          sc.activation(out=arena[:, A_Y + fcol * 512:A_Y + fcol * 512 + 512],
                                        in_=outv, func=AF.Silu,
                                        bias=b1c[:, fcol:fcol + 1], scale=f32(1.0)).then_inc(ss, 1))
                wslot_mark[sl] = em.val("pe")
            em.wait("tensor", "ac")
            for dc in range(NC):
                sl2 = em.inc("ps2") % 2
                outv = ps45[:, sl2 * 512: sl2 * 512 + 512]
                em.wait("tensor", "dv", max(0, em.val("dv") - 1))
                for half16 in range(2):
                    sl, wbf, vld, dwn = wbf_load(
                        w2bf_dram.ap().rearrange("p (c f) -> p c f", c=32)
                        [:, half16 * 16:half16 * 16 + 16, dc * 128:dc * 128 + 128], SWEEP_ST)
                    em.wait("tensor", dwn, vld)
                    for kk in range(16):
                        fc = half16 * 16 + kk
                        rh = arena[:, A_Y + fc * 512:A_Y + fc * 512 + 512]
                        if kk == 15:
                            em.inc("pe")
                            em.op("tensor", lambda t, wbf=wbf, kk=kk, outv=outv, fc=fc, rh=rh,
                                  s=pes, last=(fc == 31):
                                  t.matmul(outv, wbf[:, kk * 128:kk * 128 + 128], rh,
                                           start=(fc == 0), stop=last).then_inc(s, 1))
                        else:
                            em.op("tensor", lambda t, wbf=wbf, kk=kk, outv=outv, fc=fc, rh=rh:
                                  t.matmul(outv, wbf[:, kk * 128:kk * 128 + 128], rh,
                                           start=(fc == 0), stop=False))
                    wslot_mark[sl] = em.val("pe")
                em.wait("scalar", "pe")
                tfx = (tf0, tf1)[em.inc("tfs") % 2]
                va = em.inc("ac")
                em.op("scalar", lambda sc, outv=outv, tfx=tfx, dc=dc, ss=acs:
                      sc.activation(out=tfx[:], in_=outv, func=AF.Identity,
                                    bias=hb2[:, dc:dc + 1], scale=f32(0.5)).then_inc(ss, 1))
                em.wait("vector", "ac", va)
                em.wait("vector", "dv", v0)
                xs = x_sb[:, dc * TOK + tok0: dc * TOK + tok0 + 512]
                em.inc("dv")
                em.op("vector", lambda v, xs=xs, tfx=tfx, s=dvs:
                      v.scalar_tensor_tensor(out=xs, in0=xs, scalar=2.0, in1=tfx[:],
                                             op0=ALU.mult, op1=ALU.add).then_inc(s, 1))
                em.op("vector", lambda v: v.drain())
    else:
            em.op("scalar", lambda s: s.activation(out=srow[0:1, 0:8], in_=uu[:], func=AF.Ln,
                                                   scale=f32(1.0)))
            em.op("scalar", lambda s: s.drain())
            va = em.inc("ac")
            em.op("scalar", lambda s, ss=acs:
                  s.activation(out=srow[0:1, 8:16], in_=srow[0:1, 0:8], func=AF.Ln,
                               scale=f32(-1.0)).then_inc(ss, 1))
            em.wait("vector", "ac", va)
            em.op("vector", lambda v: v.tensor_tensor(oh_r[:], lg[:], srow[0:1, 8:16], ALU.subtract))
            em.op("vector", lambda v: v.drain())
            em.op("vector", lambda v: v.reduce_max(srow[0:1, 16:17], oh_r[:], axis=mybir.AxisListType.X))
            em.op("vector", lambda v: v.drain())
            em.op("vector", lambda v: v.tensor_scalar(out=oh_r[:], in0=oh_r[:],
                                                      scalar1=srow[0:1, 16:17], scalar2=None,
                                                      op0=ALU.is_equal))
            em.op("vector", lambda v: v.drain())
            em.op("vector", lambda v: v.memset(oh_b[:], 0.0))
            em.op("vector", lambda v: v.drain())
            v1 = em.inc("dv")
            em.op("vector", lambda v, s=dvs: v.tensor_copy(oh_b[0:1, 0:8], oh_r[:]).then_inc(s, 1))
            em.wait("tensor", "dv", v1)
            em.inc("pe")
            em.op("tensor", lambda t, s=pes:
                  t.transpose(psT[:], oh_b[:], ident[:]).then_inc(s, 1))
            em.inc("pe")
            em.op("tensor", lambda t, s=pes:
                  t.matmul(ps45[:, 0:8], ones_row32[:], oh_r[:], start=True, stop=True).then_inc(s, 1))
            em.wait("vector", "pe")
            v2 = em.inc("dv")
            em.op("vector", lambda v, s=dvs: v.tensor_copy(ohT[:], psT[0:8, 0:1]).then_inc(s, 1))
            v3 = em.inc("dv")
            em.op("vector", lambda v, s=dvs: v.tensor_copy(ohb_sb[:], ps45[:, 0:8]).then_inc(s, 1))
            em.wait("tensor", "dv", v3)
            for dc in range(NC):
                last = dc == NC - 1
                em.inc("pe")
                em.op("tensor", lambda t, dc=dc, s=pes:
                      t.matmul(ps23[:, dc:dc + 1], b_sb[:, dc * 128:dc * 128 + 128], ohT[:],
                               start=True, stop=True).then_inc(s, 1))
            em.wait("vector", "pe")
            v4 = em.inc("dv")
            em.op("vector", lambda v, s=dvs: v.tensor_copy(beff_sb[:], ps23[:, 0:8]).then_inc(s, 1))
            wm = [0, 0]
            for dcin in range(NC):
                for i in range(8):
                    sl = em.inc("wsl") % 2
                    dwn = "d_w%d" % sl
                    wt = (wt0, wt1)[sl]
                    eng = ("gpsimd", "sync")[sl]
                    em.wait(eng, "dv", wm[sl])
                    vq = em.dma(eng, wt[:], din["wexp_cm"].ap()
                                [i, :, dcin * 1024:dcin * 1024 + 1024], dwn)
                    em.wait("vector", dwn, vq)
                    aco, aci = (acc0, acc1) if i % 2 == 0 else (acc1, acc0)
                    vv = em.inc("dv")
                    if i == 0:
                        em.op("vector", lambda v, wt=wt, aco=aco, i=i, s=dvs:
                              v.tensor_scalar_mul(aco[:], wt[:], ohb_sb[:, i:i + 1]).then_inc(s, 1))
                    elif i < 7:
                        em.op("vector", lambda v, wt=wt, aco=aco, aci=aci, i=i, s=dvs:
                              v.scalar_tensor_tensor(out=aco[:], in0=wt[:], scalar=ohb_sb[:, i:i + 1],
                                                     in1=aci[:], op0=ALU.mult, op1=ALU.add).then_inc(s, 1))
                    else:
                        em.op("vector", lambda v, wt=wt, aci=aci, dcin=dcin, s=dvs:
                              v.scalar_tensor_tensor(out=weff[:, dcin * 1024:dcin * 1024 + 1024],
                                                     in0=wt[:], scalar=ohb_sb[:, i:i + 1],
                                                     in1=aci[:], op0=ALU.mult, op1=ALU.add).then_inc(s, 1))
                    wm[sl] = em.val("dv")
            WEFF_dv = em.val("dv")
            for tb in range(4):
                tok0 = tb * 512
                emit_xb_tiles(em, T, x_sb, tok0)
                em.wait("tensor", "dv")
                for eo in range(NC):
                    sl4 = em.inc("psl") % 4
                    outv = (ps01, ps23)[sl4 // 2][:, (sl4 % 2) * 512:(sl4 % 2) * 512 + 512]
                    em.wait("tensor", "ac")
                    lhs = [weff[:, din_ * 1024 + eo * 128: din_ * 1024 + eo * 128 + 128]
                           for din_ in range(NC)]
                    rhs = [xb_slice(T, din_) for din_ in range(NC)]
                    emit_mm_group(em, T, lhs, rhs, outv, 128)
                    em.wait("scalar", "pe")
                    tfx = (tf0, tf1)[em.inc("tfs") % 2]
                    va = em.inc("ac")
                    em.op("scalar", lambda s, outv=outv, tfx=tfx, eo=eo, ss=acs:
                          s.activation(out=tfx[:], in_=outv, func=AF.Identity,
                                       bias=beff_sb[:, eo:eo + 1], scale=f32(1.0)).then_inc(ss, 1))
                    em.wait("vector", "ac", va)
                    xs = x_sb[:, eo * TOK + tok0: eo * TOK + tok0 + 512]
                    em.inc("dv")
                    em.op("vector", lambda v, xs=xs, tfx=tfx, s=dvs:
                          v.tensor_tensor(xs, xs, tfx[:], ALU.add).then_inc(s, 1))
                    em.op("vector", lambda v: v.drain())

    em.wait("tensor", "ac_e", em.val("ac_e"))
    emit_ln(em, T, x_sb, None, None)
    emit_store_x(em, T, x_out)

    with nc.Block() as block:
        em.play(block)
    es.close()
    return nc



def _consts():
    m01 = np.zeros((128, 2), np.float32)
    m01[0:64, 0] = 1.0
    m01[64:128, 1] = 1.0
    s2 = np.zeros((2, 128), np.float32)
    s2[0, 0:64] = 1.0
    s2[1, 64:128] = 1.0
    return {
        "c_ones_col": np.ones((128, 1), np.float32),
        "c_ones_row": np.ones((1, 128), np.float32),
        "c_ones_row32": np.ones((1, 128), np.float32),
        "c_ident": np.eye(128, dtype=np.float32),
        "c_mask01": m01,
        "c_sel2": s2,
        "c_minus_one": np.full((1, 512), -1.0, np.float32),
    }


def _w_chunk_major(w):
    K, N = w.shape
    return np.ascontiguousarray(w.reshape(K // 128, 128, N).transpose(1, 0, 2).reshape(128, -1))


def _wo2_layout(wo_eD):
    return np.ascontiguousarray(wo_eD.reshape(16, 64, 1024).transpose(1, 0, 2).reshape(64, -1))


def _attn_inputs(layer, part, xT, W):
    ins = {"x_in": np.ascontiguousarray(xT)}
    ins.update(_consts())
    if layer == 0:
        if part == "kv":
            ins["w_k"] = _w_chunk_major(W["L0_attn_Wk"])
            ins["w_v"] = _w_chunk_major(W["L0_attn_Wv"])
        else:
            ins["w_q"] = _w_chunk_major(W["L0_attn_Wq"])
            ins["w_o2"] = _wo2_layout(W["L0_attn_Wout"])
    else:
        if part == "kv":
            ins["w_k"] = _w_chunk_major(np.ascontiguousarray(W["L1_attn_Wk"].T))
            ins["w_v"] = _w_chunk_major(np.ascontiguousarray(W["L1_attn_Wv"].T))
            ins["ks_in"] = np.ascontiguousarray(
                W["L1_attn_k_scale"].reshape(16, 64).reshape(8, 128).T)
        else:
            ins["w_q"] = _w_chunk_major(np.ascontiguousarray(W["L1_attn_Wq"].T))
            ins["w_o2"] = _wo2_layout(np.ascontiguousarray(W["L1_attn_Wo"].T))
            ins["qs_in"] = np.ascontiguousarray(W["L1_attn_q_scale"].reshape(16, 64).T)
            ins["bo_in"] = np.ascontiguousarray(W["L1_attn_bo"].reshape(8, 128).T)
    return ins


def run_attn_block(layer, shards, W):
    nc_kv = build_attn(layer, "kv")
    in_maps = [_attn_inputs(layer, "kv", shards[c], W) for c in range(8)]
    res = run_bass_kernel_spmd(nc_kv, in_maps, core_ids=list(range(8)))
    ktf = {}
    vf = {}
    for g in range(2):
        ktf[g] = np.concatenate([np.asarray(res.results[g * 4 + c]["kt_out"]) for c in range(4)], axis=0)
        vf[g] = np.concatenate([np.asarray(res.results[g * 4 + c]["v_out"]) for c in range(4)], axis=0)
    nc_core = build_attn(layer, "core")
    in_maps = []
    for c in range(8):
        m = _attn_inputs(layer, "core", shards[c], W)
        m["kt_full"] = ktf[c // 4]
        m["v_full"] = vf[c // 4]
        in_maps.append(m)
    res = run_bass_kernel_spmd(nc_core, in_maps, core_ids=list(range(8)))
    return [np.ascontiguousarray(np.asarray(r["x_out"])) for r in res.results]


def _ffn_inputs(layer, xT, W):
    ins = {"x_in": np.ascontiguousarray(xT)}
    ins.update(_consts())
    if layer == 0:
        q1 = W["L0_ffn_fc1_q"].astype(np.int32)
        q2 = W["L0_ffn_fc2_q"].astype(np.int32)
        ins["q1_cm"] = np.ascontiguousarray(q1.reshape(8, 128, 4096).transpose(1, 0, 2).reshape(128, -1))
        ins["q2_cm"] = np.ascontiguousarray(q2.reshape(32, 128, 1024).transpose(1, 0, 2).reshape(128, -1))
        ins["s1_flat"] = np.ascontiguousarray(W["L0_ffn_fc1_scales"].reshape(1, -1).astype(np.float32))
        ins["s2_flat"] = np.ascontiguousarray(W["L0_ffn_fc2_scales"].reshape(1, -1).astype(np.float32))
        ins["b1_c"] = np.ascontiguousarray(W["L0_ffn_fc1_bias"].reshape(32, 128).T.astype(np.float32))
        ins["b2_c"] = np.ascontiguousarray(W["L0_ffn_fc2_bias"].reshape(8, 128).T.astype(np.float32))
        ins["g_cols"] = np.ascontiguousarray(W["L0_ffn_ln_gamma"].reshape(8, 128).T.astype(np.float32))
        ins["b_cols"] = np.ascontiguousarray(W["L0_ffn_ln_beta"].reshape(8, 128).T.astype(np.float32))
    else:
        we = W["L1_ffn_W"].astype(np.float32)
        ins["wexp_cm"] = np.ascontiguousarray(
            we.reshape(8, 8, 128, 1024).transpose(0, 2, 1, 3).reshape(8, 128, 8192))
        ins["b_exp"] = np.ascontiguousarray(W["L1_ffn_b"].astype(np.float32))
        ins["lg_in"] = np.ascontiguousarray(W["L1_ffn_logits"].reshape(1, 8).astype(np.float32))
        ins["uu_in"] = np.ascontiguousarray(W["L1_ffn_u"].reshape(1, 8).astype(np.float32))
    return ins


def run_ffn_block(layer, shards, W):
    nc = build_ffn(layer)
    in_maps = [_ffn_inputs(layer, shards[c], W) for c in range(8)]
    res = run_bass_kernel_spmd(nc, in_maps, core_ids=list(range(8)))
    return [np.ascontiguousarray(np.asarray(r["x_out"])) for r in res.results]


def kernel(**inputs):
    W = {k: np.asarray(v) for k, v in inputs.items()}
    x = W["L0_attn_x"]
    shards = []
    for c in range(8):
        b, t0 = c // 4, (c % 4) * TOK
        shards.append(np.ascontiguousarray(x[b, t0:t0 + TOK, :].T.astype(np.float32)))

    shards = run_attn_block(0, shards, W)
    shards = run_ffn_block(0, shards, W)
    shards = run_attn_block(1, shards, W)
    shards = run_ffn_block(1, shards, W)
    out = np.empty((2, SEQ, D), np.float32)
    for c in range(8):
        b, t0 = c // 4, (c % 4) * TOK
        out[b, t0:t0 + TOK, :] = shards[c].T
    return out


# version 07
# speedup vs baseline: 1.1815x; 1.1815x over previous
import contextlib
import numpy as np
import ml_dtypes
import concourse.bass as bass
import concourse.mybir as mybir
from concourse.bass_utils import run_bass_kernel_spmd

F32 = mybir.dt.float32
BF16 = mybir.dt.bfloat16
I32 = mybir.dt.int32
AF = mybir.ActivationFunctionType
ALU = mybir.AluOpType

D = 1024
TOK = 2048
SEQ = 8192
NC = 8
HALF = 1024
EPS = 1e-5
ENGINES = ("sync", "gpsimd", "tensor", "scalar", "vector")


def f32(x):
    return np.float32(x)


class Emit:

    def __init__(self, sems):
        self.progs = {e: [] for e in ENGINES}
        self.ctr = {}
        self.sems = sems

    def op(self, eng, fn):
        self.progs[eng].append(fn)

    def val(self, name):
        return self.ctr.get(name, 0)

    def inc(self, name, k=1):
        self.ctr[name] = self.val(name) + k
        return self.ctr[name]

    def wait(self, eng, ctr, v=None):
        if v is None:
            v = self.val(ctr)
        if v <= 0:
            return
        s = self.sems[ctr]
        self.op(eng, lambda e, s=s, v=v: e.wait_ge(s, v))

    def dma(self, eng, out, in_, ctr):
        v = self.inc(ctr, 16)
        s = self.sems[ctr]
        self.op(eng, lambda e, out=out, in_=in_, s=s:
                e.dma_start(out=out, in_=in_).then_inc(s, 16))
        return v

    def play(self, block):
        for eng in ENGINES:
            prog = self.progs[eng]

            def run(e, prog=prog):
                for fn in prog:
                    fn(e)
            getattr(block, eng)(run)


SEM_NAMES = ("d_c", "d_w", "d_w0", "d_w1", "d_sk", "d_sv", "d_kv0", "d_kv1", "d_out", "cc",
             "pe", "ac", "dv", "pe_s", "pe_pv", "ac_e",
             "dv_v65", "dv_z", "dv_z1", "dv_z2", "pe_bc")


def _common_setup(nc, T, es):
    def sb(name, shape, dt):
        T[name] = es.enter_context(nc.sbuf_tensor(name, list(shape), dt))
        return T[name]

    def psa(name, shape, dt):
        T[name] = es.enter_context(nc.psum_tensor(name, list(shape), dt))
        return T[name]
    return sb, psa



def emit_xb_tiles(em, T, src_sb, tok0):
    arena = T["arena"]
    A_XT = T["A_XT"]
    em.wait("vector", "pe")
    s = em.sems["dv"]
    for ec in range(NC):
        em.inc("dv")
        em.op("vector", lambda vv, ec=ec, arena=arena, src=src_sb, tok0=tok0, A_XT=A_XT, s=s:
              vv.tensor_copy(arena[:, A_XT + ec * 512: A_XT + ec * 512 + 512],
                             src[:, ec * TOK + tok0: ec * TOK + tok0 + 512]).then_inc(s, 1))
    return em.val("dv")


def xb_slice(T, ec):
    arena = T["arena"]
    A_XT = T["A_XT"]
    return arena[:, A_XT + ec * 512: A_XT + ec * 512 + 512]


def emit_mm_group(em, T, lhs_list, rhs_list, outv, n):
    pes = em.sems["pe"]
    nl = len(lhs_list)
    for i, (lw, rh) in enumerate(zip(lhs_list, rhs_list)):
        if i < nl - 1:
            em.op("tensor", lambda t, lw=lw, rh=rh, i=i, outv=outv, n=n:
                  t.matmul(outv[0:n, :], lw, rh, start=(i == 0), stop=False))
    v = em.inc("pe")
    lw, rh = lhs_list[-1], rhs_list[-1]
    em.op("tensor", lambda t, lw=lw, rh=rh, outv=outv, n=n, s=pes, first=(nl == 1):
          t.matmul(outv[0:n, :], lw, rh, start=first, stop=True).then_inc(s, 1))
    return v


def emit_proj_group(em, T, w_sb, colbase, ncols, outv):
    lhs = [w_sb[:, colbase(ec):colbase(ec) + ncols] for ec in range(NC)]
    rhs = [xb_slice(T, ec) for ec in range(NC)]
    return emit_mm_group(em, T, lhs, rhs, outv, ncols)


def emit_ln(em, T, src_sb, gamma_beta=None, out_cols=None, tbs=None):
    arena = T["arena"]
    ps01, ps23 = T["ps01"], T["ps23"]
    ones_col = T["ones_col"]
    st_mu, st_ms, st_rs = T["st_mu"], T["st_ms"], T["st_rs"]
    st_var = T["st_var"][0:1, 0:512]
    st_B = T["st_B"][0:1, 512:1024]
    srow, ev0, ev1 = T["srow"], T["ev0"], T["ev1"]
    dvs, acs, pes = em.sems["dv"], em.sems["ac"], em.sems["pe"]
    em.op("vector", lambda v: v.drain())
    for tb in range(4):
        tok0 = tb * 512
        emit_xb_tiles(em, T, src_sb, tok0)
        for ec in range(NC):
            sq = (ev0, ev1)[ec % 2]
            em.wait("scalar", "dv")
            em.wait("scalar", "pe")
            va = em.inc("ac")
            em.op("scalar", lambda s, sq=sq, ec=ec, T=T, ss=acs:
                  s.activation(out=sq[:], in_=xb_slice(T, ec), func=AF.Square).then_inc(ss, 1))
            em.wait("tensor", "ac", va)
            em.inc("pe")
            em.op("tensor", lambda t, sq=sq, ec=ec, T=T, oc=ones_col, s=pes:
                  (t.matmul(ps01[0:1, 0:512], oc[:], xb_slice(T, ec),
                            start=(ec == 0), stop=(ec == NC - 1)),
                   t.matmul(ps01[0:1, 512:1024], oc[:], sq[:],
                            start=(ec == 0), stop=(ec == NC - 1)).then_inc(s, 1)))
        em.wait("vector", "pe")
        em.op("vector", lambda v: v.tensor_scalar_mul(st_mu[:], ps01[0:1, 0:512], 1.0 / D))
        em.op("vector", lambda v: v.tensor_scalar_mul(st_ms[:], ps01[0:1, 512:1024], 1.0 / D))
        em.op("vector", lambda v: v.drain())
        em.op("vector", lambda v: v.tensor_tensor(st_var, st_mu[:], st_mu[:], ALU.mult))
        em.op("vector", lambda v: v.drain())
        v1 = em.inc("dv")
        em.op("vector", lambda v, s=dvs:
              v.tensor_tensor(st_var, st_ms[:], st_var, ALU.subtract).then_inc(s, 1))
        em.wait("scalar", "dv", v1)
        em.op("scalar", lambda s: s.activation(out=srow[0:1, 0:512], in_=st_var[:],
                                               func=AF.Ln, bias=f32(EPS), scale=f32(1.0)))
        em.op("scalar", lambda s: s.drain())
        va = em.inc("ac")
        em.op("scalar", lambda s, ss=acs:
              s.activation(out=st_rs[:], in_=srow[0:1, 0:512], func=AF.Exp,
                           scale=f32(-0.5)).then_inc(ss, 1))
        em.wait("vector", "ac", va)
        v2 = em.inc("dv")
        em.op("vector", lambda v, s=dvs:
              v.tensor_tensor(st_B, st_mu[:], st_rs[:], ALU.mult).then_inc(s, 1))
        em.wait("tensor", "dv", v2)
        if gamma_beta is None:
            orow32 = T["ones_row32"]
            em.inc("pe")
            em.op("tensor", lambda t, orow=orow32, s=pes:
                  (t.matmul(ps23[:, 0:512], orow[:], st_rs[:], start=True, stop=True),
                   t.matmul(ps23[:, 512:1024], orow[:], st_B, start=True, stop=True).then_inc(s, 1)))
            em.wait("vector", "pe")
            for ec in range(NC):
                xs = src_sb[:, ec * TOK + tok0: ec * TOK + tok0 + 512]
                em.op("vector", lambda v, xs=xs: v.tensor_tensor(xs, xs, ps23[:, 0:512], ALU.mult))
                em.op("vector", lambda v: v.drain())
                em.inc("dv")
                em.op("vector", lambda v, xs=xs, s=dvs:
                      v.tensor_tensor(xs, xs, ps23[:, 512:1024], ALU.subtract).then_inc(s, 1))
                em.op("vector", lambda v: v.drain())
        else:
            g_c, b_c = gamma_beta
            orow32 = T["ones_row32"]
            em.inc("pe")
            em.op("tensor", lambda t, orow=orow32, s=pes:
                  (t.matmul(ps23[:, 0:512], orow[:], st_rs[:], start=True, stop=True),
                   t.matmul(ps23[:, 512:1024], orow[:], st_B, start=True, stop=True).then_inc(s, 1)))
            em.wait("vector", "pe")
            for ec in range(NC):
                xs = src_sb[:, ec * TOK + tok0: ec * TOK + tok0 + 512]
                tf = (T["tf0"], T["tf1"])[ec % 2]
                em.op("vector", lambda v, xs=xs, tf=tf:
                      v.tensor_tensor(tf[:], xs, ps23[:, 0:512], ALU.mult))
                em.op("vector", lambda v: v.drain())
                em.op("vector", lambda v, tf=tf:
                      v.tensor_tensor(tf[:], tf[:], ps23[:, 512:1024], ALU.subtract))
                em.op("vector", lambda v: v.drain())
                em.inc("dv")
                dst = arena[:, out_cols + ec * TOK + tok0: out_cols + ec * TOK + tok0 + 512]
                em.op("vector", lambda v, tf=tf, dst=dst, ec=ec, s=dvs, g_c=g_c, b_c=b_c:
                      v.tensor_scalar(out=dst, in0=tf[:], scalar1=g_c[:, ec:ec + 1],
                                      scalar2=b_c[:, ec:ec + 1], op0=ALU.mult,
                                      op1=ALU.add).then_inc(s, 1))
                em.op("vector", lambda v: v.drain())
    return em.val("dv")


def emit_store_x(em, T, x_out):
    x_sb = T["x_sb"]
    em.wait("sync", "dv")
    em.wait("sync", "ac")
    em.dma("sync", x_out.ap().rearrange("(c p) t -> p c t", p=128),
           x_sb[:].rearrange("p (c t) -> p c t", c=NC), "d_out")
    em.wait("sync", "d_out")



def build_attn(layer: int, part: str):
    nc = bass.Bass("TRN2", target_bir_lowering=False)
    din = {}

    def dram_in(name, shape, dt=F32):
        din[name] = nc.dram_tensor(name, list(shape), dt, kind="ExternalInput")
        return din[name]

    dram_in("x_in", [D, TOK])
    dram_in("c_ones_col", [128, 1])
    dram_in("c_ones_row", [1, 128])
    dram_in("c_ident", [128, 128])
    dram_in("c_mask01", [128, 2])
    dram_in("c_sel2", [2, 128])
    dram_in("c_ones_row32", [1, 128])
    if part == "kv":
        dram_in("w_k", [128, 8192])
        dram_in("w_v", [128, 8192])
        if layer == 1:
            dram_in("ks_in", [128, 8])
        kt_loc = nc.dram_tensor("kt_out", [D, TOK], BF16, kind="ExternalOutput")
        v_loc = nc.dram_tensor("v_out", [TOK, D], BF16, kind="ExternalOutput")
        qt_d = nc.dram_tensor("qt_out", [64, 32768], BF16, kind="ExternalOutput")
        kt_full = v_full = None
        dram_in("w_q", [128, 8192])
        if layer == 1:
            dram_in("qs_in", [64, 16])
    else:
        dram_in("w_o2", [64, 16384])
        if layer == 1:
            dram_in("bo_in", [128, 8])
        kt_full = dram_in("kt_full", [4 * D, TOK], BF16)
        v_full = dram_in("v_full", [SEQ, D], BF16)
        qt_in = dram_in("qt_in", [64, 32768], BF16)
        x_out = nc.dram_tensor("x_out", [D, TOK], F32, kind="ExternalOutput")
        rl_dram = nc.dram_tensor("rl_dram", [1, 1024], F32)

    es = contextlib.ExitStack()
    sems = {s: es.enter_context(nc.semaphore(s)) for s in SEM_NAMES}
    em = Emit(sems)
    T = {}
    sb, psa = _common_setup(nc, T, es)

    x_sb = sb("x_sb", [128, NC * TOK], F32)
    arena = sb("arena", [128, 18496], BF16)
    qt_sb = sb("qt_sb", [128, 16384], BF16)
    zt_sb = sb("zt_sb", [128, 16384], BF16)
    w_sb = sb("w_sb", [128, 8192], BF16)
    ev0 = sb("ev0", [128, 512], BF16)
    ev1 = sb("ev1", [128, 512], BF16)
    sq_t = sb("sq_t", [128, 512], BF16)
    tf0 = sb("tf0", [128, 512], F32)
    tf1 = sb("tf1", [128, 512], F32)
    rl_row = sb("rl_row", [1, 1024], F32)
    if part == "core":
        bcP = sb("bcP", [64, 1024], F32)
        T["bcP"] = bcP
    srow = sb("srow", [2 if part == "kv" else 1, 1024], F32)
    bcA = sb("bcA", [64, 512], BF16) if part == "kv" else None
    bc64 = None
    st_mu = sb("st_mu", [1, 512], F32)
    st_ms = sb("st_ms", [1, 512], F32)
    st_var = rl_row[0:1, 0:512].tensor if False else None
    st_rs = sb("st_rs", [1, 512], F32)
    ones_col = sb("ones_col", [128, 1], BF16)
    ones_row = sb("ones_row", [1, 128], BF16)
    ones_row32 = sb("ones_row32", [1, 128], F32)
    ident = mask01 = sel2 = None
    if part == "kv":
        ident = sb("ident", [128, 128], BF16)
        mask01 = sb("mask01", [128, 2], BF16)
        sel2 = sb("sel2", [2, 128], F32)
    qs_sb = ks_sb = bo_sb = None
    if layer == 1 and part == "core":
        bo_sb = sb("bo_sb", [128, 8], F32)
    if layer == 1 and part == "kv":
        qs_sb = sb("qs_sb", [64, 16], F32)
        ks_sb = sb("ks_sb", [128, 8], F32)

    ps01 = psa("ps01", [128, 1024], F32)
    ps23 = psa("ps23", [128, 1024], F32)
    ps45 = psa("ps45", [128, 1024], F32)
    ps6 = psa("ps6", [128, 512], F32)
    psT = psa("psT", [128, 1024], BF16)

    A_STAGE, A_XT = 0, 8192
    A_KT, A_VT, A_V65, A_P, A_WO = 0, 8192, 10240, 14400, 0
    T.update(arena=arena, x_sb=x_sb, A_XT=A_XT, ps01=ps01, ps23=ps23,
             ones_col=ones_col, ones_row=ones_row, ones_row32=ones_row32, st_mu=st_mu, st_ms=st_ms,
             st_rs=st_rs, srow=srow, ev0=ev0, ev1=ev1,
             tf0=tf0, tf1=tf1)
    T["st_var"] = rl_row
    T["st_B"] = rl_row
    if part == "core":
        T["rl_dram"] = rl_dram
    bc64 = bcA if part == "kv" else bcP[:, 0:512]

    em.dma("gpsimd", x_sb[:].rearrange("p (c t) -> p c t", c=NC),
           din["x_in"].ap().rearrange("(c p) t -> p c t", p=128), "d_c")
    const_list = [("c_ones_col", ones_col), ("c_ones_row", ones_row),
                  ("c_ones_row32", ones_row32)]
    if part == "kv":
        const_list += [("c_ident", ident), ("c_mask01", mask01), ("c_sel2", sel2)]
    for nm, dst in const_list:
        em.dma("gpsimd", dst[:], din[nm].ap(), "d_c")
    if layer == 1 and part == "core":
        em.dma("gpsimd", bo_sb[:], din["bo_in"].ap(), "d_c")
    if layer == 1 and part == "kv":
        em.dma("gpsimd", qs_sb[:], din["qs_in"].ap(), "d_c")
        em.dma("gpsimd", ks_sb[:], din["ks_in"].ap(), "d_c")
    em.wait("vector", "d_c")
    em.wait("tensor", "d_c")

    def load_w(name):
        em.wait("gpsimd", "pe")
        return em.dma("gpsimd", w_sb[:].rearrange("p (c f) -> p c f", c=NC),
                      din[name].ap().rearrange("p (c f) -> p c f", c=NC), "d_w")

    def pslot(i):
        t = (ps01, ps23)[(i // 2) % 2]
        return t[:, (i % 2) * 512:(i % 2) * 512 + 512]

    dvs, acs, pes = sems["dv"], sems["ac"], sems["pe"]

    def norm_stage_slice(dst, kind, idx):
        nrows = 2 if kind == "k" else 1
        em.wait("vector", "dv")
        em.op("vector", lambda v: v.drain())
        v1 = em.inc("dv")
        em.op("vector", lambda v, dst=dst, s=dvs:
              v.tensor_tensor(sq_t[0:dst.shape[0], :], dst, dst, ALU.mult).then_inc(s, 1))
        em.wait("tensor", "dv", v1)
        em.inc("pe")
        if kind == "k":
            em.op("tensor", lambda t, s=pes:
                  t.matmul(ps6[0:2, :], mask01[:], sq_t[:], start=True, stop=True).then_inc(s, 1))
        else:
            em.op("tensor", lambda t, s=pes:
                  t.matmul(ps6[0:1, :], ones_col[0:64, :], sq_t[0:64, :],
                           start=True, stop=True).then_inc(s, 1))
        em.wait("scalar", "pe")
        em.op("scalar", lambda s, nr=nrows:
              s.activation(out=srow[0:nr, 0:512], in_=ps6[0:nr, :], func=AF.Ln, scale=f32(1.0)))
        em.op("scalar", lambda s: s.drain())
        va = em.inc("ac")
        em.op("scalar", lambda s, nr=nrows, ss=acs:
              s.activation(out=srow[0:nr, 512:1024], in_=srow[0:nr, 0:512], func=AF.Exp,
                           scale=f32(-0.5)).then_inc(ss, 1))
        em.wait("vector", "ac", va)
        v2 = em.inc("dv")
        em.op("vector", lambda v, nr=nrows, s=dvs:
              v.tensor_single_scalar(srow[0:nr, 0:512], srow[0:nr, 512:1024],
                                     1e12, ALU.min).then_inc(s, 1))
        em.wait("tensor", "dv", v2)
        em.inc("pe")
        if kind == "k":
            em.op("tensor", lambda t, s=pes:
                  t.matmul(ps6[:, :], sel2[:], srow[0:2, 0:512], start=True, stop=True).then_inc(s, 1))
        else:
            em.op("tensor", lambda t, s=pes:
                  t.matmul(ps6[0:64, :], ones_row32[0:1, 0:64], srow[0:1, 0:512],
                           start=True, stop=True).then_inc(s, 1))
        em.wait("vector", "pe")
        nb = dst.shape[0]
        bct = bc64 if nb == 64 else sq_t[0:nb, 0:512]
        em.op("vector", lambda v, nb=nb, bct=bct: v.tensor_copy(bct, ps6[0:nb, :]))
        em.op("vector", lambda v: v.drain())
        em.op("vector", lambda v, dst=dst, nb=nb, bct=bct:
              v.tensor_tensor(dst, dst, bct, ALU.mult))
        em.op("vector", lambda v: v.drain())
        sc = (ks_sb if kind == "k" else qs_sb)
        v4 = em.inc("dv")
        em.op("vector", lambda v, dst=dst, sc=sc, idx=idx, s=dvs:
              v.tensor_scalar_mul(dst, dst, sc[0:dst.shape[0], idx:idx + 1]).then_inc(s, 1))
        em.op("vector", lambda v: v.drain())
        return v4

    if part == "kv":
      load_w("w_k")
      em.wait("tensor", "d_w")
      for half in range(2):
        em.wait("vector", "d_sk")
        for tb in range(2):
            tok0 = half * HALF + tb * 512
            emit_xb_tiles(em, T, x_sb, tok0)
            em.wait("tensor", "dv")
            for fc in range(NC):
                outv = pslot(em.inc("psl") % 4)
                em.wait("tensor", "dv")
                emit_proj_group(em, T, w_sb, lambda ec, fc=fc: ec * 1024 + fc * 128, 128, outv)
                em.wait("vector", "pe")
                dst = arena[:, A_STAGE + fc * 1024 + tb * 512: A_STAGE + fc * 1024 + tb * 512 + 512]
                em.inc("dv")
                em.op("vector", lambda vv, outv=outv, dst=dst, s=dvs:
                      vv.tensor_copy(dst, outv).then_inc(s, 1))
        if layer == 1:
            scr = qt_sb[0:128, 0:1024]
            em.wait("vector", "pe")
            em.op("vector", lambda v: v.drain())
            km_mark = {}
            for fc in range(NC):
                stg = arena[:, A_STAGE + fc * 1024: A_STAGE + fc * 1024 + 1024]
                if fc >= 1:
                    em.wait("vector", "pe", km_mark[fc - 1])
                v1 = em.inc("dv")
                em.op("vector", lambda v, stg=stg, scr=scr, s=dvs:
                      v.tensor_tensor(scr, stg, stg, ALU.mult).then_inc(s, 1))
                em.wait("tensor", "dv", v1)
                m1 = em.inc("pe")
                em.op("tensor", lambda t, scr=scr, s=sems["pe"], mk=mask01:
                      (t.matmul(ps01[0:2, 0:512], mk[:], scr[:, 0:512],
                                start=True, stop=True),
                       t.matmul(ps01[0:2, 512:1024], mk[:], scr[:, 512:1024],
                                start=True, stop=True).then_inc(s, 1)))
                em.wait("scalar", "pe", m1)
                em.op("scalar", lambda sc: sc.activation(out=srow[0:2, 0:1024],
                                                         in_=ps01[0:2, 0:1024],
                                                         func=AF.Ln, scale=f32(1.0)))
                em.op("scalar", lambda sc: sc.drain())
                va = em.inc("ac")
                em.op("scalar", lambda sc, ss=acs:
                      sc.activation(out=srow[0:2, 0:1024], in_=srow[0:2, 0:1024],
                                    func=AF.Exp, scale=f32(-0.5)).then_inc(ss, 1))
                em.wait("vector", "ac", va)
                em.op("vector", lambda v: v.drain())
                v2 = em.inc("dv")
                em.op("vector", lambda v, s=dvs:
                      v.tensor_single_scalar(srow[0:2, 0:1024], srow[0:2, 0:1024],
                                             1e12, ALU.min).then_inc(s, 1))
                em.wait("tensor", "dv", v2)
                m2 = em.inc("pe")
                em.op("tensor", lambda t, s=sems["pe"], sl2=sel2:
                      (t.matmul(ps23[:, 0:512], sl2[:], srow[0:2, 0:512],
                                start=True, stop=True),
                       t.matmul(ps23[:, 512:1024], sl2[:], srow[0:2, 512:1024],
                                start=True, stop=True).then_inc(s, 1)))
                km_mark[fc] = m2
                em.wait("vector", "pe", m2)
                em.op("vector", lambda v, stg=stg:
                      v.tensor_tensor(stg, stg, ps23[:, 0:1024], ALU.mult))
                em.op("vector", lambda v: v.drain())
                em.inc("dv")
                em.op("vector", lambda v, stg=stg, fc=fc, s=dvs, ks=ks_sb:
                      v.tensor_scalar_mul(stg, stg, ks[:, fc:fc + 1]).then_inc(s, 1))
                em.op("vector", lambda v: v.drain())
        em.wait("sync", "dv")
        em.dma("sync",
               kt_loc.ap().rearrange("(c p) t -> p c t", p=128)[:, :, half * HALF:(half + 1) * HALF],
               arena[:, A_STAGE:A_STAGE + 8192].rearrange("p (c t) -> p c t", c=8), "d_sk")

    if part == "kv":
      load_w("w_v")
      em.wait("tensor", "d_w")
      for half in range(2):
        em.wait("vector", "d_sk")
        em.wait("vector", "d_sv")
        for tb in range(2):
            tok0 = half * HALF + tb * 512
            emit_xb_tiles(em, T, x_sb, tok0)
            em.wait("tensor", "dv")
            for fc in range(NC):
                outv = pslot(em.inc("psl") % 4)
                em.wait("tensor", "dv")
                emit_proj_group(em, T, w_sb, lambda ec, fc=fc: ec * 1024 + fc * 128, 128, outv)
                em.wait("vector", "pe")
                raw = (ev0, ev1)[em.inc("evs") % 2]
                v0 = em.inc("dv")
                em.op("vector", lambda vv, outv=outv, raw=raw, s=dvs:
                      vv.tensor_copy(raw[:], outv).then_inc(s, 1))
                em.wait("tensor", "dv", v0)
                em.wait("tensor", "dv")
                em.inc("pe")
                for tt in range(4):
                    em.op("tensor", lambda t, raw=raw, tt=tt, s=pes:
                          (t.transpose(psT[:, tt * 128: tt * 128 + 128],
                                       raw[:, tt * 128:tt * 128 + 128], ident[:]).then_inc(s, 1)
                           if tt == 3 else
                           t.transpose(psT[:, tt * 128: tt * 128 + 128],
                                       raw[:, tt * 128:tt * 128 + 128], ident[:])))
                em.wait("vector", "pe")
                for tt in range(4):
                    tc = tb * 4 + tt
                    dst = arena[:, A_STAGE + tc * 1024 + fc * 128: A_STAGE + tc * 1024 + fc * 128 + 128]
                    em.inc("dv")
                    em.op("vector", lambda vv, dst=dst, tt=tt, s=dvs:
                          vv.tensor_copy(dst, psT[:, tt * 128:tt * 128 + 128]).then_inc(s, 1))
        em.wait("sync", "dv")
        em.dma("sync",
               v_loc.ap().rearrange("(c p) f -> p c f", p=128)[:, half * 8:half * 8 + 8, :],
               arena[:, A_STAGE:A_STAGE + 8192].rearrange("p (c f) -> p c f", c=8), "d_sv")
      load_w("w_q")
      em.wait("tensor", "d_w")
      T["arena"], T["A_XT"] = qt_sb, 0
      for half in range(2):
        em.wait("vector", "d_sv")
        em.wait("vector", "d_sk")
        for tb in range(2):
            tok0 = half * HALF + tb * 512
            emit_xb_tiles(em, T, x_sb, tok0)
            em.wait("tensor", "dv")
            for hc in range(16):
                sl4 = em.inc("psl") % 4
                outv = pslot(sl4)
                em.wait("tensor", "dv", em.ctr.get("pm%d" % sl4, 0))
                gm = emit_proj_group(em, T, w_sb, lambda ec, hc=hc: ec * 1024 + hc * 64, 64, outv)
                em.wait("vector", "pe", gm)
                ab, pair = hc % 2, hc // 2
                dst = arena[0:64, A_STAGE + ab * 8192 + pair * 1024 + tb * 512:
                            A_STAGE + ab * 8192 + pair * 1024 + tb * 512 + 512]
                em.inc("dv")
                em.op("vector", lambda vv, outv=outv, dst=dst, s=dvs:
                      vv.tensor_copy(dst, outv[0:64, :]).then_inc(s, 1))
                em.ctr["pm%d" % sl4] = em.val("dv")
        if layer == 1:
            scr = qt_sb[0:64, 8192:9216]
            em.wait("vector", "pe")
            em.op("vector", lambda v: v.drain())
            nm_mark = {}
            for idx in range(16):
                ab, pair = idx % 2, idx // 2
                hc = pair * 2 + ab
                qsl = arena[0:64, A_STAGE + ab * 8192 + pair * 1024:
                            A_STAGE + ab * 8192 + pair * 1024 + 1024]
                if idx >= 1:
                    em.wait("vector", "pe", nm_mark[idx - 1])
                v1 = em.inc("dv")
                em.op("vector", lambda v, qsl=qsl, scr=scr, s=dvs:
                      v.tensor_tensor(scr, qsl, qsl, ALU.mult).then_inc(s, 1))
                em.wait("tensor", "dv", v1)
                m1 = em.inc("pe")
                em.op("tensor", lambda t, scr=scr, s=sems["pe"], oc=ones_col:
                      (t.matmul(ps01[0:1, 0:512], oc[0:64, :], scr[:, 0:512],
                                start=True, stop=True),
                       t.matmul(ps01[0:1, 512:1024], oc[0:64, :], scr[:, 512:1024],
                                start=True, stop=True).then_inc(s, 1)))
                em.wait("scalar", "pe", m1)
                em.op("scalar", lambda sc: sc.activation(out=srow[0:1, 0:1024],
                                                         in_=ps01[0:1, 0:1024],
                                                         func=AF.Ln, scale=f32(1.0)))
                em.op("scalar", lambda sc: sc.drain())
                va = em.inc("ac")
                em.op("scalar", lambda sc, ss=acs:
                      sc.activation(out=rl_row[0:1, 0:1024], in_=srow[0:1, 0:1024],
                                    func=AF.Exp, scale=f32(-0.5)).then_inc(ss, 1))
                em.wait("vector", "ac", va)
                em.op("vector", lambda v: v.drain())
                v2 = em.inc("dv")
                em.op("vector", lambda v, s=dvs:
                      v.tensor_single_scalar(rl_row[0:1, 0:1024], rl_row[0:1, 0:1024],
                                             1e12, ALU.min).then_inc(s, 1))
                em.wait("tensor", "dv", v2)
                m2 = em.inc("pe")
                em.op("tensor", lambda t, s=sems["pe"], orow=ones_row32:
                      (t.matmul(ps23[0:64, 0:512], orow[0:1, 0:64], rl_row[0:1, 0:512],
                                start=True, stop=True),
                       t.matmul(ps23[0:64, 512:1024], orow[0:1, 0:64], rl_row[0:1, 512:1024],
                                start=True, stop=True).then_inc(s, 1)))
                nm_mark[idx] = m2
                em.wait("vector", "pe", m2)
                em.op("vector", lambda v, qsl=qsl:
                      v.tensor_tensor(qsl, qsl, ps23[0:64, 0:1024], ALU.mult))
                em.op("vector", lambda v: v.drain())
                em.inc("dv")
                em.op("vector", lambda v, qsl=qsl, hc=hc, s=dvs, qs=qs_sb:
                      v.tensor_scalar_mul(qsl, qsl, qs[:, hc:hc + 1]).then_inc(s, 1))
                em.op("vector", lambda v: v.drain())
        em.wait("sync", "dv")
        em.dma("sync", qt_d.ap()[:, half * 16384:(half + 1) * 16384],
               arena[0:64, A_STAGE:A_STAGE + 16384], "d_sk")
      T["arena"], T["A_XT"] = arena, A_XT
      em.wait("sync", "d_sk")
      em.wait("sync", "d_sv")
      with nc.Block() as block:
          em.play(block)
      es.close()
      return nc

    CC_ALL = 0

    em.op("vector", lambda v: v.memset(qt_sb[64:128, :], 0.0))
    em.op("vector", lambda v: v.memset(zt_sb[64:128, :], 0.0))
    em.op("vector", lambda v: v.memset(arena[64:128, 0:8192], 0.0))
    em.op("vector", lambda v: v.memset(arena[:, 14400:18496], 0.0))
    em.op("vector", lambda v: v.drain())
    vpad = em.inc("dv")
    em.op("vector", lambda v, s=dvs: v.nop().then_inc(s, 1))
    em.wait("tensor", "dv", vpad)
    em.wait("sync", "dv", vpad)

    def kt_off(par, ab):
        return A_KT + (ab * 2 + par) * 2048

    kv_ld = {}
    for half in range(2):
        em.wait("tensor", "ac_e", em.val("ac_e"))
        if half == 1:
            em.wait("gpsimd", "pe_s", 1024)
        em.wait("gpsimd", "dv", vpad)
        vq = em.dma("gpsimd", qt_sb[0:64, :], din["qt_in"].ap()
                    [:, half * 16384:(half + 1) * 16384], "d_w")
        em.wait("tensor", "d_w", vq)
        QP_pe = em.val("pe")
        QP_dv = em.val("dv")

        em.wait("vector", "pe")
        em.op("vector", lambda v:
              v.memset(arena[:, A_V65 + 0 * 2080:A_V65 + 1 * 2080]
                       .rearrange("p (c h u) -> p c h u", h=2, u=65)[:, :, :, 64:65], 1.0))
        em.op("vector", lambda v:
              v.memset(arena[:, A_V65 + 1 * 2080:A_V65 + 2 * 2080]
                       .rearrange("p (c h u) -> p c h u", h=2, u=65)[:, :, :, 64:65], 1.0))
        em.op("vector", lambda v: v.drain())
        vones = em.inc("dv")
        em.op("vector", lambda v, s=dvs: v.nop().then_inc(s, 1))

        for tidx in range(16):
            pair, qb = tidx // 2, tidx % 2
            j = half * 16 + tidx
            if tidx > 0:
                _emit_trio_evac(em, T, sems, half, tidx - 1, pair_prev=(tidx - 1) // 2,
                                qb_prev=(tidx - 1) % 2, zt_sb=zt_sb, ps45=ps45, ps6=ps6,
                                rl_row=rl_row, bcA=bcA)
            for g in range(4):
                Gg = j * 4 + g
                par = Gg % 2
                kvs = "d_kv%d" % par
                if Gg >= 2:
                    em.wait("sync", "pe_s", 16 * (Gg - 1))
                if Gg >= 1:
                    em.wait("sync", "dv_v65", Gg)
                if tidx == 0:
                    em.wait("sync", "pe", QP_pe)
                    em.wait("sync", "dv", QP_dv)

                em.dma("sync", arena[0:64, kt_off(par, 0):kt_off(par, 0) + 2048],
                       kt_full.ap()[g * 1024 + pair * 128: g * 1024 + pair * 128 + 64, :], kvs)
                em.dma("sync", arena[0:64, kt_off(par, 1):kt_off(par, 1) + 2048],
                       kt_full.ap()[g * 1024 + pair * 128 + 64: g * 1024 + pair * 128 + 128, :], kvs)
                em.dma("sync", arena[:, A_VT:A_VT + 2048]
                       .rearrange("p (c f) -> p c f", c=16),
                       v_full.ap().rearrange("(c p) f -> p c f", p=128)
                       [:, g * 16:(g + 1) * 16, pair * 128:(pair + 1) * 128], kvs)
                kv_ld[Gg] = (kvs, em.val(kvs))
                em.wait("vector", kv_ld[Gg][0], kv_ld[Gg][1])
                if Gg >= 2:
                    em.wait("vector", "pe_pv", 16 * (Gg - 1))
                if tidx == 0:
                    em.wait("vector", "dv", vones)
                vsrc = arena[:, A_VT:A_VT + 2048] \
                    .rearrange("p (c h dd) -> p c h dd", c=16, h=2, dd=64)
                vdst = arena[:, A_V65 + par * 2080:A_V65 + (par + 1) * 2080] \
                    .rearrange("p (c h u) -> p c h u", c=16, h=2, u=65)
                em.inc("dv_v65")
                em.op("vector", lambda v, vsrc=vsrc, vdst=vdst, s=sems["dv_v65"]:
                      v.tensor_copy(vdst[:, :, :, 0:64], vsrc).then_inc(s, 1))
            for ch in range(64):
                I = half * 1024 + tidx * 64 + ch
                Gg = I // 16
                par = Gg % 2
                psS = (ps01, ps23)[I % 2]
                c16 = ch % 16
                qA = qt_sb[0:128, pair * 1024 + qb * 512: pair * 1024 + qb * 512 + 512]
                qB = qt_sb[0:128, 8192 + pair * 1024 + qb * 512: 8192 + pair * 1024 + qb * 512 + 512]
                if c16 == 0:
                    em.wait("tensor", kv_ld[Gg][0], kv_ld[Gg][1])
                em.wait("tensor", "ac_e", I - 1)
                ktA = arena[0:128, kt_off(par, 0) + c16 * 128: kt_off(par, 0) + c16 * 128 + 128]
                ktB = arena[0:128, kt_off(par, 1) + c16 * 128: kt_off(par, 1) + c16 * 128 + 128]
                em.op("tensor", lambda t, ktA=ktA, qA=qA, psS=psS:
                      t.matmul(psS[:, 0:512], ktA, qA, start=True, stop=True))
                em.inc("pe_s")
                em.op("tensor", lambda t, ktB=ktB, qB=qB, psS=psS, s=sems["pe_s"]:
                      t.matmul(psS[:, 512:1024], ktB, qB, start=True, stop=True).then_inc(s, 1))
                if I - 2 >= half * 1024:
                    _emit_pv(em, T, sems, half, I - 2, arena, qt_sb, zt_sb, ps45, ps6,
                             rl_row, bcA, A_P, A_V65, kt_off)
                em.wait("scalar", "pe_s", I + 1)
                em.wait("scalar", "pe_pv", I - 2)
                em.inc("ac_e")
                pAB = arena[:, A_P + (I % 3) * 1024: A_P + (I % 3) * 1024 + 1024]
                em.op("scalar", lambda s, psS=psS, pAB=pAB, ss=sems["ac_e"]:
                      s.activation(out=pAB, in_=psS[:, 0:1024], func=AF.Exp,
                                   scale=f32(0.125)).then_inc(ss, 1))
        for Itail in (half * 1024 + 1022, half * 1024 + 1023):
            em.wait("tensor", "ac_e", Itail + 1)
            _emit_pv(em, T, sems, half, Itail, arena, qt_sb, zt_sb, ps45, ps6,
                     rl_row, bcA, A_P, A_V65, kt_off)
        _emit_trio_evac(em, T, sems, half, 15, pair_prev=7, qb_prev=1, zt_sb=zt_sb,
                        ps45=ps45, ps6=ps6, rl_row=rl_row, bcA=bcA)

        em.wait("gpsimd", "pe_pv", (half + 1) * 1024)
        em.dma("gpsimd", arena[0:64, A_WO:A_WO + 16384],
               din["w_o2"].ap(), "d_w")
        em.wait("tensor", "d_w")
        em.wait("tensor", "dv_z", (half + 1) * 16)
        for dc in range(NC):
            for tb in range(2):
                tok0 = half * HALF + tb * 512
                sl = em.inc("ops") % 2
                outv = ps45[:, sl * 512: sl * 512 + 512]
                em.wait("tensor", "dv")
                lhs = [arena[0:128, A_WO + hc * 1024 + dc * 128: A_WO + hc * 1024 + dc * 128 + 128]
                       for hc in range(16)]
                rhs = [zt_sb[0:128, hc * 1024 + tb * 512: hc * 1024 + tb * 512 + 512]
                       for hc in range(16)]
                emit_mm_group(em, T, lhs, rhs, outv, 128)
                em.wait("vector", "pe")
                xs = x_sb[:, dc * TOK + tok0: dc * TOK + tok0 + 512]
                if layer == 0:
                    em.inc("dv")
                    em.op("vector", lambda v, xs=xs, outv=outv, s=dvs:
                          v.scalar_tensor_tensor(out=xs, in0=xs, scalar=2.0, in1=outv,
                                                 op0=ALU.mult, op1=ALU.add).then_inc(s, 1))
                    em.op("vector", lambda v: v.drain())
                else:
                    em.wait("scalar", "pe")
                    tfx = (tf0, tf1)[em.inc("tfs") % 2]
                    va = em.inc("ac")
                    em.op("scalar", lambda s, outv=outv, tfx=tfx, dc=dc, ss=acs, bo=bo_sb:
                          s.activation(out=tfx[:], in_=outv, func=AF.Identity,
                                       bias=bo[:, dc:dc + 1], scale=f32(1.0)).then_inc(ss, 1))
                    em.wait("vector", "ac", va)
                    em.inc("dv")
                    em.op("vector", lambda v, xs=xs, tfx=tfx, s=dvs:
                          v.tensor_tensor(xs, xs, tfx[:], ALU.add).then_inc(s, 1))
                    em.op("vector", lambda v: v.drain())

    em.wait("tensor", "ac_e", em.val("ac_e"))
    emit_ln(em, T, x_sb, None, None)
    emit_store_x(em, T, x_out)

    with nc.Block() as block:
        em.play(block)
    es.close()
    return nc


def _emit_pv(em, T, sems, half, Iprev, arena, qt_sb, zt_sb, ps45, ps6, rl_row, bcA,
             A_P, A_V65, kt_off):
    i_loc = Iprev - half * 1024
    tidx = i_loc // 64
    ch = i_loc % 64
    pair, qb = tidx // 2, tidx % 2
    Gg = Iprev // 16
    par = Gg % 2
    c16 = ch % 16
    j = half * 16 + tidx
    if ch == 0:
        em.wait("tensor", "dv_z", j)
    if c16 == 0:
        em.wait("tensor", "dv_v65", Gg + 1)
    vA = arena[:, A_V65 + par * 2080 + c16 * 130: A_V65 + par * 2080 + c16 * 130 + 65]
    vB = arena[:, A_V65 + par * 2080 + c16 * 130 + 65: A_V65 + par * 2080 + c16 * 130 + 130]
    pA = arena[:, A_P + (Iprev % 3) * 1024: A_P + (Iprev % 3) * 1024 + 512]
    pB = arena[:, A_P + (Iprev % 3) * 1024 + 512: A_P + (Iprev % 3) * 1024 + 1024]
    em.op("tensor", lambda t, vA=vA, pA=pA, st=(ch == 0), sp=(ch == 63):
          t.matmul(ps45[0:65, 0:512], vA, pA, start=st, stop=sp))
    em.inc("pe_pv")
    em.op("tensor", lambda t, vB=vB, pB=pB, st=(ch == 0), sp=(ch == 63), s=sems["pe_pv"]:
          t.matmul(ps45[0:65, 512:1024], vB, pB, start=st, stop=sp).then_inc(s, 1))



def _emit_trio_evac(em, T, sems, half, tidx, pair_prev, qb_prev, zt_sb, ps45, ps6,
                    rl_row, bcA):
    j = half * 16 + tidx
    dvs = em.sems["dv"]
    bcP = T["bcP"]
    srow = T["srow"]
    em.wait("scalar", "pe_pv", 64 * (j + 1))
    em.op("scalar", lambda s: s.activation(out=srow[0:1, 0:1024], in_=ps45[64:65, 0:1024],
                                           func=AF.Ln, scale=f32(1.0)))
    em.op("scalar", lambda s: s.drain())
    em.inc("dv_z1")
    em.op("scalar", lambda s, ss=sems["dv_z1"]:
          s.activation(out=rl_row[0:1, 0:1024], in_=srow[0:1, 0:1024], func=AF.Exp,
                       scale=f32(-1.0)).then_inc(ss, 1))
    em.wait("gpsimd", "dv_z1", j + 1)
    em.wait("gpsimd", "dv_z", j)
    em.dma("gpsimd", T["rl_dram"].ap(), rl_row[0:1, 0:1024], "d_w1")
    em.wait("gpsimd", "d_w1")
    em.dma("gpsimd", bcP[:, 0:1024], T["rl_dram"].ap().partition_broadcast(64), "d_w0")
    em.wait("vector", "d_w0")
    dstA = zt_sb[0:64, (2 * pair_prev) * 1024 + qb_prev * 512:
                 (2 * pair_prev) * 1024 + qb_prev * 512 + 512]
    em.inc("dv_z2")
    em.op("vector", lambda v, dstA=dstA, s=sems["dv_z2"]:
          v.tensor_tensor(dstA, ps45[0:64, 0:512], bcP[:, 0:512], ALU.mult).then_inc(s, 1))
    dstB = zt_sb[0:64, (2 * pair_prev + 1) * 1024 + qb_prev * 512:
                 (2 * pair_prev + 1) * 1024 + qb_prev * 512 + 512]
    em.inc("dv_z")
    em.op("vector", lambda v, dstB=dstB, s=sems["dv_z"]:
          v.tensor_tensor(dstB, ps45[0:64, 512:1024], bcP[:, 512:1024], ALU.mult).then_inc(s, 1))





def build_ffn(layer: int):
    nc = bass.Bass("TRN2", target_bir_lowering=False)
    din = {}

    def dram_in(name, shape, dt=F32):
        din[name] = nc.dram_tensor(name, list(shape), dt, kind="ExternalInput")
        return din[name]

    dram_in("x_in", [D, TOK])
    dram_in("c_ones_col", [128, 1])
    dram_in("c_ones_row32", [1, 128])
    dram_in("c_minus_one", [1, 512])
    dram_in("c_ident", [128, 128])
    if layer == 0:
        dram_in("q1_cm", [128, 8 * 4096], I32)
        dram_in("q2_cm", [128, 32 * 1024], I32)
        dram_in("s1_flat", [1, 32768])
        dram_in("s2_flat", [1, 32768])
        dram_in("b1_c", [128, 32])
        dram_in("b2_c", [128, 8])
        dram_in("g_cols", [128, 8])
        dram_in("b_cols", [128, 8])
    else:
        dram_in("wexp_cm", [8, 128, 8192])
        dram_in("b_exp", [8, 1024])
        dram_in("lg_in", [1, 8])
        dram_in("uu_in", [1, 8])
    x_out = nc.dram_tensor("x_out", [D, TOK], F32, kind="ExternalOutput")
    if layer == 0:
        w1bf_dram = nc.dram_tensor("w1bf_dram", [128, 8 * 4096], BF16)
        w2bf_dram = nc.dram_tensor("w2bf_dram", [128, 32 * 1024], BF16)

    es = contextlib.ExitStack()
    sems = {ss: es.enter_context(nc.semaphore(ss)) for ss in SEM_NAMES}
    em = Emit(sems)
    T = {}
    sb, psa = _common_setup(nc, T, es)

    x_sb = sb("x_sb", [128, NC * TOK], F32)
    if layer == 0:
        arena = sb("arena", [128, 36864], BF16)
        A_H, A_Y, A_XT = 0, 16384, 32768
        i32a = sb("i32a", [128, 2048], I32)
        i32b = sb("i32b", [128, 2048], I32)
        spa = sb("spa", [128, 2048], F32)
        spb = sb("spb", [128, 2048], F32)
        w1bf0 = sb("w1bf0", [128, 2048], BF16)
        w1bf1 = sb("w1bf1", [128, 2048], BF16)
        sw0 = sb("sw0", [128, 2048], BF16)
        sw1 = sb("sw1", [128, 2048], BF16)
        g_c = sb("g_c", [128, 8], F32)
        b_c = sb("b_c", [128, 8], F32)
        b1c = sb("b1c", [128, 32], F32)
        b2c = sb("b2c", [128, 8], F32)
        hb2 = sb("hb2", [128, 8], F32)
    else:
        arena = sb("arena", [128, 12288], BF16)
        A_XT = 8192
        weff = sb("weff", [128, 8192], BF16)
        wt0 = sb("wt0", [128, 1024], F32)
        wt1 = sb("wt1", [128, 1024], F32)
        acc0 = sb("acc0", [128, 1024], F32)
        acc1 = sb("acc1", [128, 1024], F32)
        b_sb = sb("b_sb", [8, 1024], BF16)
        lg = sb("lg", [1, 8], F32)
        uu = sb("uu", [1, 8], F32)
        oh_r = sb("oh_r", [1, 8], F32)
        oh_b = sb("oh_b", [128, 128], BF16)
        ohT = sb("ohT", [8, 1], BF16)
        ohb_sb = sb("ohb_sb", [128, 8], F32)
        beff_sb = sb("beff_sb", [128, 8], F32)
    mo = sb("minus_one", [1, 512], F32)
    ev0 = sb("ev0", [128, 512], BF16)
    ev1 = sb("ev1", [128, 512], BF16)
    tf0 = sb("tf0", [128, 512], F32)
    tf1 = sb("tf1", [128, 512], F32)
    srow = sb("srow", [2, 1024], F32)
    st_mu = sb("st_mu", [1, 512], F32)
    st_ms = sb("st_ms", [1, 512], F32)
    st_vb = sb("st_vb", [1, 1024], F32)
    st_rs = sb("st_rs", [1, 512], F32)
    ones_col = sb("ones_col", [128, 1], BF16)
    ones_row32 = sb("ones_row32", [1, 128], F32)
    ident = sb("ident", [128, 128], BF16)

    ps01 = psa("ps01", [128, 1024], F32)
    ps23 = psa("ps23", [128, 1024], F32)
    ps45 = psa("ps45", [128, 1024], F32)
    if layer == 1:
        psT = psa("psT", [128, 128], BF16)

    T.update(arena=arena, x_sb=x_sb, A_XT=A_XT, ps01=ps01, ps23=ps23,
             ones_col=ones_col, ones_row32=ones_row32, st_mu=st_mu, st_ms=st_ms,
             st_rs=st_rs, srow=srow, ev0=ev0, ev1=ev1,
             tf0=tf0, tf1=tf1, minus_one=mo, st_var=st_vb, st_B=st_vb)
    dvs, acs, pes = sems["dv"], sems["ac"], sems["pe"]

    em.dma("gpsimd", x_sb[:].rearrange("p (c t) -> p c t", c=NC),
           din["x_in"].ap().rearrange("(c p) t -> p c t", p=128), "d_c")
    for nm, dst in (("c_ones_col", ones_col), ("c_ones_row32", ones_row32),
                    ("c_minus_one", mo), ("c_ident", ident)):
        em.dma("gpsimd", dst[:], din[nm].ap(), "d_c")
    if layer == 0:
        for nm, dst in (("g_cols", g_c), ("b_cols", b_c), ("b1_c", b1c), ("b2_c", b2c)):
            em.dma("gpsimd", dst[:], din[nm].ap(), "d_c")
    else:
        em.dma("gpsimd", b_sb[:], din["b_exp"].ap(), "d_c")
        em.dma("gpsimd", lg[:], din["lg_in"].ap(), "d_c")
        em.dma("gpsimd", uu[:], din["uu_in"].ap(), "d_c")
    em.wait("vector", "d_c")
    em.wait("tensor", "d_c")
    em.wait("scalar", "d_c")

    if layer == 0:
        v0 = em.inc("dv")
        em.op("vector", lambda v, s=dvs: v.tensor_scalar_mul(hb2[:], b2c[:], 0.5).then_inc(s, 1))

        slot_mark = [0, 0]

        def sweep(nblk, qsrc, sflat, dst):
            for blk in range(nblk):
                sl = em.inc("wsl") % 2
                dwn, dsn = "d_w%d" % sl, "d_kv%d" % sl
                i32t, spt, wbf = ((i32a, spa, sw0), (i32b, spb, sw1))[sl]
                em.wait("gpsimd", "dv", slot_mark[sl])
                em.wait("sync", "dv", slot_mark[sl])
                vq = em.dma("gpsimd", i32t[:], qsrc(blk), dwn)
                vs = em.dma("sync", spt[:], sflat(blk), dsn)
                em.wait("vector", dwn, vq)
                em.wait("vector", dsn, vs)
                em.wait("vector", "d_sk", max(0, em.val("d_sk") - 16))
                em.inc("dv")
                em.op("vector", lambda v, i32t=i32t, spt=spt, wbf=wbf, s=dvs:
                      v.scalar_tensor_tensor(out=wbf[:], in0=i32t[:], scalar=-128.0,
                                             in1=spt[:], op0=ALU.add,
                                             op1=ALU.mult).then_inc(s, 1))
                em.wait("sync", "dv")
                em.dma("sync", dst(blk), wbf[:], "d_sk")
                slot_mark[sl] = em.val("dv")

        def sweep_part(which, b0, nblk):
            qn, sn, dst = (("q1_cm", "s1_flat", w1bf_dram) if which == 1
                           else ("q2_cm", "s2_flat", w2bf_dram))
            sweep(nblk,
                  lambda blk: din[qn].ap()[:, (b0 + blk) * 2048:(b0 + blk) * 2048 + 2048],
                  lambda blk: din[sn].ap()[0:1, (b0 + blk) * 2048:(b0 + blk) * 2048 + 2048]
                  .partition_broadcast(128),
                  lambda blk: dst.ap()[:, (b0 + blk) * 2048:(b0 + blk) * 2048 + 2048])

        sweep_part(1, 0, 16)
        SWEEP_ST1 = em.val("d_sk")
        emit_ln(em, T, x_sb, (g_c, b_c), A_H)
        sweep_part(2, 0, 16)
        SWEEP_ST = em.val("d_sk")

        wslot_mark = [0, 0]

        def wbf_load(cols_ap, gate):
            sl = em.inc("wbl") % 2
            dwn = "d_w%d" % sl
            wbf = (w1bf0, w1bf1)[sl]
            em.wait("gpsimd", "pe", wslot_mark[sl])
            if em.val("hot_gate") < gate:
                em.ctr["hot_gate"] = gate
                em.wait("gpsimd", "d_sk", gate)
            v = em.dma("gpsimd", wbf[:], cols_ap, dwn)
            return sl, wbf, v, dwn

        for tb in range(4):
            tok0 = tb * 512
            for fp in range(16):
                sl, wbf, vld, dwn = wbf_load(
                    w1bf_dram.ap().rearrange("p (c f) -> p c f", c=8)
                    [:, :, fp * 256:fp * 256 + 256], SWEEP_ST1)
                em.wait("tensor", dwn, vld)
                for fo in range(2):
                    fcol = fp * 2 + fo
                    sl4 = em.inc("psl") % 4
                    outv = (ps01, ps23)[sl4 // 2][:, (sl4 % 2) * 512:(sl4 % 2) * 512 + 512]
                    em.wait("tensor", "ac", max(0, em.val("ac") - 2))
                    for ec in range(NC):
                        kk = ec * 2 + fo
                        rh = arena[:, A_H + ec * TOK + tok0:A_H + ec * TOK + tok0 + 512]
                        if ec == NC - 1:
                            em.inc("pe")
                            em.op("tensor", lambda t, wbf=wbf, kk=kk, outv=outv, ec=ec, rh=rh, s=pes:
                                  t.matmul(outv, wbf[:, kk * 128:kk * 128 + 128], rh,
                                           start=(ec == 0), stop=True).then_inc(s, 1))
                        else:
                            em.op("tensor", lambda t, wbf=wbf, kk=kk, outv=outv, ec=ec, rh=rh:
                                  t.matmul(outv, wbf[:, kk * 128:kk * 128 + 128], rh,
                                           start=(ec == 0), stop=False))
                    em.wait("scalar", "pe")
                    em.inc("ac")
                    em.op("scalar", lambda sc, outv=outv, fcol=fcol, ss=acs:
                          sc.activation(out=arena[:, A_Y + fcol * 512:A_Y + fcol * 512 + 512],
                                        in_=outv, func=AF.Silu,
                                        bias=b1c[:, fcol:fcol + 1], scale=f32(1.0)).then_inc(ss, 1))
                wslot_mark[sl] = em.val("pe")
            em.wait("tensor", "ac")
            for dc in range(NC):
                sl2 = em.inc("ps2") % 2
                outv = ps45[:, sl2 * 512: sl2 * 512 + 512]
                em.wait("tensor", "dv", max(0, em.val("dv") - 1))
                for half16 in range(2):
                    sl, wbf, vld, dwn = wbf_load(
                        w2bf_dram.ap().rearrange("p (c f) -> p c f", c=32)
                        [:, half16 * 16:half16 * 16 + 16, dc * 128:dc * 128 + 128], SWEEP_ST)
                    em.wait("tensor", dwn, vld)
                    for kk in range(16):
                        fc = half16 * 16 + kk
                        rh = arena[:, A_Y + fc * 512:A_Y + fc * 512 + 512]
                        if kk == 15:
                            em.inc("pe")
                            em.op("tensor", lambda t, wbf=wbf, kk=kk, outv=outv, fc=fc, rh=rh,
                                  s=pes, last=(fc == 31):
                                  t.matmul(outv, wbf[:, kk * 128:kk * 128 + 128], rh,
                                           start=(fc == 0), stop=last).then_inc(s, 1))
                        else:
                            em.op("tensor", lambda t, wbf=wbf, kk=kk, outv=outv, fc=fc, rh=rh:
                                  t.matmul(outv, wbf[:, kk * 128:kk * 128 + 128], rh,
                                           start=(fc == 0), stop=False))
                    wslot_mark[sl] = em.val("pe")
                em.wait("scalar", "pe")
                tfx = (tf0, tf1)[em.inc("tfs") % 2]
                va = em.inc("ac")
                em.op("scalar", lambda sc, outv=outv, tfx=tfx, dc=dc, ss=acs:
                      sc.activation(out=tfx[:], in_=outv, func=AF.Identity,
                                    bias=hb2[:, dc:dc + 1], scale=f32(0.5)).then_inc(ss, 1))
                em.wait("vector", "ac", va)
                em.wait("vector", "dv", v0)
                xs = x_sb[:, dc * TOK + tok0: dc * TOK + tok0 + 512]
                em.inc("dv")
                em.op("vector", lambda v, xs=xs, tfx=tfx, s=dvs:
                      v.scalar_tensor_tensor(out=xs, in0=xs, scalar=2.0, in1=tfx[:],
                                             op0=ALU.mult, op1=ALU.add).then_inc(s, 1))
                em.op("vector", lambda v: v.drain())
    else:
            em.op("scalar", lambda s: s.activation(out=srow[0:1, 0:8], in_=uu[:], func=AF.Ln,
                                                   scale=f32(1.0)))
            em.op("scalar", lambda s: s.drain())
            va = em.inc("ac")
            em.op("scalar", lambda s, ss=acs:
                  s.activation(out=srow[0:1, 8:16], in_=srow[0:1, 0:8], func=AF.Ln,
                               scale=f32(-1.0)).then_inc(ss, 1))
            em.wait("vector", "ac", va)
            em.op("vector", lambda v: v.tensor_tensor(oh_r[:], lg[:], srow[0:1, 8:16], ALU.subtract))
            em.op("vector", lambda v: v.drain())
            em.op("vector", lambda v: v.reduce_max(srow[0:1, 16:17], oh_r[:], axis=mybir.AxisListType.X))
            em.op("vector", lambda v: v.drain())
            em.op("vector", lambda v: v.tensor_scalar(out=oh_r[:], in0=oh_r[:],
                                                      scalar1=srow[0:1, 16:17], scalar2=None,
                                                      op0=ALU.is_equal))
            em.op("vector", lambda v: v.drain())
            em.op("vector", lambda v: v.memset(oh_b[:], 0.0))
            em.op("vector", lambda v: v.drain())
            v1 = em.inc("dv")
            em.op("vector", lambda v, s=dvs: v.tensor_copy(oh_b[0:1, 0:8], oh_r[:]).then_inc(s, 1))
            em.wait("tensor", "dv", v1)
            em.inc("pe")
            em.op("tensor", lambda t, s=pes:
                  t.transpose(psT[:], oh_b[:], ident[:]).then_inc(s, 1))
            em.inc("pe")
            em.op("tensor", lambda t, s=pes:
                  t.matmul(ps45[:, 0:8], ones_row32[:], oh_r[:], start=True, stop=True).then_inc(s, 1))
            em.wait("vector", "pe")
            v2 = em.inc("dv")
            em.op("vector", lambda v, s=dvs: v.tensor_copy(ohT[:], psT[0:8, 0:1]).then_inc(s, 1))
            v3 = em.inc("dv")
            em.op("vector", lambda v, s=dvs: v.tensor_copy(ohb_sb[:], ps45[:, 0:8]).then_inc(s, 1))
            em.wait("tensor", "dv", v3)
            for dc in range(NC):
                last = dc == NC - 1
                em.inc("pe")
                em.op("tensor", lambda t, dc=dc, s=pes:
                      t.matmul(ps23[:, dc:dc + 1], b_sb[:, dc * 128:dc * 128 + 128], ohT[:],
                               start=True, stop=True).then_inc(s, 1))
            em.wait("vector", "pe")
            v4 = em.inc("dv")
            em.op("vector", lambda v, s=dvs: v.tensor_copy(beff_sb[:], ps23[:, 0:8]).then_inc(s, 1))
            wm = [0, 0]
            for dcin in range(NC):
                for i in range(8):
                    sl = em.inc("wsl") % 2
                    dwn = "d_w%d" % sl
                    wt = (wt0, wt1)[sl]
                    eng = ("gpsimd", "sync")[sl]
                    em.wait(eng, "dv", wm[sl])
                    vq = em.dma(eng, wt[:], din["wexp_cm"].ap()
                                [i, :, dcin * 1024:dcin * 1024 + 1024], dwn)
                    em.wait("vector", dwn, vq)
                    aco, aci = (acc0, acc1) if i % 2 == 0 else (acc1, acc0)
                    vv = em.inc("dv")
                    if i == 0:
                        em.op("vector", lambda v, wt=wt, aco=aco, i=i, s=dvs:
                              v.tensor_scalar_mul(aco[:], wt[:], ohb_sb[:, i:i + 1]).then_inc(s, 1))
                    elif i < 7:
                        em.op("vector", lambda v, wt=wt, aco=aco, aci=aci, i=i, s=dvs:
                              v.scalar_tensor_tensor(out=aco[:], in0=wt[:], scalar=ohb_sb[:, i:i + 1],
                                                     in1=aci[:], op0=ALU.mult, op1=ALU.add).then_inc(s, 1))
                    else:
                        em.op("vector", lambda v, wt=wt, aci=aci, dcin=dcin, s=dvs:
                              v.scalar_tensor_tensor(out=weff[:, dcin * 1024:dcin * 1024 + 1024],
                                                     in0=wt[:], scalar=ohb_sb[:, i:i + 1],
                                                     in1=aci[:], op0=ALU.mult, op1=ALU.add).then_inc(s, 1))
                    wm[sl] = em.val("dv")
            WEFF_dv = em.val("dv")
            for tb in range(4):
                tok0 = tb * 512
                emit_xb_tiles(em, T, x_sb, tok0)
                em.wait("tensor", "dv")
                for eo in range(NC):
                    sl4 = em.inc("psl") % 4
                    outv = (ps01, ps23)[sl4 // 2][:, (sl4 % 2) * 512:(sl4 % 2) * 512 + 512]
                    em.wait("tensor", "ac")
                    lhs = [weff[:, din_ * 1024 + eo * 128: din_ * 1024 + eo * 128 + 128]
                           for din_ in range(NC)]
                    rhs = [xb_slice(T, din_) for din_ in range(NC)]
                    emit_mm_group(em, T, lhs, rhs, outv, 128)
                    em.wait("scalar", "pe")
                    tfx = (tf0, tf1)[em.inc("tfs") % 2]
                    va = em.inc("ac")
                    em.op("scalar", lambda s, outv=outv, tfx=tfx, eo=eo, ss=acs:
                          s.activation(out=tfx[:], in_=outv, func=AF.Identity,
                                       bias=beff_sb[:, eo:eo + 1], scale=f32(1.0)).then_inc(ss, 1))
                    em.wait("vector", "ac", va)
                    xs = x_sb[:, eo * TOK + tok0: eo * TOK + tok0 + 512]
                    em.inc("dv")
                    em.op("vector", lambda v, xs=xs, tfx=tfx, s=dvs:
                          v.tensor_tensor(xs, xs, tfx[:], ALU.add).then_inc(s, 1))
                    em.op("vector", lambda v: v.drain())

    em.wait("tensor", "ac_e", em.val("ac_e"))
    emit_ln(em, T, x_sb, None, None)
    emit_store_x(em, T, x_out)

    with nc.Block() as block:
        em.play(block)
    es.close()
    return nc



def _consts():
    m01 = np.zeros((128, 2), np.float32)
    m01[0:64, 0] = 1.0
    m01[64:128, 1] = 1.0
    s2 = np.zeros((2, 128), np.float32)
    s2[0, 0:64] = 1.0
    s2[1, 64:128] = 1.0
    return {
        "c_ones_col": np.ones((128, 1), np.float32),
        "c_ones_row": np.ones((1, 128), np.float32),
        "c_ones_row32": np.ones((1, 128), np.float32),
        "c_ident": np.eye(128, dtype=np.float32),
        "c_mask01": m01,
        "c_sel2": s2,
        "c_minus_one": np.full((1, 512), -1.0, np.float32),
    }


def _w_chunk_major(w):
    K, N = w.shape
    return np.ascontiguousarray(w.reshape(K // 128, 128, N).transpose(1, 0, 2).reshape(128, -1))


def _wo2_layout(wo_eD):
    return np.ascontiguousarray(wo_eD.reshape(16, 64, 1024).transpose(1, 0, 2).reshape(64, -1))


def _attn_inputs(layer, part, xT, W):
    ins = {"x_in": np.ascontiguousarray(xT)}
    ins.update(_consts())
    if layer == 0:
        if part == "kv":
            ins["w_k"] = _w_chunk_major(W["L0_attn_Wk"])
            ins["w_v"] = _w_chunk_major(W["L0_attn_Wv"])
            ins["w_q"] = _w_chunk_major(W["L0_attn_Wq"])
        else:
            ins["w_o2"] = _wo2_layout(W["L0_attn_Wout"])
    else:
        if part == "kv":
            ins["w_k"] = _w_chunk_major(np.ascontiguousarray(W["L1_attn_Wk"].T))
            ins["w_v"] = _w_chunk_major(np.ascontiguousarray(W["L1_attn_Wv"].T))
            ins["w_q"] = _w_chunk_major(np.ascontiguousarray(W["L1_attn_Wq"].T))
            ins["ks_in"] = np.ascontiguousarray(
                W["L1_attn_k_scale"].reshape(16, 64).reshape(8, 128).T)
            ins["qs_in"] = np.ascontiguousarray(W["L1_attn_q_scale"].reshape(16, 64).T)
        else:
            ins["w_o2"] = _wo2_layout(np.ascontiguousarray(W["L1_attn_Wo"].T))
            ins["bo_in"] = np.ascontiguousarray(W["L1_attn_bo"].reshape(8, 128).T)
    return ins


def run_attn_block(layer, shards, W):
    nc_kv = build_attn(layer, "kv")
    in_maps = [_attn_inputs(layer, "kv", shards[c], W) for c in range(8)]
    res = run_bass_kernel_spmd(nc_kv, in_maps, core_ids=list(range(8)))
    ktf = {}
    vf = {}
    for g in range(2):
        ktf[g] = np.concatenate([np.asarray(res.results[g * 4 + c]["kt_out"]) for c in range(4)], axis=0)
        vf[g] = np.concatenate([np.asarray(res.results[g * 4 + c]["v_out"]) for c in range(4)], axis=0)
    qts = [np.ascontiguousarray(np.asarray(r["qt_out"])) for r in res.results]
    nc_core = build_attn(layer, "core")
    in_maps = []
    for c in range(8):
        m = _attn_inputs(layer, "core", shards[c], W)
        m["kt_full"] = ktf[c // 4]
        m["v_full"] = vf[c // 4]
        m["qt_in"] = qts[c]
        in_maps.append(m)
    res = run_bass_kernel_spmd(nc_core, in_maps, core_ids=list(range(8)))
    return [np.ascontiguousarray(np.asarray(r["x_out"])) for r in res.results]


def _ffn_inputs(layer, xT, W):
    ins = {"x_in": np.ascontiguousarray(xT)}
    ins.update(_consts())
    if layer == 0:
        q1 = W["L0_ffn_fc1_q"].astype(np.int32)
        q2 = W["L0_ffn_fc2_q"].astype(np.int32)
        ins["q1_cm"] = np.ascontiguousarray(q1.reshape(8, 128, 4096).transpose(1, 0, 2).reshape(128, -1))
        ins["q2_cm"] = np.ascontiguousarray(q2.reshape(32, 128, 1024).transpose(1, 0, 2).reshape(128, -1))
        ins["s1_flat"] = np.ascontiguousarray(W["L0_ffn_fc1_scales"].reshape(1, -1).astype(np.float32))
        ins["s2_flat"] = np.ascontiguousarray(W["L0_ffn_fc2_scales"].reshape(1, -1).astype(np.float32))
        ins["b1_c"] = np.ascontiguousarray(W["L0_ffn_fc1_bias"].reshape(32, 128).T.astype(np.float32))
        ins["b2_c"] = np.ascontiguousarray(W["L0_ffn_fc2_bias"].reshape(8, 128).T.astype(np.float32))
        ins["g_cols"] = np.ascontiguousarray(W["L0_ffn_ln_gamma"].reshape(8, 128).T.astype(np.float32))
        ins["b_cols"] = np.ascontiguousarray(W["L0_ffn_ln_beta"].reshape(8, 128).T.astype(np.float32))
    else:
        we = W["L1_ffn_W"].astype(np.float32)
        ins["wexp_cm"] = np.ascontiguousarray(
            we.reshape(8, 8, 128, 1024).transpose(0, 2, 1, 3).reshape(8, 128, 8192))
        ins["b_exp"] = np.ascontiguousarray(W["L1_ffn_b"].astype(np.float32))
        ins["lg_in"] = np.ascontiguousarray(W["L1_ffn_logits"].reshape(1, 8).astype(np.float32))
        ins["uu_in"] = np.ascontiguousarray(W["L1_ffn_u"].reshape(1, 8).astype(np.float32))
    return ins


def run_ffn_block(layer, shards, W):
    nc = build_ffn(layer)
    in_maps = [_ffn_inputs(layer, shards[c], W) for c in range(8)]
    res = run_bass_kernel_spmd(nc, in_maps, core_ids=list(range(8)))
    return [np.ascontiguousarray(np.asarray(r["x_out"])) for r in res.results]


def kernel(**inputs):
    W = {k: np.asarray(v) for k, v in inputs.items()}
    x = W["L0_attn_x"]
    shards = []
    for c in range(8):
        b, t0 = c // 4, (c % 4) * TOK
        shards.append(np.ascontiguousarray(x[b, t0:t0 + TOK, :].T.astype(np.float32)))

    shards = run_attn_block(0, shards, W)
    shards = run_ffn_block(0, shards, W)
    shards = run_attn_block(1, shards, W)
    shards = run_ffn_block(1, shards, W)
    out = np.empty((2, SEQ, D), np.float32)
    for c in range(8):
        b, t0 = c // 4, (c % 4) * TOK
        out[b, t0:t0 + TOK, :] = shards[c].T
    return out


# version 08 (final)
# speedup vs baseline: 1.1981x; 1.0141x over previous
import contextlib
import numpy as np
import ml_dtypes
import concourse.bass as bass
import concourse.mybir as mybir
from concourse.bass_utils import run_bass_kernel_spmd

F32 = mybir.dt.float32
BF16 = mybir.dt.bfloat16
I32 = mybir.dt.int32
AF = mybir.ActivationFunctionType
ALU = mybir.AluOpType

D = 1024
TOK = 2048
SEQ = 8192
NC = 8
HALF = 1024
EPS = 1e-5
ENGINES = ("sync", "gpsimd", "tensor", "scalar", "vector")


def f32(x):
    return np.float32(x)


class Emit:

    def __init__(self, sems):
        self.progs = {e: [] for e in ENGINES}
        self.ctr = {}
        self.sems = sems

    def op(self, eng, fn):
        self.progs[eng].append(fn)

    def val(self, name):
        return self.ctr.get(name, 0)

    def inc(self, name, k=1):
        self.ctr[name] = self.val(name) + k
        return self.ctr[name]

    def wait(self, eng, ctr, v=None):
        if v is None:
            v = self.val(ctr)
        if v <= 0:
            return
        s = self.sems[ctr]
        self.op(eng, lambda e, s=s, v=v: e.wait_ge(s, v))

    def dma(self, eng, out, in_, ctr):
        v = self.inc(ctr, 16)
        s = self.sems[ctr]
        self.op(eng, lambda e, out=out, in_=in_, s=s:
                e.dma_start(out=out, in_=in_).then_inc(s, 16))
        return v

    def play(self, block):
        for eng in ENGINES:
            prog = self.progs[eng]

            def run(e, prog=prog):
                for fn in prog:
                    fn(e)
            getattr(block, eng)(run)


SEM_NAMES = ("d_c", "d_w", "d_w0", "d_w1", "d_sk", "d_sv", "d_kv0", "d_kv1", "d_out", "cc",
             "pe", "ac", "dv", "pe_s", "pe_pv", "ac_e",
             "dv_v65", "dv_z", "dv_z1", "dv_z2", "pe_bc")


def _common_setup(nc, T, es):
    def sb(name, shape, dt):
        T[name] = es.enter_context(nc.sbuf_tensor(name, list(shape), dt))
        return T[name]

    def psa(name, shape, dt):
        T[name] = es.enter_context(nc.psum_tensor(name, list(shape), dt))
        return T[name]
    return sb, psa



def emit_xb_tiles(em, T, src_sb, tok0):
    arena = T["arena"]
    A_XT = T["A_XT"]
    em.wait("vector", "pe")
    s = em.sems["dv"]
    for ec in range(NC):
        em.inc("dv")
        em.op("vector", lambda vv, ec=ec, arena=arena, src=src_sb, tok0=tok0, A_XT=A_XT, s=s:
              vv.tensor_copy(arena[:, A_XT + ec * 512: A_XT + ec * 512 + 512],
                             src[:, ec * TOK + tok0: ec * TOK + tok0 + 512]).then_inc(s, 1))
    return em.val("dv")


def xb_slice(T, ec):
    arena = T["arena"]
    A_XT = T["A_XT"]
    return arena[:, A_XT + ec * 512: A_XT + ec * 512 + 512]


def emit_mm_group(em, T, lhs_list, rhs_list, outv, n):
    pes = em.sems["pe"]
    nl = len(lhs_list)
    for i, (lw, rh) in enumerate(zip(lhs_list, rhs_list)):
        if i < nl - 1:
            em.op("tensor", lambda t, lw=lw, rh=rh, i=i, outv=outv, n=n:
                  t.matmul(outv[0:n, :], lw, rh, start=(i == 0), stop=False))
    v = em.inc("pe")
    lw, rh = lhs_list[-1], rhs_list[-1]
    em.op("tensor", lambda t, lw=lw, rh=rh, outv=outv, n=n, s=pes, first=(nl == 1):
          t.matmul(outv[0:n, :], lw, rh, start=first, stop=True).then_inc(s, 1))
    return v


def emit_proj_group(em, T, w_sb, colbase, ncols, outv):
    lhs = [w_sb[:, colbase(ec):colbase(ec) + ncols] for ec in range(NC)]
    rhs = [xb_slice(T, ec) for ec in range(NC)]
    return emit_mm_group(em, T, lhs, rhs, outv, ncols)


def emit_ln(em, T, src_sb, gamma_beta=None, out_cols=None, tbs=None):
    arena = T["arena"]
    ps01, ps23 = T["ps01"], T["ps23"]
    ones_col = T["ones_col"]
    st_mu, st_ms, st_rs = T["st_mu"], T["st_ms"], T["st_rs"]
    st_var = T["st_var"][0:1, 0:512]
    st_B = T["st_B"][0:1, 512:1024]
    srow, ev0, ev1 = T["srow"], T["ev0"], T["ev1"]
    dvs, acs, pes = em.sems["dv"], em.sems["ac"], em.sems["pe"]
    em.op("vector", lambda v: v.drain())
    for tb in range(4):
        tok0 = tb * 512
        emit_xb_tiles(em, T, src_sb, tok0)
        for ec in range(NC):
            sq = (ev0, ev1)[ec % 2]
            em.wait("scalar", "dv")
            em.wait("scalar", "pe")
            va = em.inc("ac")
            em.op("scalar", lambda s, sq=sq, ec=ec, T=T, ss=acs:
                  s.activation(out=sq[:], in_=xb_slice(T, ec), func=AF.Square).then_inc(ss, 1))
            em.wait("tensor", "ac", va)
            em.inc("pe")
            em.op("tensor", lambda t, sq=sq, ec=ec, T=T, oc=ones_col, s=pes:
                  (t.matmul(ps01[0:1, 0:512], oc[:], xb_slice(T, ec),
                            start=(ec == 0), stop=(ec == NC - 1)),
                   t.matmul(ps01[0:1, 512:1024], oc[:], sq[:],
                            start=(ec == 0), stop=(ec == NC - 1)).then_inc(s, 1)))
        em.wait("vector", "pe")
        em.op("vector", lambda v: v.tensor_scalar_mul(st_mu[:], ps01[0:1, 0:512], 1.0 / D))
        em.op("vector", lambda v: v.tensor_scalar_mul(st_ms[:], ps01[0:1, 512:1024], 1.0 / D))
        em.op("vector", lambda v: v.drain())
        em.op("vector", lambda v: v.tensor_tensor(st_var, st_mu[:], st_mu[:], ALU.mult))
        em.op("vector", lambda v: v.drain())
        v1 = em.inc("dv")
        em.op("vector", lambda v, s=dvs:
              v.tensor_tensor(st_var, st_ms[:], st_var, ALU.subtract).then_inc(s, 1))
        em.wait("scalar", "dv", v1)
        em.op("scalar", lambda s: s.activation(out=srow[0:1, 0:512], in_=st_var[:],
                                               func=AF.Ln, bias=f32(EPS), scale=f32(1.0)))
        em.op("scalar", lambda s: s.drain())
        va = em.inc("ac")
        em.op("scalar", lambda s, ss=acs:
              s.activation(out=st_rs[:], in_=srow[0:1, 0:512], func=AF.Exp,
                           scale=f32(-0.5)).then_inc(ss, 1))
        em.wait("vector", "ac", va)
        v2 = em.inc("dv")
        em.op("vector", lambda v, s=dvs:
              v.tensor_tensor(st_B, st_mu[:], st_rs[:], ALU.mult).then_inc(s, 1))
        em.wait("tensor", "dv", v2)
        if gamma_beta is None:
            orow32 = T["ones_row32"]
            em.inc("pe")
            em.op("tensor", lambda t, orow=orow32, s=pes:
                  (t.matmul(ps23[:, 0:512], orow[:], st_rs[:], start=True, stop=True),
                   t.matmul(ps23[:, 512:1024], orow[:], st_B, start=True, stop=True).then_inc(s, 1)))
            em.wait("vector", "pe")
            for ec in range(NC):
                xs = src_sb[:, ec * TOK + tok0: ec * TOK + tok0 + 512]
                em.op("vector", lambda v, xs=xs: v.tensor_tensor(xs, xs, ps23[:, 0:512], ALU.mult))
                em.op("vector", lambda v: v.drain())
                em.inc("dv")
                em.op("vector", lambda v, xs=xs, s=dvs:
                      v.tensor_tensor(xs, xs, ps23[:, 512:1024], ALU.subtract).then_inc(s, 1))
                em.op("vector", lambda v: v.drain())
        else:
            g_c, b_c = gamma_beta
            orow32 = T["ones_row32"]
            em.inc("pe")
            em.op("tensor", lambda t, orow=orow32, s=pes:
                  (t.matmul(ps23[:, 0:512], orow[:], st_rs[:], start=True, stop=True),
                   t.matmul(ps23[:, 512:1024], orow[:], st_B, start=True, stop=True).then_inc(s, 1)))
            em.wait("vector", "pe")
            for ec in range(NC):
                xs = src_sb[:, ec * TOK + tok0: ec * TOK + tok0 + 512]
                tf = (T["tf0"], T["tf1"])[ec % 2]
                em.op("vector", lambda v, xs=xs, tf=tf:
                      v.tensor_tensor(tf[:], xs, ps23[:, 0:512], ALU.mult))
                em.op("vector", lambda v: v.drain())
                em.op("vector", lambda v, tf=tf:
                      v.tensor_tensor(tf[:], tf[:], ps23[:, 512:1024], ALU.subtract))
                em.op("vector", lambda v: v.drain())
                em.inc("dv")
                dst = arena[:, out_cols + ec * TOK + tok0: out_cols + ec * TOK + tok0 + 512]
                em.op("vector", lambda v, tf=tf, dst=dst, ec=ec, s=dvs, g_c=g_c, b_c=b_c:
                      v.tensor_scalar(out=dst, in0=tf[:], scalar1=g_c[:, ec:ec + 1],
                                      scalar2=b_c[:, ec:ec + 1], op0=ALU.mult,
                                      op1=ALU.add).then_inc(s, 1))
                em.op("vector", lambda v: v.drain())
    return em.val("dv")


def emit_store_x(em, T, x_out):
    x_sb = T["x_sb"]
    em.wait("sync", "dv")
    em.wait("sync", "ac")
    em.dma("sync", x_out.ap().rearrange("(c p) t -> p c t", p=128),
           x_sb[:].rearrange("p (c t) -> p c t", c=NC), "d_out")
    em.wait("sync", "d_out")



def build_attn(layer: int, part: str):
    nc = bass.Bass("TRN2", target_bir_lowering=False)
    din = {}

    def dram_in(name, shape, dt=F32):
        din[name] = nc.dram_tensor(name, list(shape), dt, kind="ExternalInput")
        return din[name]

    dram_in("x_in", [D, TOK])
    dram_in("c_ones_col", [128, 1])
    dram_in("c_ones_row", [1, 128])
    dram_in("c_ident", [128, 128])
    dram_in("c_mask01", [128, 2])
    dram_in("c_sel2", [2, 128])
    dram_in("c_ones_row32", [1, 128])
    if part == "kv":
        dram_in("w_k", [128, 8192])
        dram_in("w_v", [128, 8192])
        if layer == 1:
            dram_in("ks_in", [128, 8])
        kt_loc = nc.dram_tensor("kt_out", [D, TOK], BF16, kind="ExternalOutput")
        v_loc = nc.dram_tensor("v_out", [TOK, D], BF16, kind="ExternalOutput")
        kt_full = v_full = None
    else:
        dram_in("w_q", [128, 8192])
        dram_in("w_o2", [64, 16384])
        if layer == 1:
            dram_in("qs_in", [64, 16])
            dram_in("bo_in", [128, 8])
        kt_full = dram_in("kt_full", [4 * D, TOK], BF16)
        v_full = dram_in("v_full", [SEQ, D], BF16)
        x_out = nc.dram_tensor("x_out", [D, TOK], F32, kind="ExternalOutput")
        rl_dram = nc.dram_tensor("rl_dram", [1, 1024], F32)

    es = contextlib.ExitStack()
    sems = {s: es.enter_context(nc.semaphore(s)) for s in SEM_NAMES}
    em = Emit(sems)
    T = {}
    sb, psa = _common_setup(nc, T, es)

    x_sb = sb("x_sb", [128, NC * TOK], F32)
    arena = sb("arena", [128, 18496], BF16)
    qt_sb = sb("qt_sb", [128, 16384], BF16)
    zt_sb = sb("zt_sb", [128, 16384], BF16)
    w_sb = sb("w_sb", [128, 8192], BF16)
    ev0 = sb("ev0", [128, 512], BF16)
    ev1 = sb("ev1", [128, 512], BF16)
    sq_t = sb("sq_t", [128, 512], BF16)
    tf0 = sb("tf0", [128, 512], F32)
    tf1 = sb("tf1", [128, 512], F32)
    rl_row = sb("rl_row", [1, 1024], F32)
    if part == "core":
        bcP = sb("bcP", [64, 1024], F32)
        T["bcP"] = bcP
    srow = sb("srow", [2 if part == "kv" else 1, 1024], F32)
    bcA = sb("bcA", [64, 512], BF16) if part == "kv" else None
    bc64 = None
    st_mu = sb("st_mu", [1, 512], F32)
    st_ms = sb("st_ms", [1, 512], F32)
    st_var = rl_row[0:1, 0:512].tensor if False else None
    st_rs = sb("st_rs", [1, 512], F32)
    ones_col = sb("ones_col", [128, 1], BF16)
    ones_row = sb("ones_row", [1, 128], BF16)
    ones_row32 = sb("ones_row32", [1, 128], F32)
    ident = mask01 = sel2 = None
    if part == "kv":
        ident = sb("ident", [128, 128], BF16)
        mask01 = sb("mask01", [128, 2], BF16)
        sel2 = sb("sel2", [2, 128], F32)
    qs_sb = ks_sb = bo_sb = None
    if layer == 1 and part == "core":
        qs_sb = sb("qs_sb", [64, 16], F32)
        bo_sb = sb("bo_sb", [128, 8], F32)
    if layer == 1 and part == "kv":
        ks_sb = sb("ks_sb", [128, 8], F32)

    ps01 = psa("ps01", [128, 1024], F32)
    ps23 = psa("ps23", [128, 1024], F32)
    ps45 = psa("ps45", [128, 1024], F32)
    ps6 = psa("ps6", [128, 512], F32)
    psT = psa("psT", [128, 1024], BF16)

    A_STAGE, A_XT = 0, 8192
    A_KT, A_VT, A_V65, A_P, A_WO = 0, 8192, 10240, 14400, 0
    T.update(arena=arena, x_sb=x_sb, A_XT=A_XT, ps01=ps01, ps23=ps23,
             ones_col=ones_col, ones_row=ones_row, ones_row32=ones_row32, st_mu=st_mu, st_ms=st_ms,
             st_rs=st_rs, srow=srow, ev0=ev0, ev1=ev1,
             tf0=tf0, tf1=tf1)
    T["st_var"] = rl_row
    T["st_B"] = rl_row
    if part == "core":
        T["rl_dram"] = rl_dram
    bc64 = bcA if part == "kv" else bcP[:, 0:512]

    em.dma("gpsimd", x_sb[:].rearrange("p (c t) -> p c t", c=NC),
           din["x_in"].ap().rearrange("(c p) t -> p c t", p=128), "d_c")
    const_list = [("c_ones_col", ones_col), ("c_ones_row", ones_row),
                  ("c_ones_row32", ones_row32)]
    if part == "kv":
        const_list += [("c_ident", ident), ("c_mask01", mask01), ("c_sel2", sel2)]
    for nm, dst in const_list:
        em.dma("gpsimd", dst[:], din[nm].ap(), "d_c")
    if layer == 1 and part == "core":
        em.dma("gpsimd", qs_sb[:], din["qs_in"].ap(), "d_c")
        em.dma("gpsimd", bo_sb[:], din["bo_in"].ap(), "d_c")
    if layer == 1 and part == "kv":
        em.dma("gpsimd", ks_sb[:], din["ks_in"].ap(), "d_c")
    em.wait("vector", "d_c")
    em.wait("tensor", "d_c")

    def load_w(name):
        em.wait("gpsimd", "pe")
        return em.dma("gpsimd", w_sb[:].rearrange("p (c f) -> p c f", c=NC),
                      din[name].ap().rearrange("p (c f) -> p c f", c=NC), "d_w")

    def pslot(i):
        t = (ps01, ps23)[(i // 2) % 2]
        return t[:, (i % 2) * 512:(i % 2) * 512 + 512]

    dvs, acs, pes = sems["dv"], sems["ac"], sems["pe"]

    def norm_stage_slice(dst, kind, idx):
        nrows = 2 if kind == "k" else 1
        em.wait("vector", "dv")
        em.op("vector", lambda v: v.drain())
        v1 = em.inc("dv")
        em.op("vector", lambda v, dst=dst, s=dvs:
              v.tensor_tensor(sq_t[0:dst.shape[0], :], dst, dst, ALU.mult).then_inc(s, 1))
        em.wait("tensor", "dv", v1)
        em.inc("pe")
        if kind == "k":
            em.op("tensor", lambda t, s=pes:
                  t.matmul(ps6[0:2, :], mask01[:], sq_t[:], start=True, stop=True).then_inc(s, 1))
        else:
            em.op("tensor", lambda t, s=pes:
                  t.matmul(ps6[0:1, :], ones_col[0:64, :], sq_t[0:64, :],
                           start=True, stop=True).then_inc(s, 1))
        em.wait("scalar", "pe")
        em.op("scalar", lambda s, nr=nrows:
              s.activation(out=srow[0:nr, 0:512], in_=ps6[0:nr, :], func=AF.Ln, scale=f32(1.0)))
        em.op("scalar", lambda s: s.drain())
        va = em.inc("ac")
        em.op("scalar", lambda s, nr=nrows, ss=acs:
              s.activation(out=srow[0:nr, 512:1024], in_=srow[0:nr, 0:512], func=AF.Exp,
                           scale=f32(-0.5)).then_inc(ss, 1))
        em.wait("vector", "ac", va)
        v2 = em.inc("dv")
        em.op("vector", lambda v, nr=nrows, s=dvs:
              v.tensor_single_scalar(srow[0:nr, 0:512], srow[0:nr, 512:1024],
                                     1e12, ALU.min).then_inc(s, 1))
        em.wait("tensor", "dv", v2)
        em.inc("pe")
        if kind == "k":
            em.op("tensor", lambda t, s=pes:
                  t.matmul(ps6[:, :], sel2[:], srow[0:2, 0:512], start=True, stop=True).then_inc(s, 1))
        else:
            em.op("tensor", lambda t, s=pes:
                  t.matmul(ps6[0:64, :], ones_row32[0:1, 0:64], srow[0:1, 0:512],
                           start=True, stop=True).then_inc(s, 1))
        em.wait("vector", "pe")
        nb = dst.shape[0]
        bct = bc64 if nb == 64 else sq_t[0:nb, 0:512]
        em.op("vector", lambda v, nb=nb, bct=bct: v.tensor_copy(bct, ps6[0:nb, :]))
        em.op("vector", lambda v: v.drain())
        em.op("vector", lambda v, dst=dst, nb=nb, bct=bct:
              v.tensor_tensor(dst, dst, bct, ALU.mult))
        em.op("vector", lambda v: v.drain())
        sc = (ks_sb if kind == "k" else qs_sb)
        v4 = em.inc("dv")
        em.op("vector", lambda v, dst=dst, sc=sc, idx=idx, s=dvs:
              v.tensor_scalar_mul(dst, dst, sc[0:dst.shape[0], idx:idx + 1]).then_inc(s, 1))
        em.op("vector", lambda v: v.drain())
        return v4

    if part == "kv":
      load_w("w_k")
      em.wait("tensor", "d_w")
      for half in range(2):
        em.wait("vector", "d_sk")
        for tb in range(2):
            tok0 = half * HALF + tb * 512
            emit_xb_tiles(em, T, x_sb, tok0)
            em.wait("tensor", "dv")
            for fc in range(NC):
                outv = pslot(em.inc("psl") % 4)
                em.wait("tensor", "dv")
                emit_proj_group(em, T, w_sb, lambda ec, fc=fc: ec * 1024 + fc * 128, 128, outv)
                em.wait("vector", "pe")
                dst = arena[:, A_STAGE + fc * 1024 + tb * 512: A_STAGE + fc * 1024 + tb * 512 + 512]
                em.inc("dv")
                em.op("vector", lambda vv, outv=outv, dst=dst, s=dvs:
                      vv.tensor_copy(dst, outv).then_inc(s, 1))
        if layer == 1:
            scr = qt_sb[0:128, 0:1024]
            em.wait("vector", "pe")
            em.op("vector", lambda v: v.drain())
            km_mark = {}
            for fc in range(NC):
                stg = arena[:, A_STAGE + fc * 1024: A_STAGE + fc * 1024 + 1024]
                if fc >= 1:
                    em.wait("vector", "pe", km_mark[fc - 1])
                v1 = em.inc("dv")
                em.op("vector", lambda v, stg=stg, scr=scr, s=dvs:
                      v.tensor_tensor(scr, stg, stg, ALU.mult).then_inc(s, 1))
                em.wait("tensor", "dv", v1)
                m1 = em.inc("pe")
                em.op("tensor", lambda t, scr=scr, s=sems["pe"], mk=mask01:
                      (t.matmul(ps01[0:2, 0:512], mk[:], scr[:, 0:512],
                                start=True, stop=True),
                       t.matmul(ps01[0:2, 512:1024], mk[:], scr[:, 512:1024],
                                start=True, stop=True).then_inc(s, 1)))
                em.wait("scalar", "pe", m1)
                em.op("scalar", lambda sc: sc.activation(out=srow[0:2, 0:1024],
                                                         in_=ps01[0:2, 0:1024],
                                                         func=AF.Ln, scale=f32(1.0)))
                em.op("scalar", lambda sc: sc.drain())
                va = em.inc("ac")
                em.op("scalar", lambda sc, ss=acs:
                      sc.activation(out=srow[0:2, 0:1024], in_=srow[0:2, 0:1024],
                                    func=AF.Exp, scale=f32(-0.5)).then_inc(ss, 1))
                em.wait("vector", "ac", va)
                em.op("vector", lambda v: v.drain())
                v2 = em.inc("dv")
                em.op("vector", lambda v, s=dvs:
                      v.tensor_single_scalar(srow[0:2, 0:1024], srow[0:2, 0:1024],
                                             1e12, ALU.min).then_inc(s, 1))
                em.wait("tensor", "dv", v2)
                m2 = em.inc("pe")
                em.op("tensor", lambda t, s=sems["pe"], sl2=sel2:
                      (t.matmul(ps23[:, 0:512], sl2[:], srow[0:2, 0:512],
                                start=True, stop=True),
                       t.matmul(ps23[:, 512:1024], sl2[:], srow[0:2, 512:1024],
                                start=True, stop=True).then_inc(s, 1)))
                km_mark[fc] = m2
                em.wait("vector", "pe", m2)
                em.op("vector", lambda v, stg=stg:
                      v.tensor_tensor(stg, stg, ps23[:, 0:1024], ALU.mult))
                em.op("vector", lambda v: v.drain())
                em.inc("dv")
                em.op("vector", lambda v, stg=stg, fc=fc, s=dvs, ks=ks_sb:
                      v.tensor_scalar_mul(stg, stg, ks[:, fc:fc + 1]).then_inc(s, 1))
                em.op("vector", lambda v: v.drain())
        em.wait("sync", "dv")
        em.dma("sync",
               kt_loc.ap().rearrange("(c p) t -> p c t", p=128)[:, :, half * HALF:(half + 1) * HALF],
               arena[:, A_STAGE:A_STAGE + 8192].rearrange("p (c t) -> p c t", c=8), "d_sk")

    if part == "kv":
      load_w("w_v")
      em.wait("tensor", "d_w")
      for half in range(2):
        em.wait("vector", "d_sk")
        em.wait("vector", "d_sv")
        for tb in range(2):
            tok0 = half * HALF + tb * 512
            emit_xb_tiles(em, T, x_sb, tok0)
            em.wait("tensor", "dv")
            for fc in range(NC):
                outv = pslot(em.inc("psl") % 4)
                em.wait("tensor", "dv")
                emit_proj_group(em, T, w_sb, lambda ec, fc=fc: ec * 1024 + fc * 128, 128, outv)
                em.wait("vector", "pe")
                raw = (ev0, ev1)[em.inc("evs") % 2]
                v0 = em.inc("dv")
                em.op("vector", lambda vv, outv=outv, raw=raw, s=dvs:
                      vv.tensor_copy(raw[:], outv).then_inc(s, 1))
                em.wait("tensor", "dv", v0)
                em.wait("tensor", "dv")
                em.inc("pe")
                for tt in range(4):
                    em.op("tensor", lambda t, raw=raw, tt=tt, s=pes:
                          (t.transpose(psT[:, tt * 128: tt * 128 + 128],
                                       raw[:, tt * 128:tt * 128 + 128], ident[:]).then_inc(s, 1)
                           if tt == 3 else
                           t.transpose(psT[:, tt * 128: tt * 128 + 128],
                                       raw[:, tt * 128:tt * 128 + 128], ident[:])))
                em.wait("vector", "pe")
                for tt in range(4):
                    tc = tb * 4 + tt
                    dst = arena[:, A_STAGE + tc * 1024 + fc * 128: A_STAGE + tc * 1024 + fc * 128 + 128]
                    em.inc("dv")
                    em.op("vector", lambda vv, dst=dst, tt=tt, s=dvs:
                          vv.tensor_copy(dst, psT[:, tt * 128:tt * 128 + 128]).then_inc(s, 1))
        em.wait("sync", "dv")
        em.dma("sync",
               v_loc.ap().rearrange("(c p) f -> p c f", p=128)[:, half * 8:half * 8 + 8, :],
               arena[:, A_STAGE:A_STAGE + 8192].rearrange("p (c f) -> p c f", c=8), "d_sv")
      em.wait("sync", "d_sk")
      em.wait("sync", "d_sv")
      with nc.Block() as block:
          em.play(block)
      es.close()
      return nc

    CC_ALL = 0
    load_w("w_q")

    em.op("vector", lambda v: v.memset(qt_sb[64:128, :], 0.0))
    em.op("vector", lambda v: v.memset(zt_sb[64:128, :], 0.0))
    em.op("vector", lambda v: v.memset(arena[64:128, 0:8192], 0.0))
    em.op("vector", lambda v: v.memset(arena[:, 14400:18496], 0.0))
    em.op("vector", lambda v: v.drain())
    vpad = em.inc("dv")
    em.op("vector", lambda v, s=dvs: v.nop().then_inc(s, 1))
    em.wait("tensor", "dv", vpad)
    em.wait("sync", "dv", vpad)

    def kt_off(par, ab):
        return A_KT + (ab * 2 + par) * 2048

    kv_ld = {}
    for half in range(2):
        em.wait("tensor", "d_w")
        em.wait("tensor", "ac_e", em.val("ac_e"))
        for tb in range(2):
            tok0 = half * HALF + tb * 512
            emit_xb_tiles(em, T, x_sb, tok0)
            em.wait("tensor", "dv")
            for hg in range(8):
                outv = pslot(em.inc("psl") % 4)
                em.wait("tensor", "dv")
                emit_proj_group(em, T, w_sb, lambda ec, hg=hg: ec * 1024 + hg * 128, 128, outv)
                em.wait("vector", "pe")
                dst0 = qt_sb[0:64, hg * 1024 + tb * 512: hg * 1024 + tb * 512 + 512]
                dst1 = qt_sb[0:64, 8192 + hg * 1024 + tb * 512:
                             8192 + hg * 1024 + tb * 512 + 512]
                em.op("vector", lambda vv, outv=outv, dst0=dst0:
                      vv.tensor_copy(dst0, outv[0:64, :]))
                em.inc("dv")
                em.op("vector", lambda vv, outv=outv, dst1=dst1, s=dvs:
                      vv.tensor_copy(dst1, outv[64:128, :]).then_inc(s, 1))
        if layer == 1:
            scr = arena[0:64, 0:1024]
            em.wait("vector", "pe")
            em.op("vector", lambda v: v.drain())
            nm_mark = {}
            for idx in range(16):
                ab, pair = idx % 2, idx // 2
                hc = pair * 2 + ab
                qsl = qt_sb[0:64, ab * 8192 + pair * 1024: ab * 8192 + pair * 1024 + 1024]
                if idx >= 1:
                    em.wait("vector", "pe", nm_mark[idx - 1][0])
                v1 = em.inc("dv")
                em.op("vector", lambda v, qsl=qsl, scr=scr, s=dvs:
                      v.tensor_tensor(scr, qsl, qsl, ALU.mult).then_inc(s, 1))
                em.wait("tensor", "dv", v1)
                m1 = em.inc("pe")
                em.op("tensor", lambda t, scr=scr, s=sems["pe"], oc=ones_col:
                      (t.matmul(ps01[0:1, 0:512], oc[0:64, :], scr[:, 0:512],
                                start=True, stop=True),
                       t.matmul(ps01[0:1, 512:1024], oc[0:64, :], scr[:, 512:1024],
                                start=True, stop=True).then_inc(s, 1)))
                em.wait("scalar", "pe", m1)
                em.op("scalar", lambda sc: sc.activation(out=srow[0:1, 0:1024],
                                                         in_=ps01[0:1, 0:1024],
                                                         func=AF.Ln, scale=f32(1.0)))
                em.op("scalar", lambda sc: sc.drain())
                va = em.inc("ac")
                em.op("scalar", lambda sc, ss=acs:
                      sc.activation(out=rl_row[0:1, 0:1024], in_=srow[0:1, 0:1024],
                                    func=AF.Exp, scale=f32(-0.5)).then_inc(ss, 1))
                em.wait("vector", "ac", va)
                em.op("vector", lambda v: v.drain())
                v2 = em.inc("dv")
                em.op("vector", lambda v, s=dvs:
                      v.tensor_single_scalar(rl_row[0:1, 0:1024], rl_row[0:1, 0:1024],
                                             1e12, ALU.min).then_inc(s, 1))
                em.wait("tensor", "dv", v2)
                m2 = em.inc("pe")
                em.op("tensor", lambda t, s=sems["pe"], orow=ones_row32:
                      (t.matmul(ps23[0:64, 0:512], orow[0:1, 0:64], rl_row[0:1, 0:512],
                                start=True, stop=True),
                       t.matmul(ps23[0:64, 512:1024], orow[0:1, 0:64], rl_row[0:1, 512:1024],
                                start=True, stop=True).then_inc(s, 1)))
                em.wait("vector", "pe", m2)
                em.op("vector", lambda v, qsl=qsl:
                      v.tensor_tensor(qsl, qsl, ps23[0:64, 0:1024], ALU.mult))
                em.op("vector", lambda v: v.drain())
                v3 = em.inc("dv")
                em.op("vector", lambda v, qsl=qsl, hc=hc, s=dvs, qs=qs_sb:
                      v.tensor_scalar_mul(qsl, qsl, qs[:, hc:hc + 1]).then_inc(s, 1))
                nm_mark[idx] = (m2, v3)
        QP_pe = em.val("pe")
        QP_dv = em.val("dv")

        em.wait("vector", "pe")
        em.op("vector", lambda v:
              v.memset(arena[:, A_V65 + 0 * 2080:A_V65 + 1 * 2080]
                       .rearrange("p (c h u) -> p c h u", h=2, u=65)[:, :, :, 64:65], 1.0))
        em.op("vector", lambda v:
              v.memset(arena[:, A_V65 + 1 * 2080:A_V65 + 2 * 2080]
                       .rearrange("p (c h u) -> p c h u", h=2, u=65)[:, :, :, 64:65], 1.0))
        em.op("vector", lambda v: v.drain())
        vones = em.inc("dv")
        em.op("vector", lambda v, s=dvs: v.nop().then_inc(s, 1))

        for tidx in range(16):
            pair, qb = tidx // 2, tidx % 2
            j = half * 16 + tidx
            if tidx > 0:
                _emit_trio_evac(em, T, sems, half, tidx - 1, pair_prev=(tidx - 1) // 2,
                                qb_prev=(tidx - 1) % 2, zt_sb=zt_sb, ps45=ps45, ps6=ps6,
                                rl_row=rl_row, bcA=bcA)
            for g in range(4):
                Gg = j * 4 + g
                par = Gg % 2
                kvs = "d_kv%d" % par
                if Gg >= 2:
                    em.wait("sync", "pe_s", 16 * (Gg - 1))
                if Gg >= 1:
                    em.wait("sync", "dv_v65", Gg)
                if tidx == 0:
                    em.wait("sync", "pe", QP_pe)
                    em.wait("sync", "dv", QP_dv)

                em.dma("sync", arena[0:64, kt_off(par, 0):kt_off(par, 0) + 2048],
                       kt_full.ap()[g * 1024 + pair * 128: g * 1024 + pair * 128 + 64, :], kvs)
                em.dma("sync", arena[0:64, kt_off(par, 1):kt_off(par, 1) + 2048],
                       kt_full.ap()[g * 1024 + pair * 128 + 64: g * 1024 + pair * 128 + 128, :], kvs)
                em.dma("sync", arena[:, A_VT:A_VT + 2048]
                       .rearrange("p (c f) -> p c f", c=16),
                       v_full.ap().rearrange("(c p) f -> p c f", p=128)
                       [:, g * 16:(g + 1) * 16, pair * 128:(pair + 1) * 128], kvs)
                kv_ld[Gg] = (kvs, em.val(kvs))
                em.wait("vector", kv_ld[Gg][0], kv_ld[Gg][1])
                if Gg >= 2:
                    em.wait("vector", "pe_pv", 16 * (Gg - 1))
                if tidx == 0:
                    em.wait("vector", "dv", vones)
                vsrc = arena[:, A_VT:A_VT + 2048] \
                    .rearrange("p (c h dd) -> p c h dd", c=16, h=2, dd=64)
                vdst = arena[:, A_V65 + par * 2080:A_V65 + (par + 1) * 2080] \
                    .rearrange("p (c h u) -> p c h u", c=16, h=2, u=65)
                em.inc("dv_v65")
                em.op("vector", lambda v, vsrc=vsrc, vdst=vdst, s=sems["dv_v65"]:
                      v.tensor_copy(vdst[:, :, :, 0:64], vsrc).then_inc(s, 1))
            for ch in range(64):
                I = half * 1024 + tidx * 64 + ch
                Gg = I // 16
                par = Gg % 2
                psS = (ps01, ps23)[I % 2]
                c16 = ch % 16
                qA = qt_sb[0:128, pair * 1024 + qb * 512: pair * 1024 + qb * 512 + 512]
                qB = qt_sb[0:128, 8192 + pair * 1024 + qb * 512: 8192 + pair * 1024 + qb * 512 + 512]
                if c16 == 0:
                    em.wait("tensor", kv_ld[Gg][0], kv_ld[Gg][1])
                em.wait("tensor", "ac_e", I - 1)
                ktA = arena[0:128, kt_off(par, 0) + c16 * 128: kt_off(par, 0) + c16 * 128 + 128]
                ktB = arena[0:128, kt_off(par, 1) + c16 * 128: kt_off(par, 1) + c16 * 128 + 128]
                em.op("tensor", lambda t, ktA=ktA, qA=qA, psS=psS:
                      t.matmul(psS[:, 0:512], ktA, qA, start=True, stop=True))
                em.inc("pe_s")
                em.op("tensor", lambda t, ktB=ktB, qB=qB, psS=psS, s=sems["pe_s"]:
                      t.matmul(psS[:, 512:1024], ktB, qB, start=True, stop=True).then_inc(s, 1))
                if I - 2 >= half * 1024:
                    _emit_pv(em, T, sems, half, I - 2, arena, qt_sb, zt_sb, ps45, ps6,
                             rl_row, bcA, A_P, A_V65, kt_off)
                em.wait("scalar", "pe_s", I + 1)
                em.wait("scalar", "pe_pv", I - 2)
                em.inc("ac_e")
                pAB = arena[:, A_P + (I % 3) * 1024: A_P + (I % 3) * 1024 + 1024]
                em.op("scalar", lambda s, psS=psS, pAB=pAB, ss=sems["ac_e"]:
                      s.activation(out=pAB, in_=psS[:, 0:1024], func=AF.Exp,
                                   scale=f32(0.125)).then_inc(ss, 1))
        for Itail in (half * 1024 + 1022, half * 1024 + 1023):
            em.wait("tensor", "ac_e", Itail + 1)
            _emit_pv(em, T, sems, half, Itail, arena, qt_sb, zt_sb, ps45, ps6,
                     rl_row, bcA, A_P, A_V65, kt_off)
        _emit_trio_evac(em, T, sems, half, 15, pair_prev=7, qb_prev=1, zt_sb=zt_sb,
                        ps45=ps45, ps6=ps6, rl_row=rl_row, bcA=bcA)

        em.wait("gpsimd", "pe_pv", (half + 1) * 1024)
        em.dma("gpsimd", arena[0:64, A_WO:A_WO + 16384],
               din["w_o2"].ap(), "d_w")
        em.wait("tensor", "d_w")
        em.wait("tensor", "dv_z", (half + 1) * 16)
        for dc in range(NC):
            for tb in range(2):
                tok0 = half * HALF + tb * 512
                sl = em.inc("ops") % 2
                outv = ps45[:, sl * 512: sl * 512 + 512]
                em.wait("tensor", "dv")
                lhs = [arena[0:128, A_WO + hc * 1024 + dc * 128: A_WO + hc * 1024 + dc * 128 + 128]
                       for hc in range(16)]
                rhs = [zt_sb[0:128, hc * 1024 + tb * 512: hc * 1024 + tb * 512 + 512]
                       for hc in range(16)]
                emit_mm_group(em, T, lhs, rhs, outv, 128)
                em.wait("vector", "pe")
                xs = x_sb[:, dc * TOK + tok0: dc * TOK + tok0 + 512]
                if layer == 0:
                    em.inc("dv")
                    em.op("vector", lambda v, xs=xs, outv=outv, s=dvs:
                          v.scalar_tensor_tensor(out=xs, in0=xs, scalar=2.0, in1=outv,
                                                 op0=ALU.mult, op1=ALU.add).then_inc(s, 1))
                    em.op("vector", lambda v: v.drain())
                else:
                    em.wait("scalar", "pe")
                    tfx = (tf0, tf1)[em.inc("tfs") % 2]
                    va = em.inc("ac")
                    em.op("scalar", lambda s, outv=outv, tfx=tfx, dc=dc, ss=acs, bo=bo_sb:
                          s.activation(out=tfx[:], in_=outv, func=AF.Identity,
                                       bias=bo[:, dc:dc + 1], scale=f32(1.0)).then_inc(ss, 1))
                    em.wait("vector", "ac", va)
                    em.inc("dv")
                    em.op("vector", lambda v, xs=xs, tfx=tfx, s=dvs:
                          v.tensor_tensor(xs, xs, tfx[:], ALU.add).then_inc(s, 1))
                    em.op("vector", lambda v: v.drain())

    em.wait("tensor", "ac_e", em.val("ac_e"))
    emit_ln(em, T, x_sb, None, None)
    emit_store_x(em, T, x_out)

    with nc.Block() as block:
        em.play(block)
    es.close()
    return nc


def _emit_pv(em, T, sems, half, Iprev, arena, qt_sb, zt_sb, ps45, ps6, rl_row, bcA,
             A_P, A_V65, kt_off):
    i_loc = Iprev - half * 1024
    tidx = i_loc // 64
    ch = i_loc % 64
    pair, qb = tidx // 2, tidx % 2
    Gg = Iprev // 16
    par = Gg % 2
    c16 = ch % 16
    j = half * 16 + tidx
    if ch == 0:
        em.wait("tensor", "dv_z", j)
    if c16 == 0:
        em.wait("tensor", "dv_v65", Gg + 1)
    vA = arena[:, A_V65 + par * 2080 + c16 * 130: A_V65 + par * 2080 + c16 * 130 + 65]
    vB = arena[:, A_V65 + par * 2080 + c16 * 130 + 65: A_V65 + par * 2080 + c16 * 130 + 130]
    pA = arena[:, A_P + (Iprev % 3) * 1024: A_P + (Iprev % 3) * 1024 + 512]
    pB = arena[:, A_P + (Iprev % 3) * 1024 + 512: A_P + (Iprev % 3) * 1024 + 1024]
    em.op("tensor", lambda t, vA=vA, pA=pA, st=(ch == 0), sp=(ch == 63):
          t.matmul(ps45[0:65, 0:512], vA, pA, start=st, stop=sp))
    em.inc("pe_pv")
    em.op("tensor", lambda t, vB=vB, pB=pB, st=(ch == 0), sp=(ch == 63), s=sems["pe_pv"]:
          t.matmul(ps45[0:65, 512:1024], vB, pB, start=st, stop=sp).then_inc(s, 1))



def _emit_trio_evac(em, T, sems, half, tidx, pair_prev, qb_prev, zt_sb, ps45, ps6,
                    rl_row, bcA):
    j = half * 16 + tidx
    dvs = em.sems["dv"]
    bcP = T["bcP"]
    srow = T["srow"]
    em.wait("scalar", "pe_pv", 64 * (j + 1))
    em.op("scalar", lambda s: s.activation(out=srow[0:1, 0:1024], in_=ps45[64:65, 0:1024],
                                           func=AF.Ln, scale=f32(1.0)))
    em.op("scalar", lambda s: s.drain())
    em.inc("dv_z1")
    em.op("scalar", lambda s, ss=sems["dv_z1"]:
          s.activation(out=rl_row[0:1, 0:1024], in_=srow[0:1, 0:1024], func=AF.Exp,
                       scale=f32(-1.0)).then_inc(ss, 1))
    em.wait("gpsimd", "dv_z1", j + 1)
    em.wait("gpsimd", "dv_z", j)
    em.dma("gpsimd", T["rl_dram"].ap(), rl_row[0:1, 0:1024], "d_w1")
    em.wait("gpsimd", "d_w1")
    em.dma("gpsimd", bcP[:, 0:1024], T["rl_dram"].ap().partition_broadcast(64), "d_w0")
    em.wait("vector", "d_w0")
    dstA = zt_sb[0:64, (2 * pair_prev) * 1024 + qb_prev * 512:
                 (2 * pair_prev) * 1024 + qb_prev * 512 + 512]
    em.inc("dv_z2")
    em.op("vector", lambda v, dstA=dstA, s=sems["dv_z2"]:
          v.tensor_tensor(dstA, ps45[0:64, 0:512], bcP[:, 0:512], ALU.mult).then_inc(s, 1))
    dstB = zt_sb[0:64, (2 * pair_prev + 1) * 1024 + qb_prev * 512:
                 (2 * pair_prev + 1) * 1024 + qb_prev * 512 + 512]
    em.inc("dv_z")
    em.op("vector", lambda v, dstB=dstB, s=sems["dv_z"]:
          v.tensor_tensor(dstB, ps45[0:64, 512:1024], bcP[:, 512:1024], ALU.mult).then_inc(s, 1))





def build_ffn(layer: int):
    nc = bass.Bass("TRN2", target_bir_lowering=False)
    din = {}

    def dram_in(name, shape, dt=F32):
        din[name] = nc.dram_tensor(name, list(shape), dt, kind="ExternalInput")
        return din[name]

    dram_in("x_in", [D, TOK])
    dram_in("c_ones_col", [128, 1])
    dram_in("c_ones_row32", [1, 128])
    dram_in("c_minus_one", [1, 512])
    dram_in("c_ident", [128, 128])
    if layer == 0:
        dram_in("q1_cm", [128, 8 * 4096], I32)
        dram_in("q2_cm", [128, 32 * 1024], I32)
        dram_in("s1_flat", [1, 32768])
        dram_in("s2_flat", [1, 32768])
        dram_in("b1_c", [128, 32])
        dram_in("b2_c", [128, 8])
        dram_in("g_cols", [128, 8])
        dram_in("b_cols", [128, 8])
    else:
        dram_in("wexp_cm", [8, 128, 8192])
        dram_in("b_exp", [8, 1024])
        dram_in("lg_in", [1, 8])
        dram_in("uu_in", [1, 8])
    x_out = nc.dram_tensor("x_out", [D, TOK], F32, kind="ExternalOutput")
    if layer == 0:
        w1bf_dram = nc.dram_tensor("w1bf_dram", [128, 8 * 4096], BF16)
        w2bf_dram = nc.dram_tensor("w2bf_dram", [128, 32 * 1024], BF16)

    es = contextlib.ExitStack()
    sems = {ss: es.enter_context(nc.semaphore(ss)) for ss in SEM_NAMES}
    em = Emit(sems)
    T = {}
    sb, psa = _common_setup(nc, T, es)

    x_sb = sb("x_sb", [128, NC * TOK], F32)
    if layer == 0:
        arena = sb("arena", [128, 36864], BF16)
        A_H, A_Y, A_XT = 0, 16384, 32768
        i32a = sb("i32a", [128, 2048], I32)
        i32b = sb("i32b", [128, 2048], I32)
        spa = sb("spa", [128, 2048], F32)
        spb = sb("spb", [128, 2048], F32)
        w1bf0 = sb("w1bf0", [128, 2048], BF16)
        w1bf1 = sb("w1bf1", [128, 2048], BF16)
        sw0 = sb("sw0", [128, 2048], BF16)
        sw1 = sb("sw1", [128, 2048], BF16)
        g_c = sb("g_c", [128, 8], F32)
        b_c = sb("b_c", [128, 8], F32)
        b1c = sb("b1c", [128, 32], F32)
        b2c = sb("b2c", [128, 8], F32)
        hb2 = sb("hb2", [128, 8], F32)
    else:
        arena = sb("arena", [128, 12288], BF16)
        A_XT = 8192
        weff = sb("weff", [128, 8192], BF16)
        wt0 = sb("wt0", [128, 1024], F32)
        wt1 = sb("wt1", [128, 1024], F32)
        acc0 = sb("acc0", [128, 1024], F32)
        acc1 = sb("acc1", [128, 1024], F32)
        b_sb = sb("b_sb", [8, 1024], BF16)
        lg = sb("lg", [1, 8], F32)
        uu = sb("uu", [1, 8], F32)
        oh_r = sb("oh_r", [1, 8], F32)
        oh_b = sb("oh_b", [128, 128], BF16)
        ohT = sb("ohT", [8, 1], BF16)
        ohb_sb = sb("ohb_sb", [128, 8], F32)
        beff_sb = sb("beff_sb", [128, 8], F32)
    mo = sb("minus_one", [1, 512], F32)
    ev0 = sb("ev0", [128, 512], BF16)
    ev1 = sb("ev1", [128, 512], BF16)
    tf0 = sb("tf0", [128, 512], F32)
    tf1 = sb("tf1", [128, 512], F32)
    srow = sb("srow", [2, 1024], F32)
    st_mu = sb("st_mu", [1, 512], F32)
    st_ms = sb("st_ms", [1, 512], F32)
    st_vb = sb("st_vb", [1, 1024], F32)
    st_rs = sb("st_rs", [1, 512], F32)
    ones_col = sb("ones_col", [128, 1], BF16)
    ones_row32 = sb("ones_row32", [1, 128], F32)
    ident = sb("ident", [128, 128], BF16)

    ps01 = psa("ps01", [128, 1024], F32)
    ps23 = psa("ps23", [128, 1024], F32)
    ps45 = psa("ps45", [128, 1024], F32)
    if layer == 1:
        psT = psa("psT", [128, 128], BF16)

    T.update(arena=arena, x_sb=x_sb, A_XT=A_XT, ps01=ps01, ps23=ps23,
             ones_col=ones_col, ones_row32=ones_row32, st_mu=st_mu, st_ms=st_ms,
             st_rs=st_rs, srow=srow, ev0=ev0, ev1=ev1,
             tf0=tf0, tf1=tf1, minus_one=mo, st_var=st_vb, st_B=st_vb)
    dvs, acs, pes = sems["dv"], sems["ac"], sems["pe"]

    em.dma("gpsimd", x_sb[:].rearrange("p (c t) -> p c t", c=NC),
           din["x_in"].ap().rearrange("(c p) t -> p c t", p=128), "d_c")
    for nm, dst in (("c_ones_col", ones_col), ("c_ones_row32", ones_row32),
                    ("c_minus_one", mo), ("c_ident", ident)):
        em.dma("gpsimd", dst[:], din[nm].ap(), "d_c")
    if layer == 0:
        for nm, dst in (("g_cols", g_c), ("b_cols", b_c), ("b1_c", b1c), ("b2_c", b2c)):
            em.dma("gpsimd", dst[:], din[nm].ap(), "d_c")
    else:
        em.dma("gpsimd", b_sb[:], din["b_exp"].ap(), "d_c")
        em.dma("gpsimd", lg[:], din["lg_in"].ap(), "d_c")
        em.dma("gpsimd", uu[:], din["uu_in"].ap(), "d_c")
    em.wait("vector", "d_c")
    em.wait("tensor", "d_c")
    em.wait("scalar", "d_c")

    if layer == 0:
        v0 = em.inc("dv")
        em.op("vector", lambda v, s=dvs: v.tensor_scalar_mul(hb2[:], b2c[:], 0.5).then_inc(s, 1))

        slot_mark = [0, 0]

        def sweep(nblk, qsrc, sflat, dst):
            for blk in range(nblk):
                sl = em.inc("wsl") % 2
                dwn, dsn = "d_w%d" % sl, "d_kv%d" % sl
                i32t, spt, wbf = ((i32a, spa, sw0), (i32b, spb, sw1))[sl]
                em.wait("gpsimd", "dv", slot_mark[sl])
                em.wait("sync", "dv", slot_mark[sl])
                vq = em.dma("gpsimd", i32t[:], qsrc(blk), dwn)
                vs = em.dma("sync", spt[:], sflat(blk), dsn)
                em.wait("vector", dwn, vq)
                em.wait("vector", dsn, vs)
                em.wait("vector", "d_sk", max(0, em.val("d_sk") - 16))
                em.inc("dv")
                em.op("vector", lambda v, i32t=i32t, spt=spt, wbf=wbf, s=dvs:
                      v.scalar_tensor_tensor(out=wbf[:], in0=i32t[:], scalar=-128.0,
                                             in1=spt[:], op0=ALU.add,
                                             op1=ALU.mult).then_inc(s, 1))
                em.wait("sync", "dv")
                em.dma("sync", dst(blk), wbf[:], "d_sk")
                slot_mark[sl] = em.val("dv")

        def sweep_part(which, b0, nblk):
            qn, sn, dst = (("q1_cm", "s1_flat", w1bf_dram) if which == 1
                           else ("q2_cm", "s2_flat", w2bf_dram))
            sweep(nblk,
                  lambda blk: din[qn].ap()[:, (b0 + blk) * 2048:(b0 + blk) * 2048 + 2048],
                  lambda blk: din[sn].ap()[0:1, (b0 + blk) * 2048:(b0 + blk) * 2048 + 2048]
                  .partition_broadcast(128),
                  lambda blk: dst.ap()[:, (b0 + blk) * 2048:(b0 + blk) * 2048 + 2048])

        sweep_part(1, 0, 16)
        SWEEP_ST1 = em.val("d_sk")
        emit_ln(em, T, x_sb, (g_c, b_c), A_H)
        sweep_part(2, 0, 16)
        SWEEP_ST = em.val("d_sk")

        wslot_mark = [0, 0]

        def wbf_load(cols_ap, gate):
            sl = em.inc("wbl") % 2
            dwn = "d_w%d" % sl
            wbf = (w1bf0, w1bf1)[sl]
            em.wait("gpsimd", "pe", wslot_mark[sl])
            if em.val("hot_gate") < gate:
                em.ctr["hot_gate"] = gate
                em.wait("gpsimd", "d_sk", gate)
            v = em.dma("gpsimd", wbf[:], cols_ap, dwn)
            return sl, wbf, v, dwn

        for tb in range(4):
            tok0 = tb * 512
            for fp in range(16):
                sl, wbf, vld, dwn = wbf_load(
                    w1bf_dram.ap().rearrange("p (c f) -> p c f", c=8)
                    [:, :, fp * 256:fp * 256 + 256], SWEEP_ST1)
                em.wait("tensor", dwn, vld)
                for fo in range(2):
                    fcol = fp * 2 + fo
                    sl4 = em.inc("psl") % 4
                    outv = (ps01, ps23)[sl4 // 2][:, (sl4 % 2) * 512:(sl4 % 2) * 512 + 512]
                    em.wait("tensor", "ac", max(0, em.val("ac") - 2))
                    for ec in range(NC):
                        kk = ec * 2 + fo
                        rh = arena[:, A_H + ec * TOK + tok0:A_H + ec * TOK + tok0 + 512]
                        if ec == NC - 1:
                            em.inc("pe")
                            em.op("tensor", lambda t, wbf=wbf, kk=kk, outv=outv, ec=ec, rh=rh, s=pes:
                                  t.matmul(outv, wbf[:, kk * 128:kk * 128 + 128], rh,
                                           start=(ec == 0), stop=True).then_inc(s, 1))
                        else:
                            em.op("tensor", lambda t, wbf=wbf, kk=kk, outv=outv, ec=ec, rh=rh:
                                  t.matmul(outv, wbf[:, kk * 128:kk * 128 + 128], rh,
                                           start=(ec == 0), stop=False))
                    em.wait("scalar", "pe")
                    em.inc("ac")
                    em.op("scalar", lambda sc, outv=outv, fcol=fcol, ss=acs:
                          sc.activation(out=arena[:, A_Y + fcol * 512:A_Y + fcol * 512 + 512],
                                        in_=outv, func=AF.Silu,
                                        bias=b1c[:, fcol:fcol + 1], scale=f32(1.0)).then_inc(ss, 1))
                wslot_mark[sl] = em.val("pe")
            em.wait("tensor", "ac")
            for dc in range(NC):
                sl2 = em.inc("ps2") % 2
                outv = ps45[:, sl2 * 512: sl2 * 512 + 512]
                em.wait("tensor", "dv", max(0, em.val("dv") - 1))
                for half16 in range(2):
                    sl, wbf, vld, dwn = wbf_load(
                        w2bf_dram.ap().rearrange("p (c f) -> p c f", c=32)
                        [:, half16 * 16:half16 * 16 + 16, dc * 128:dc * 128 + 128], SWEEP_ST)
                    em.wait("tensor", dwn, vld)
                    for kk in range(16):
                        fc = half16 * 16 + kk
                        rh = arena[:, A_Y + fc * 512:A_Y + fc * 512 + 512]
                        if kk == 15:
                            em.inc("pe")
                            em.op("tensor", lambda t, wbf=wbf, kk=kk, outv=outv, fc=fc, rh=rh,
                                  s=pes, last=(fc == 31):
                                  t.matmul(outv, wbf[:, kk * 128:kk * 128 + 128], rh,
                                           start=(fc == 0), stop=last).then_inc(s, 1))
                        else:
                            em.op("tensor", lambda t, wbf=wbf, kk=kk, outv=outv, fc=fc, rh=rh:
                                  t.matmul(outv, wbf[:, kk * 128:kk * 128 + 128], rh,
                                           start=(fc == 0), stop=False))
                    wslot_mark[sl] = em.val("pe")
                em.wait("scalar", "pe")
                tfx = (tf0, tf1)[em.inc("tfs") % 2]
                va = em.inc("ac")
                em.op("scalar", lambda sc, outv=outv, tfx=tfx, dc=dc, ss=acs:
                      sc.activation(out=tfx[:], in_=outv, func=AF.Identity,
                                    bias=hb2[:, dc:dc + 1], scale=f32(0.5)).then_inc(ss, 1))
                em.wait("vector", "ac", va)
                em.wait("vector", "dv", v0)
                xs = x_sb[:, dc * TOK + tok0: dc * TOK + tok0 + 512]
                em.inc("dv")
                em.op("vector", lambda v, xs=xs, tfx=tfx, s=dvs:
                      v.scalar_tensor_tensor(out=xs, in0=xs, scalar=2.0, in1=tfx[:],
                                             op0=ALU.mult, op1=ALU.add).then_inc(s, 1))
                em.op("vector", lambda v: v.drain())
    else:
            em.op("scalar", lambda s: s.activation(out=srow[0:1, 0:8], in_=uu[:], func=AF.Ln,
                                                   scale=f32(1.0)))
            em.op("scalar", lambda s: s.drain())
            va = em.inc("ac")
            em.op("scalar", lambda s, ss=acs:
                  s.activation(out=srow[0:1, 8:16], in_=srow[0:1, 0:8], func=AF.Ln,
                               scale=f32(-1.0)).then_inc(ss, 1))
            em.wait("vector", "ac", va)
            em.op("vector", lambda v: v.tensor_tensor(oh_r[:], lg[:], srow[0:1, 8:16], ALU.subtract))
            em.op("vector", lambda v: v.drain())
            em.op("vector", lambda v: v.reduce_max(srow[0:1, 16:17], oh_r[:], axis=mybir.AxisListType.X))
            em.op("vector", lambda v: v.drain())
            em.op("vector", lambda v: v.tensor_scalar(out=oh_r[:], in0=oh_r[:],
                                                      scalar1=srow[0:1, 16:17], scalar2=None,
                                                      op0=ALU.is_equal))
            em.op("vector", lambda v: v.drain())
            em.op("vector", lambda v: v.memset(oh_b[:], 0.0))
            em.op("vector", lambda v: v.drain())
            v1 = em.inc("dv")
            em.op("vector", lambda v, s=dvs: v.tensor_copy(oh_b[0:1, 0:8], oh_r[:]).then_inc(s, 1))
            em.wait("tensor", "dv", v1)
            em.inc("pe")
            em.op("tensor", lambda t, s=pes:
                  t.transpose(psT[:], oh_b[:], ident[:]).then_inc(s, 1))
            em.inc("pe")
            em.op("tensor", lambda t, s=pes:
                  t.matmul(ps45[:, 0:8], ones_row32[:], oh_r[:], start=True, stop=True).then_inc(s, 1))
            em.wait("vector", "pe")
            v2 = em.inc("dv")
            em.op("vector", lambda v, s=dvs: v.tensor_copy(ohT[:], psT[0:8, 0:1]).then_inc(s, 1))
            v3 = em.inc("dv")
            em.op("vector", lambda v, s=dvs: v.tensor_copy(ohb_sb[:], ps45[:, 0:8]).then_inc(s, 1))
            em.wait("tensor", "dv", v3)
            for dc in range(NC):
                last = dc == NC - 1
                em.inc("pe")
                em.op("tensor", lambda t, dc=dc, s=pes:
                      t.matmul(ps23[:, dc:dc + 1], b_sb[:, dc * 128:dc * 128 + 128], ohT[:],
                               start=True, stop=True).then_inc(s, 1))
            em.wait("vector", "pe")
            v4 = em.inc("dv")
            em.op("vector", lambda v, s=dvs: v.tensor_copy(beff_sb[:], ps23[:, 0:8]).then_inc(s, 1))
            wm = [0, 0]
            for dcin in range(NC):
                for i in range(8):
                    sl = em.inc("wsl") % 2
                    dwn = "d_w%d" % sl
                    wt = (wt0, wt1)[sl]
                    eng = ("gpsimd", "sync")[sl]
                    em.wait(eng, "dv", wm[sl])
                    vq = em.dma(eng, wt[:], din["wexp_cm"].ap()
                                [i, :, dcin * 1024:dcin * 1024 + 1024], dwn)
                    em.wait("vector", dwn, vq)
                    aco, aci = (acc0, acc1) if i % 2 == 0 else (acc1, acc0)
                    vv = em.inc("dv")
                    if i == 0:
                        em.op("vector", lambda v, wt=wt, aco=aco, i=i, s=dvs:
                              v.tensor_scalar_mul(aco[:], wt[:], ohb_sb[:, i:i + 1]).then_inc(s, 1))
                    elif i < 7:
                        em.op("vector", lambda v, wt=wt, aco=aco, aci=aci, i=i, s=dvs:
                              v.scalar_tensor_tensor(out=aco[:], in0=wt[:], scalar=ohb_sb[:, i:i + 1],
                                                     in1=aci[:], op0=ALU.mult, op1=ALU.add).then_inc(s, 1))
                    else:
                        em.op("vector", lambda v, wt=wt, aci=aci, dcin=dcin, s=dvs:
                              v.scalar_tensor_tensor(out=weff[:, dcin * 1024:dcin * 1024 + 1024],
                                                     in0=wt[:], scalar=ohb_sb[:, i:i + 1],
                                                     in1=aci[:], op0=ALU.mult, op1=ALU.add).then_inc(s, 1))
                    wm[sl] = em.val("dv")
            WEFF_dv = em.val("dv")
            for tb in range(4):
                tok0 = tb * 512
                emit_xb_tiles(em, T, x_sb, tok0)
                em.wait("tensor", "dv")
                for eo in range(NC):
                    sl4 = em.inc("psl") % 4
                    outv = (ps01, ps23)[sl4 // 2][:, (sl4 % 2) * 512:(sl4 % 2) * 512 + 512]
                    em.wait("tensor", "ac")
                    lhs = [weff[:, din_ * 1024 + eo * 128: din_ * 1024 + eo * 128 + 128]
                           for din_ in range(NC)]
                    rhs = [xb_slice(T, din_) for din_ in range(NC)]
                    emit_mm_group(em, T, lhs, rhs, outv, 128)
                    em.wait("scalar", "pe")
                    tfx = (tf0, tf1)[em.inc("tfs") % 2]
                    va = em.inc("ac")
                    em.op("scalar", lambda s, outv=outv, tfx=tfx, eo=eo, ss=acs:
                          s.activation(out=tfx[:], in_=outv, func=AF.Identity,
                                       bias=beff_sb[:, eo:eo + 1], scale=f32(1.0)).then_inc(ss, 1))
                    em.wait("vector", "ac", va)
                    xs = x_sb[:, eo * TOK + tok0: eo * TOK + tok0 + 512]
                    em.inc("dv")
                    em.op("vector", lambda v, xs=xs, tfx=tfx, s=dvs:
                          v.tensor_tensor(xs, xs, tfx[:], ALU.add).then_inc(s, 1))
                    em.op("vector", lambda v: v.drain())

    em.wait("tensor", "ac_e", em.val("ac_e"))
    emit_ln(em, T, x_sb, None, None)
    emit_store_x(em, T, x_out)

    with nc.Block() as block:
        em.play(block)
    es.close()
    return nc



def _consts():
    m01 = np.zeros((128, 2), np.float32)
    m01[0:64, 0] = 1.0
    m01[64:128, 1] = 1.0
    s2 = np.zeros((2, 128), np.float32)
    s2[0, 0:64] = 1.0
    s2[1, 64:128] = 1.0
    return {
        "c_ones_col": np.ones((128, 1), np.float32),
        "c_ones_row": np.ones((1, 128), np.float32),
        "c_ones_row32": np.ones((1, 128), np.float32),
        "c_ident": np.eye(128, dtype=np.float32),
        "c_mask01": m01,
        "c_sel2": s2,
        "c_minus_one": np.full((1, 512), -1.0, np.float32),
    }


def _w_chunk_major(w):
    K, N = w.shape
    return np.ascontiguousarray(w.reshape(K // 128, 128, N).transpose(1, 0, 2).reshape(128, -1))


def _wo2_layout(wo_eD):
    return np.ascontiguousarray(wo_eD.reshape(16, 64, 1024).transpose(1, 0, 2).reshape(64, -1))


def _attn_inputs(layer, part, xT, W):
    ins = {"x_in": np.ascontiguousarray(xT)}
    ins.update(_consts())
    if layer == 0:
        if part == "kv":
            ins["w_k"] = _w_chunk_major(W["L0_attn_Wk"])
            ins["w_v"] = _w_chunk_major(W["L0_attn_Wv"])
        else:
            ins["w_q"] = _w_chunk_major(W["L0_attn_Wq"])
            ins["w_o2"] = _wo2_layout(W["L0_attn_Wout"])
    else:
        if part == "kv":
            ins["w_k"] = _w_chunk_major(np.ascontiguousarray(W["L1_attn_Wk"].T))
            ins["w_v"] = _w_chunk_major(np.ascontiguousarray(W["L1_attn_Wv"].T))
            ins["ks_in"] = np.ascontiguousarray(
                W["L1_attn_k_scale"].reshape(16, 64).reshape(8, 128).T)
        else:
            ins["w_q"] = _w_chunk_major(np.ascontiguousarray(W["L1_attn_Wq"].T))
            ins["w_o2"] = _wo2_layout(np.ascontiguousarray(W["L1_attn_Wo"].T))
            ins["qs_in"] = np.ascontiguousarray(W["L1_attn_q_scale"].reshape(16, 64).T)
            ins["bo_in"] = np.ascontiguousarray(W["L1_attn_bo"].reshape(8, 128).T)
    return ins


def run_attn_block(layer, shards, W):
    nc_kv = build_attn(layer, "kv")
    in_maps = [_attn_inputs(layer, "kv", shards[c], W) for c in range(8)]
    res = run_bass_kernel_spmd(nc_kv, in_maps, core_ids=list(range(8)))
    ktf = {}
    vf = {}
    for g in range(2):
        ktf[g] = np.concatenate([np.asarray(res.results[g * 4 + c]["kt_out"]) for c in range(4)], axis=0)
        vf[g] = np.concatenate([np.asarray(res.results[g * 4 + c]["v_out"]) for c in range(4)], axis=0)
    nc_core = build_attn(layer, "core")
    in_maps = []
    for c in range(8):
        m = _attn_inputs(layer, "core", shards[c], W)
        m["kt_full"] = ktf[c // 4]
        m["v_full"] = vf[c // 4]
        in_maps.append(m)
    res = run_bass_kernel_spmd(nc_core, in_maps, core_ids=list(range(8)))
    return [np.ascontiguousarray(np.asarray(r["x_out"])) for r in res.results]


def _ffn_inputs(layer, xT, W):
    ins = {"x_in": np.ascontiguousarray(xT)}
    ins.update(_consts())
    if layer == 0:
        q1 = W["L0_ffn_fc1_q"].astype(np.int32)
        q2 = W["L0_ffn_fc2_q"].astype(np.int32)
        ins["q1_cm"] = np.ascontiguousarray(q1.reshape(8, 128, 4096).transpose(1, 0, 2).reshape(128, -1))
        ins["q2_cm"] = np.ascontiguousarray(q2.reshape(32, 128, 1024).transpose(1, 0, 2).reshape(128, -1))
        ins["s1_flat"] = np.ascontiguousarray(W["L0_ffn_fc1_scales"].reshape(1, -1).astype(np.float32))
        ins["s2_flat"] = np.ascontiguousarray(W["L0_ffn_fc2_scales"].reshape(1, -1).astype(np.float32))
        ins["b1_c"] = np.ascontiguousarray(W["L0_ffn_fc1_bias"].reshape(32, 128).T.astype(np.float32))
        ins["b2_c"] = np.ascontiguousarray(W["L0_ffn_fc2_bias"].reshape(8, 128).T.astype(np.float32))
        ins["g_cols"] = np.ascontiguousarray(W["L0_ffn_ln_gamma"].reshape(8, 128).T.astype(np.float32))
        ins["b_cols"] = np.ascontiguousarray(W["L0_ffn_ln_beta"].reshape(8, 128).T.astype(np.float32))
    else:
        we = W["L1_ffn_W"].astype(np.float32)
        ins["wexp_cm"] = np.ascontiguousarray(
            we.reshape(8, 8, 128, 1024).transpose(0, 2, 1, 3).reshape(8, 128, 8192))
        ins["b_exp"] = np.ascontiguousarray(W["L1_ffn_b"].astype(np.float32))
        ins["lg_in"] = np.ascontiguousarray(W["L1_ffn_logits"].reshape(1, 8).astype(np.float32))
        ins["uu_in"] = np.ascontiguousarray(W["L1_ffn_u"].reshape(1, 8).astype(np.float32))
    return ins


def run_ffn_block(layer, shards, W):
    nc = build_ffn(layer)
    in_maps = [_ffn_inputs(layer, shards[c], W) for c in range(8)]
    res = run_bass_kernel_spmd(nc, in_maps, core_ids=list(range(8)))
    return [np.ascontiguousarray(np.asarray(r["x_out"])) for r in res.results]


def kernel(**inputs):
    W = {k: np.asarray(v) for k, v in inputs.items()}
    x = W["L0_attn_x"]
    shards = []
    for c in range(8):
        b, t0 = c // 4, (c % 4) * TOK
        shards.append(np.ascontiguousarray(x[b, t0:t0 + TOK, :].T.astype(np.float32)))

    shards = run_attn_block(0, shards, W)
    shards = run_ffn_block(0, shards, W)
    shards = run_attn_block(1, shards, W)
    shards = run_ffn_block(1, shards, W)
    out = np.empty((2, SEQ, D), np.float32)
    for c in range(8):
        b, t0 = c // 4, (c % 4) * TOK
        out[b, t0:t0 + TOK, :] = shards[c].T
    return out
